# version 03 (final)
# baseline (speedup 1.0000x reference)
import numpy as np
import ml_dtypes
from contextlib import ExitStack

import concourse.bass as bass
import concourse.mybir as mybir
from concourse.bass_utils import run_bass_kernel_spmd

F32 = mybir.dt.float32
BF16 = mybir.dt.bfloat16
I32 = mybir.dt.int32
U32 = mybir.dt.uint32
AF = mybir.ActivationFunctionType
ALU = mybir.AluOpType
AX = mybir.AxisListType
BF16NP = ml_dtypes.bfloat16

B, L, D, DI, NST = 4, 4096, 1024, 2048, 16
TC = 2048
TP = 2176
PAD = 128
NC = 8
LN_EPS = 1e-5
DECAY = 0.95
GELU_C = 0.044715
GELU_S = 0.7978845608028654


class Ctx:

    def __init__(self, M):
        self.M = M
        self.tot = {}
        self.trace = []
        self.sec = "?"
        self.names = {}
        self.prefix = ""

    def inc(self, instr, sem, n, key=None):
        instr.then_inc(sem, n)
        k = id(sem)
        self.tot[k] = self.tot.get(k, 0) + n
        if key is not None:
            self.M[self.prefix + key] = self.tot[k]
        self.trace.append((self.sec, "inc", k, n, key))
        return self.tot[k]

    def w(self, eng, sem, key):
        v = self.M.get(self.prefix + key, 0)
        eng.wait_ge(sem, v)
        self.trace.append((self.sec, "wait", id(sem), v, key))

    def wg(self, eng, sem, fullkey):
        v = self.M.get(fullkey, 0)
        eng.wait_ge(sem, v)
        self.trace.append((self.sec, "wait", id(sem), v, fullkey))

    def setk(self, key, sem):
        self.M[self.prefix + key] = self.tot.get(id(sem), 0)

    def wv(self, eng, sem, val):
        eng.wait_ge(sem, val)
        self.trace.append((self.sec, "wait", id(sem), val, None))

    def val(self, sem):
        return self.tot.get(id(sem), 0)

    def bar(self, instr, eng, sem):
        n = self.inc(instr, sem, 1)
        eng.wait_ge(sem, n)
        self.trace.append((self.sec, "wait", id(sem), n, "bar"))
        return n


def check_deadlock(C):
    qs = {}
    for item in C.trace:
        qs.setdefault(item[0], []).append(item)
    ptr = {k: 0 for k in qs}
    sems = {}
    progressed = True
    while progressed:
        progressed = False
        for k, items in qs.items():
            while ptr[k] < len(items):
                sec, kind, sid, val, key = items[ptr[k]]
                if kind == "inc":
                    sems[sid] = sems.get(sid, 0) + val
                    ptr[k] += 1
                    progressed = True
                elif sems.get(sid, 0) >= val:
                    ptr[k] += 1
                    progressed = True
                else:
                    break
    stuck = {k: (ptr[k], len(qs[k])) for k in qs if ptr[k] < len(qs[k])}
    if stuck:
        msg = []
        for k in stuck:
            sec, kind, sid, val, key = qs[k][ptr[k]]
            msg.append(f"{k}@{ptr[k]}/{len(qs[k])}: waiting {C.names.get(sid, sid)}>={val} (key={key}, have={sems.get(sid, 0)})")
        raise RuntimeError("DEADLOCK: " + "; ".join(msg))


def two_pass(emit):
    M = {}
    emit(Ctx(M))
    C = Ctx(M)
    nc = emit(C)
    check_deadlock(C)
    return nc


def t_chunks(total, step=512):
    out, c0 = [], 0
    while c0 < total:
        w = min(step, total - c0)
        out.append((c0, w))
        c0 += w
    return out



def emit_l0a(C):
    nc = bass.Bass("TRN2", target_bir_lowering=False)
    M = C.M
    x_in = nc.dram_tensor("x", [D, TP], F32, kind="ExternalInput")
    w_in = nc.dram_tensor("w_in", [2 * DI, D], F32, kind="ExternalInput")
    conv_w = nc.dram_tensor("conv_w", [DI, 1, 4], F32, kind="ExternalInput")
    conv_b = nc.dram_tensor("conv_b", [DI], F32, kind="ExternalInput")
    w_x = nc.dram_tensor("w_x", [2 * NST, DI], F32, kind="ExternalInput")
    eye = nc.dram_tensor("eye128", [128, 128], BF16, kind="ExternalInput")
    onesc = nc.dram_tensor("onesc", [128, 1], F32, kind="ExternalInput")
    onesr = nc.dram_tensor("onesr", [1, 128], F32, kind="ExternalInput")

    o_xc = nc.dram_tensor("o_xconv", [DI, TC], BF16, kind="ExternalOutput")
    o_z = nc.dram_tensor("o_z", [DI, TC], BF16, kind="ExternalOutput")
    o_sc = nc.dram_tensor("o_sc", [NST, TC], F32, kind="ExternalOutput")
    o_xln = nc.dram_tensor("o_xln", [D, TC], F32, kind="ExternalOutput")

    DT = D // 128
    ET = 2 * DI // 128
    IT = DI // 128
    CH = t_chunks(TP)
    CHO = t_chunks(TC)

    es = ExitStack()
    sb = lambda n, s, d=F32: es.enter_context(nc.sbuf_tensor(n, s, d))
    ps = lambda n, s, d=F32: es.enter_context(nc.psum_tensor(n, s, d))

    sx = [sb(f"sx{i}", [128, TP]) for i in range(DT)]
    sxb = [sb(f"sxb{i}", [128, TP], BF16) for i in range(DT)]
    ssq = sb("ssq", [128, 512])
    seye = sb("seye", [128, 128], BF16)
    sonesc = sb("sonesc", [128, 1])
    sonesr = sb("sonesr", [1, 128])
    s_mu = sb("s_mu", [1, TP])
    s_var = sb("s_var", [1, TP])
    s_mr = sb("s_mr", [1, TP])
    s_wrow = [sb(f"s_wrow{i}", [128, D], BF16) for i in range(2)]
    s_lhs = [sb(f"s_lhs{i}", [128, 128], BF16) for i in range(DT)]
    s_xz = sb("s_xz", [128, TP])
    s_conv = sb("s_conv", [128, TC])
    s_cw = sb("s_cw", [128, 64])
    s_cb = sb("s_cb", [128, 16])
    s_xcb = [sb(f"s_xcb{i}", [128, TC], BF16) for i in range(2)]
    s_zb = [sb(f"s_zb{i}", [128, TC], BF16) for i in range(2)]
    s_wx = sb("s_wx", [32, DI], BF16)
    s_wxT = [sb(f"s_wxT{i}", [128, 32], BF16) for i in range(IT)]
    s_sc32 = sb("s_sc32", [32, TC])

    p_bc1 = ps("p_bc1", [128, 512])
    p_bc2 = ps("p_bc2", [128, 512])
    p_tr = ps("p_tr", [128, 128], BF16)
    p_mm = ps("p_mm", [128, 512])
    p_sp = ps("p_sp", [32, 2048])

    qS = es.enter_context(nc.semaphore("qS"))
    qG = es.enter_context(nc.semaphore("qG"))
    cT = es.enter_context(nc.semaphore("cT"))
    cV = es.enter_context(nc.semaphore("cV"))
    cS = es.enter_context(nc.semaphore("cS"))

    with es, nc.Block() as block:
        @block.sync
        def _sync_(sync):
            C.sec = "sync"
            for i in range(DT):
                C.inc(sync.dma_start(out=sx[i][:], in_=x_in[128 * i:128 * (i + 1), :]), qS, 16)
            C.inc(sync.dma_start(out=seye[:], in_=eye[:]), qS, 16)
            C.inc(sync.dma_start(out=sonesc[:], in_=onesc[:]), qS, 16)
            C.inc(sync.dma_start(out=sonesr[:], in_=onesr[:]), qS, 16, key="ld_x")
            for i in range(DT):
                C.w(sync, cV, f"bf16{i}")
                C.inc(sync.dma_start(out=o_xln[128 * i:128 * (i + 1), :], in_=sx[i][:, PAD:]), qS, 16)
            C.w(sync, cV, "sc_ev")
            C.inc(sync.dma_start(out=o_sc[:], in_=s_sc32[16:32, :]), qS, 16)
            C.wv(sync, qS, C.val(qS))
            C.w(sync, qG, "qG_total")
            C.w(sync, cS, "cS_total")

        @block.gpsimd
        def _gpsimd_(g):
            C.sec = "gpsimd"
            C.inc(g.dma_start(out=s_cw[:], in_=conv_w[:].rearrange("(t p) o k -> p t o k", p=128)), qG, 16)
            for _t in range(16):
                C.inc(g.dma_start(out=s_cb[:, _t:_t + 1],
                                  in_=conv_b[128 * _t:128 * (_t + 1)].unsqueeze(1)), qG, 16)
            C.inc(g.dma_start(out=s_wx[:], in_=w_x[:]), qG, 16, key="ld_misc")
            for j in range(ET):
                if j >= 2:
                    C.w(g, cT, f"wrow_free{j - 2}")
                C.inc(g.dma_start(out=s_wrow[j % 2][:], in_=w_in[128 * j:128 * (j + 1), :]), qG, 16,
                      key=f"wrow{j}")
                if j >= 2:
                    jo = j - 2
                    C.w(g, cS, f"act{jo}")
                    if jo < IT:
                        C.inc(g.dma_start(out=o_xc[128 * jo:128 * (jo + 1), :], in_=s_xcb[jo % 2][:]),
                              qG, 16, key=f"actout{jo}")
                    else:
                        C.inc(g.dma_start(out=o_z[128 * (jo - IT):128 * (jo - IT + 1), :],
                                          in_=s_zb[jo % 2][:]), qG, 16, key=f"actout{jo}")
            for jo in (ET - 2, ET - 1):
                C.w(g, cS, f"act{jo}")
                C.inc(g.dma_start(out=o_z[128 * (jo - IT):128 * (jo - IT + 1), :],
                                  in_=s_zb[jo % 2][:]), qG, 16, key=f"actout{jo}")
            M["qG_total"] = C.val(qG)

        @block.scalar
        def _scalar_(s):
            C.sec = "scalar"
            C.w(s, qS, "ld_x")
            for ci, (c0, w) in enumerate(CH):
                for i in range(DT):
                    g = ci * DT + i
                    if g > 0:
                        C.w(s, cT, f"st_mm_g{g - 1}")
                    C.inc(s.activation(ssq[:, 0:w], in_=sx[i][:, c0:c0 + w], func=AF.Square),
                          cS, 1, key=f"sq_g{g}")
            C.w(s, cV, "var_row")
            s.activation(s_var[:], in_=s_var[:], func=AF.Ln)
            C.inc(s.activation(s_var[:], in_=s_var[:], func=AF.Exp, scale=-0.5), cS, 1, key="rstd")
            for j in range(ET):
                inner = j < IT
                C.w(s, cV, f"conv{j}" if inner else f"xz_ev{j}")
                if j >= 2:
                    C.w(s, qG, f"actout{j - 2}")
                    if j - 2 < IT:
                        C.w(s, cT, f"sp_mm{j - 2}")
                if inner:
                    C.inc(s.activation(s_xcb[j % 2][:], in_=s_conv[:],
                                       func=AF.Silu, bias=s_cb[:, j:j + 1]), cS, 1, key=f"act{j}")
                else:
                    C.inc(s.activation(s_zb[j % 2][:], in_=s_xz[:, PAD:], func=AF.Copy),
                          cS, 1, key=f"act{j}")
            M["cS_total"] = C.val(cS)

        @block.tensor
        def _tensor_(t):
            C.sec = "tensor"
            C.w(t, qS, "ld_x")
            for ci, (c0, w) in enumerate(CH):
                if ci > 0:
                    C.w(t, cV, f"st_ev{ci - 1}")
                for i in range(DT):
                    g = ci * DT + i
                    C.w(t, cS, f"sq_g{g}")
                    t.matmul(p_bc1[0:1, 0:w], sonesc[:], sx[i][:, c0:c0 + w],
                             start=(i == 0), stop=(i == DT - 1))
                    C.inc(t.matmul(p_bc2[0:1, 0:w], sonesc[:], ssq[:, 0:w],
                                   start=(i == 0), stop=(i == DT - 1)), cT, 1, key=f"st_mm_g{g}")
            C.w(t, cS, "rstd")
            C.w(t, cV, "mr_row")
            for ci, (c0, w) in enumerate(CH):
                if ci > 0:
                    C.w(t, cV, f"ln_ev{ci - 1}")
                t.matmul(p_bc1[:, 0:w], sonesr[:], s_var[:, c0:c0 + w], start=True, stop=True)
                C.inc(t.matmul(p_bc2[:, 0:w], sonesr[:], s_mr[:, c0:c0 + w], start=True, stop=True),
                      cT, 1, key=f"bc{ci}")
            C.w(t, qG, "ld_misc")
            for i in range(IT):
                if i > 0:
                    C.w(t, cV, f"wxT_ev{i - 1}")
                C.inc(t.transpose(p_tr[:, 0:32], s_wx[:, 128 * i:128 * (i + 1)], seye[0:32, 0:32]),
                      cT, 1, key=f"wxT{i}")
            for i in range(DT):
                C.w(t, cV, f"bf16{i}")
            for j in range(ET):
                C.w(t, qG, f"wrow{j}")
                for i in range(DT):
                    tg = j * DT + i
                    if tg == 0:
                        C.w(t, cV, f"wxT_ev{IT - 1}")
                    else:
                        C.w(t, cV, f"lhs_ev_g{tg - 1}")
                    C.inc(t.transpose(p_tr[:], s_wrow[j % 2][:, 128 * i:128 * (i + 1)], seye[:]),
                          cT, 1, key=f"lhsT_g{tg}")
                M[f"wrow_free{j}"] = C.val(cT)
                C.w(t, cV, f"lhs_ev_g{j * DT + DT - 1}")
                for ci, (c0, w) in enumerate(CH):
                    pg = j * len(CH) + ci
                    if pg > 0:
                        C.w(t, cV, f"xz_psfree_g{pg - 1}")
                    for i in range(DT):
                        mmx = t.matmul(p_mm[:, 0:w], s_lhs[i][:], sxb[i][:, c0:c0 + w],
                                       start=(i == 0), stop=(i == DT - 1))
                    C.inc(mmx, cT, 1, key=f"xz_mm_g{pg}")
                if j < IT:
                    C.w(t, cS, f"act{j}")
                    C.w(t, cV, f"wxT_ev{j}")
                    for (c0, w) in CHO:
                        mms = t.matmul(p_sp[:, c0:c0 + w], s_wxT[j][:], s_xcb[j % 2][:, c0:c0 + w],
                                       start=(j == 0), stop=(j == IT - 1))
                    C.inc(mms, cT, 1, key=f"sp_mm{j}")

        @block.vector
        def _vector_(v):
            C.sec = "vector"
            for ci, (c0, w) in enumerate(CH):
                C.w(v, cT, f"st_mm_g{ci * DT + DT - 1}")
                v.tensor_copy(s_mu[:, c0:c0 + w], p_bc1[0:1, 0:w])
                C.inc(v.tensor_copy(s_var[:, c0:c0 + w], p_bc2[0:1, 0:w]), cV, 1, key=f"st_ev{ci}")
            v.wait_ge(cV, C.val(cV))
            v.tensor_scalar(s_mu[:], s_mu[:], 1.0 / D, None, ALU.mult)
            v.tensor_scalar(s_var[:], s_var[:], 1.0 / D, None, ALU.mult)
            v.tensor_tensor(s_mr[:], s_mu[:], s_mu[:], ALU.mult)
            v.tensor_tensor(s_var[:], s_var[:], s_mr[:], ALU.subtract)
            C.inc(v.tensor_scalar(s_var[:], s_var[:], float(LN_EPS), None, ALU.add), cV, 1, key="var_row")
            C.w(v, cS, "rstd")
            C.inc(v.tensor_tensor(s_mr[:], s_mu[:], s_var[:], ALU.mult), cV, 1, key="mr_row")
            for ci, (c0, w) in enumerate(CH):
                C.w(v, cT, f"bc{ci}")
                for i in range(DT):
                    v.scalar_tensor_tensor(sx[i][:, c0:c0 + w], sx[i][:, c0:c0 + w], 1.0,
                                           p_bc1[:, 0:w], ALU.mult, ALU.mult)
                    last = v.tensor_tensor(sx[i][:, c0:c0 + w], sx[i][:, c0:c0 + w],
                                           p_bc2[:, 0:w], ALU.subtract)
                C.inc(last, cV, 1, key=f"ln_ev{ci}")
            for i in range(DT):
                C.inc(v.tensor_copy(sxb[i][:], sx[i][:]), cV, 1, key=f"bf16{i}")
            for i in range(IT):
                C.w(v, cT, f"wxT{i}")
                C.inc(v.tensor_copy(s_wxT[i][:], p_tr[:, 0:32]), cV, 1, key=f"wxT_ev{i}")
            for j in range(ET):
                for i in range(DT):
                    tg = j * DT + i
                    C.w(v, cT, f"lhsT_g{tg}")
                    C.inc(v.tensor_copy(s_lhs[i][:], p_tr[:]), cV, 1, key=f"lhs_ev_g{tg}")
                if j >= 1:
                    C.w(v, cS, f"act{j - 1}")
                for ci, (c0, w) in enumerate(CH):
                    pg = j * len(CH) + ci
                    C.w(v, cT, f"xz_mm_g{pg}")
                    C.inc(v.tensor_copy(s_xz[:, c0:c0 + w], p_mm[:, 0:w]), cV, 1,
                          key=f"xz_psfree_g{pg}")
                M[f"xz_ev{j}"] = C.val(cV)
                if j < IT:
                    for k in range(4):
                        src = s_xz[:, PAD - 3 + k: PAD - 3 + k + TC]
                        wcol = s_cw[:, 4 * j + k: 4 * j + k + 1]
                        if k == 0:
                            v.tensor_scalar(s_conv[:], src, wcol, None, ALU.mult)
                        else:
                            last = v.scalar_tensor_tensor(s_conv[:], src, wcol, s_conv[:],
                                                          ALU.mult, ALU.add)
                    C.inc(last, cV, 1, key=f"conv{j}")
            C.w(v, cT, f"sp_mm{IT - 1}")
            C.inc(v.tensor_copy(s_sc32[:], p_sp[:]), cV, 1, key="sc_ev")

    return nc


def build_l0a():
    return two_pass(emit_l0a)



def emit_ema(C):
    nc = bass.Bass("TRN2", target_bir_lowering=False)
    xce = nc.dram_tensor("xce", [256, B * L], BF16, kind="ExternalInput")
    o_h = nc.dram_tensor("o_h", [256, B * L], BF16, kind="ExternalOutput")
    es = ExitStack()
    s_in = [es.enter_context(nc.sbuf_tensor(f"s_in{i}", [128, L], BF16)) for i in range(2)]
    s_out = [es.enter_context(nc.sbuf_tensor(f"s_out{i}", [128, L], BF16)) for i in range(2)]
    s_dec = es.enter_context(nc.sbuf_tensor("s_dec", [128, L], F32))
    qS = es.enter_context(nc.semaphore("qS"))
    cV = es.enter_context(nc.semaphore("cV"))
    cG = es.enter_context(nc.semaphore("cG"))
    with es, nc.Block() as block:
        @block.gpsimd
        def _gpsimd_(g):
            C.sec = "gpsimd"
            C.inc(g.memset(s_dec[:], DECAY), cG, 1, key="dec")

        @block.sync
        def _sync_(sync):
            C.sec = "sync"
            for k in range(8):
                dt_, s = k // 4, k % 4
                if k >= 2:
                    C.w(sync, cV, f"scan{k - 2}")
                C.inc(sync.dma_start(out=s_in[k % 2][:],
                                     in_=xce[128 * dt_:128 * (dt_ + 1), L * s:L * (s + 1)]),
                      qS, 16, key=f"ld{k}")
                if k >= 2:
                    ko = k - 2
                    dt2, s2 = ko // 4, ko % 4
                    C.inc(sync.dma_start(out=o_h[128 * dt2:128 * (dt2 + 1), L * s2:L * (s2 + 1)],
                                         in_=s_out[ko % 2][:]), qS, 16, key=f"st{ko}")
            for ko in (6, 7):
                C.w(sync, cV, f"scan{ko}")
                dt2, s2 = ko // 4, ko % 4
                C.inc(sync.dma_start(out=o_h[128 * dt2:128 * (dt2 + 1), L * s2:L * (s2 + 1)],
                                     in_=s_out[ko % 2][:]), qS, 16, key=f"st{ko}")
            C.wv(sync, qS, C.val(qS))
            C.w(sync, cG, "dec")

        @block.vector
        def _vector_(v):
            C.sec = "vector"
            C.w(v, cG, "dec")
            for k in range(8):
                C.w(v, qS, f"ld{k}")
                if k >= 2:
                    C.w(v, qS, f"st{k - 2}")
                C.inc(v.tensor_tensor_scan(s_out[k % 2][:], s_dec[:], s_in[k % 2][:],
                                           0.0, ALU.mult, ALU.add), cV, 1, key=f"scan{k}")
    return nc


def build_ema():
    return two_pass(emit_ema)



def emit_l0b(C):
    nc = bass.Bass("TRN2", target_bir_lowering=False)
    M = C.M
    h_in = nc.dram_tensor("h", [DI, TC], BF16, kind="ExternalInput")
    sc_in = nc.dram_tensor("s_c", [NST, TC], F32, kind="ExternalInput")
    z_in = nc.dram_tensor("z", [DI, TC], BF16, kind="ExternalInput")
    blog_in = nc.dram_tensor("b_log", [DI, NST], F32, kind="ExternalInput")
    wout_in = nc.dram_tensor("w_out", [D, DI], F32, kind="ExternalInput")
    lng_in = nc.dram_tensor("ln_g", [D], F32, kind="ExternalInput")
    lnb_in = nc.dram_tensor("ln_b", [D], F32, kind="ExternalInput")
    xln_in = nc.dram_tensor("xln", [D, TC], F32, kind="ExternalInput")
    x_in = nc.dram_tensor("x", [D, TC], F32, kind="ExternalInput")
    eye = nc.dram_tensor("eye128", [128, 128], BF16, kind="ExternalInput")
    onesc = nc.dram_tensor("onesc", [128, 1], F32, kind="ExternalInput")
    onesr = nc.dram_tensor("onesr", [1, 128], F32, kind="ExternalInput")

    o_x1 = nc.dram_tensor("o_x1", [D, TC], F32, kind="ExternalOutput")

    IT = DI // 128
    OT = D // 128
    CHO = t_chunks(TC)

    es = ExitStack()
    sb = lambda n, s, d=F32: es.enter_context(nc.sbuf_tensor(n, s, d))
    ps = lambda n, s, d=F32: es.enter_context(nc.psum_tensor(n, s, d))

    seye = sb("seye", [128, 128], BF16)
    sonesc = sb("sonesc", [128, 1])
    sonesr = sb("sonesr", [1, 128])
    s_scb = sb("s_scb", [NST, TC], BF16)
    s_bt = sb("s_bt", [128, NST])
    s_btb = sb("s_btb", [128, NST], BF16)
    s_BT = [sb(f"s_BT{j}", [NST, 128], BF16) for j in range(IT)]
    s_h = sb("s_h", [128, TC], BF16)
    s_z = sb("s_z", [128, TC], BF16)
    s_slz = sb("s_slz", [128, TC], BF16)
    s_y = [sb(f"s_y{j}", [128, TC], BF16) for j in range(IT)]
    s_wrow = [sb(f"s_wrow{i}", [128, DI], BF16) for i in range(2)]
    s_lhsW = [sb(f"s_lhsW{j}", [128, 128], BF16) for j in range(IT)]
    s_res = [sb(f"s_res{i}", [128, TC]) for i in range(OT)]
    s_sq = sb("s_sq", [128, 512])
    s_mu = sb("s_mu", [1, TC])
    s_var = sb("s_var", [1, TC])
    s_mr = sb("s_mr", [1, TC])
    s_g = sb("s_g", [128, OT])
    s_bb = sb("s_bb", [128, OT])
    s_xlnt = sb("s_xlnt", [128, TC])
    s_xt = sb("s_xt", [128, TC])

    p_bc1 = ps("p_bc1", [128, 512])
    p_bc2 = ps("p_bc2", [128, 512])
    p_tr = ps("p_tr", [128, 128], BF16)
    p_big = ps("p_big", [128, 2048])

    qS = es.enter_context(nc.semaphore("qS"))
    qG = es.enter_context(nc.semaphore("qG"))
    cT = es.enter_context(nc.semaphore("cT"))
    cV = es.enter_context(nc.semaphore("cV"))
    cS = es.enter_context(nc.semaphore("cS"))

    with es, nc.Block() as block:
        @block.sync
        def _sync_(sync):
            C.sec = "sync"
            C.inc(sync.dma_start(out=seye[:], in_=eye[:]), qS, 16)
            C.inc(sync.dma_start(out=sonesc[:], in_=onesc[:]), qS, 16)
            C.inc(sync.dma_start(out=sonesr[:], in_=onesr[:]), qS, 16)
            for _t in range(OT):
                C.inc(sync.dma_start(out=s_g[:, _t:_t + 1],
                                     in_=lng_in[128 * _t:128 * (_t + 1)].unsqueeze(1)), qS, 16)
                C.inc(sync.dma_start(out=s_bb[:, _t:_t + 1],
                                     in_=lnb_in[128 * _t:128 * (_t + 1)].unsqueeze(1)), qS, 16,
                      key="ld_c" if _t == OT - 1 else None)
            for i in range(OT):
                C.w(sync, cV, f"x1_{i}")
                C.inc(sync.dma_start(out=o_x1[128 * i:128 * (i + 1), :], in_=s_res[i][:]), qS, 16,
                      key=f"x1out{i}")
            C.wv(sync, qS, C.val(qS))
            C.w(sync, qG, "qG_total")
            C.w(sync, cS, "cS_total")

        @block.gpsimd
        def _gpsimd_(g):
            C.sec = "gpsimd"
            C.inc(g.dma_start(out=s_scb[:], in_=sc_in[:]), qG, 16, key="ld_sc")
            for j in range(IT):
                C.inc(g.dma_start(out=s_bt[:], in_=blog_in[128 * j:128 * (j + 1), :]), qG, 16,
                      key=f"blog{j}")
                C.w(g, cS, f"bexp{j}")
            for j in range(IT):
                if j >= 1:
                    C.w(g, cV, f"yd2_{j - 1}")
                C.inc(g.dma_start(out=s_h[:], in_=h_in[128 * j:128 * (j + 1), :]), qG, 16,
                      key=f"hld{j}")
                C.inc(g.dma_start(out=s_z[:], in_=z_in[128 * j:128 * (j + 1), :]), qG, 16,
                      key=f"zld{j}")
            for i in range(OT):
                if i >= 2:
                    C.w(g, cT, f"wo_free{i - 2}")
                C.inc(g.dma_start(out=s_wrow[i % 2][:], in_=wout_in[128 * i:128 * (i + 1), :]),
                      qG, 16, key=f"worow{i}")
                C.w(g, cV, f"att_ev{i - 1}" if i >= 1 else "none")
                C.inc(g.dma_start(out=s_xlnt[:],
                                  in_=xln_in[128 * i:128 * (i + 1), :]), qG, 16, key=f"xlnld{i}")
            for i in range(OT):
                C.w(g, cV, f"x1_{i - 1}" if i >= 1 else "none")
                C.inc(g.dma_start(out=s_xt[:], in_=x_in[128 * i:128 * (i + 1), :]), qG, 16,
                      key=f"xld{i}")
            M["qG_total"] = C.val(qG)

        @block.scalar
        def _scalar_(s):
            C.sec = "scalar"
            for j in range(IT):
                C.w(s, qG, f"blog{j}")
                if j > 0:
                    C.w(s, cT, f"bT{j - 1}")
                C.inc(s.activation(s_btb[:], in_=s_bt[:], func=AF.Exp), cS, 1, key=f"bexp{j}")
            for j in range(IT):
                C.w(s, qG, f"zld{j}")
                if j >= 1:
                    C.w(s, cV, f"yd2_{j - 1}")
                C.inc(s.activation(s_slz[:], in_=s_z[:], func=AF.Silu), cS, 1, key=f"silz{j}")
            for ci, (c0, w) in enumerate(CHO):
                for i in range(OT):
                    g2 = ci * OT + i
                    C.w(s, cV, f"att_ev{i}")
                    if g2 > 0:
                        C.w(s, cT, f"rst_mm_g{g2 - 1}")
                    C.inc(s.activation(s_sq[:, 0:w], in_=s_res[i][:, c0:c0 + w], func=AF.Square),
                          cS, 1, key=f"rsq_g{g2}")
            C.w(s, cV, "var_row2")
            s.activation(s_var[:], in_=s_var[:], func=AF.Ln)
            C.inc(s.activation(s_var[:], in_=s_var[:], func=AF.Exp, scale=-0.5), cS, 1, key="rstd2")
            M["cS_total"] = C.val(cS)

        @block.tensor
        def _tensor_(t):
            C.sec = "tensor"
            C.w(t, qS, "ld_c")
            for j in range(IT):
                C.w(t, cS, f"bexp{j}")
                if j > 0:
                    C.w(t, cV, f"bT_ev{j - 1}")
                C.inc(t.transpose(p_tr[0:NST, :], s_btb[:], seye[:]), cT, 1, key=f"bT{j}")
            C.w(t, qG, "ld_sc")
            for j in range(IT):
                C.w(t, cV, f"bT_ev{IT - 1}" if j == 0 else f"yd2_{j - 1}")
                for (c0, w) in CHO:
                    mc = t.matmul(p_big[:, c0:c0 + w], s_BT[j][:], s_scb[:, c0:c0 + w],
                                  start=True, stop=True)
                C.inc(mc, cT, 1, key=f"cproj{j}")
            for i in range(OT):
                C.w(t, qG, f"worow{i}")
                for j in range(IT):
                    tg = i * IT + j
                    C.w(t, cV, f"bT_ev{IT - 1}" if tg == 0 else f"wo_ev_g{tg - 1}")
                    C.inc(t.transpose(p_tr[:], s_wrow[i % 2][:, 128 * j:128 * (j + 1)], seye[:]),
                          cT, 1, key=f"woT_g{tg}")
                M[f"wo_free{i}"] = C.val(cT)
                C.w(t, cV, f"wo_ev_g{i * IT + IT - 1}")
                C.w(t, cV, f"yd2_{IT - 1}")
                if i == 0:
                    pass
                else:
                    C.w(t, cV, f"att_ev{i - 1}")
                for (c0, w) in CHO:
                    for j in range(IT):
                        ma = t.matmul(p_big[:, c0:c0 + w], s_lhsW[j][:], s_y[j][:, c0:c0 + w],
                                      start=(j == 0), stop=(j == IT - 1))
                C.inc(ma, cT, 1, key=f"att{i}")
            for ci, (c0, w) in enumerate(CHO):
                if ci > 0:
                    C.w(t, cV, f"st_ev{ci - 1}")
                for i in range(OT):
                    g2 = ci * OT + i
                    C.w(t, cS, f"rsq_g{g2}")
                    t.matmul(p_bc1[0:1, 0:w], sonesc[:], s_res[i][:, c0:c0 + w],
                             start=(i == 0), stop=(i == OT - 1))
                    C.inc(t.matmul(p_bc2[0:1, 0:w], sonesc[:], s_sq[:, 0:w],
                                   start=(i == 0), stop=(i == OT - 1)), cT, 1, key=f"rst_mm_g{g2}")
            C.w(t, cS, "rstd2")
            C.w(t, cV, "mr_row2")
            for ci, (c0, w) in enumerate(CHO):
                if ci > 0:
                    C.w(t, cV, f"ln2_ev{ci - 1}")
                t.matmul(p_bc1[:, 0:w], sonesr[:], s_var[:, c0:c0 + w], start=True, stop=True)
                C.inc(t.matmul(p_bc2[:, 0:w], sonesr[:], s_mr[:, c0:c0 + w], start=True, stop=True),
                      cT, 1, key=f"bc2_{ci}")

        @block.vector
        def _vector_(v):
            C.sec = "vector"
            for j in range(IT):
                C.w(v, cT, f"bT{j}")
                C.inc(v.tensor_copy(s_BT[j][:], p_tr[0:NST, :]), cV, 1, key=f"bT_ev{j}")
            for j in range(IT):
                C.w(v, cT, f"cproj{j}")
                C.w(v, qG, f"hld{j}")
                C.w(v, cS, f"silz{j}")
                v.tensor_tensor(s_y[j][:], s_h[:], p_big[:], ALU.mult)
                C.inc(v.tensor_tensor(s_y[j][:], s_y[j][:], s_slz[:], ALU.mult), cV, 1,
                      key=f"yd2_{j}")
            for i in range(OT):
                for j in range(IT):
                    tg = i * IT + j
                    C.w(v, cT, f"woT_g{tg}")
                    C.inc(v.tensor_copy(s_lhsW[j][:], p_tr[:]), cV, 1, key=f"wo_ev_g{tg}")
                C.w(v, cT, f"att{i}")
                C.w(v, qG, f"xlnld{i}")
                C.inc(v.tensor_tensor(s_res[i][:], s_xlnt[:], p_big[:], ALU.add), cV, 1,
                      key=f"att_ev{i}")
            for ci, (c0, w) in enumerate(CHO):
                C.w(v, cT, f"rst_mm_g{ci * OT + OT - 1}")
                v.tensor_copy(s_mu[:, c0:c0 + w], p_bc1[0:1, 0:w])
                C.inc(v.tensor_copy(s_var[:, c0:c0 + w], p_bc2[0:1, 0:w]), cV, 1, key=f"st_ev{ci}")
            v.wait_ge(cV, C.val(cV))
            v.tensor_scalar(s_mu[:], s_mu[:], 1.0 / D, None, ALU.mult)
            v.tensor_scalar(s_var[:], s_var[:], 1.0 / D, None, ALU.mult)
            v.tensor_tensor(s_mr[:], s_mu[:], s_mu[:], ALU.mult)
            v.tensor_tensor(s_var[:], s_var[:], s_mr[:], ALU.subtract)
            C.inc(v.tensor_scalar(s_var[:], s_var[:], float(LN_EPS), None, ALU.add), cV, 1,
                  key="var_row2")
            C.w(v, cS, "rstd2")
            C.inc(v.tensor_tensor(s_mr[:], s_mu[:], s_var[:], ALU.mult), cV, 1, key="mr_row2")
            for ci, (c0, w) in enumerate(CHO):
                C.w(v, cT, f"bc2_{ci}")
                for i in range(OT):
                    v.scalar_tensor_tensor(s_res[i][:, c0:c0 + w], s_res[i][:, c0:c0 + w], 1.0,
                                           p_bc1[:, 0:w], ALU.mult, ALU.mult)
                    last = v.tensor_tensor(s_res[i][:, c0:c0 + w], s_res[i][:, c0:c0 + w],
                                           p_bc2[:, 0:w], ALU.subtract)
                C.inc(last, cV, 1, key=f"ln2_ev{ci}")
            for i in range(OT):
                C.w(v, qG, f"xld{i}")
                v.tensor_scalar(s_res[i][:], s_res[i][:], s_g[:, i:i + 1], s_bb[:, i:i + 1],
                                ALU.mult, op1=ALU.add)
                C.inc(v.tensor_tensor(s_res[i][:], s_res[i][:], s_xt[:], ALU.add), cV, 1, key=f"x1_{i}")

    return nc


def build_l0b():
    return two_pass(emit_l0b)



def emit_moer(C):
    nc = bass.Bass("TRN2", target_bir_lowering=False)
    M = C.M
    x1_in = nc.dram_tensor("x1", [D, TC], F32, kind="ExternalInput")
    wr_in = nc.dram_tensor("w_router", [D, 8], F32, kind="ExternalInput")
    iota8 = nc.dram_tensor("iota8", [128, 8], F32, kind="ExternalInput")
    tri = nc.dram_tensor("tri128", [128, 128], BF16, kind="ExternalInput")
    onesc = nc.dram_tensor("onesc", [128, 1], F32, kind="ExternalInput")
    onesr = nc.dram_tensor("onesr", [1, 128], F32, kind="ExternalInput")
    onescb = nc.dram_tensor("onescb", [128, 1], BF16, kind="ExternalInput")

    o_xln = nc.dram_tensor("o_xln1", [D, TC], BF16, kind="ExternalOutput")
    o_meta = nc.dram_tensor("o_meta", [TC, 8], F32, kind="ExternalOutput")
    o_cnt = nc.dram_tensor("o_cnt", [1, 8], F32, kind="ExternalOutput")

    DT = D // 128
    TT_ = TC // 128
    CHO = t_chunks(TC)

    es = ExitStack()
    sb = lambda n, s, d=F32: es.enter_context(nc.sbuf_tensor(n, s, d))
    ps = lambda n, s, d=F32: es.enter_context(nc.psum_tensor(n, s, d))

    sx = [sb(f"sx{i}", [128, TC]) for i in range(DT)]
    sxb = [sb(f"sxb{i}", [128, TC], BF16) for i in range(DT)]
    ssq = sb("ssq", [128, 512])
    sonesc = sb("sonesc", [128, 1])
    sonesr = sb("sonesr", [1, 128])
    sonescb = sb("sonescb", [128, 1], BF16)
    stri = sb("stri", [128, 128], BF16)
    sio = sb("sio", [128, 8])
    s_wr = [sb(f"s_wr{i}", [128, 8], BF16) for i in range(DT)]
    s_mu = sb("s_mu", [1, TC])
    s_var = sb("s_var", [1, TC])
    s_mr = sb("s_mr", [1, TC])
    s_prob = sb("s_prob", [128, 8])
    s_col = sb("s_col", [128, 1])
    s_mx = sb("s_mx", [128, 8])
    s_mi = sb("s_mi", [128, 8], U32)
    s_mif = sb("s_mif", [128, 2])
    s_oh1 = sb("s_oh1", [128, 8])
    s_oh2 = sb("s_oh2", [128, 8])
    s_ind = sb("s_ind", [128, 8], BF16)
    s_pre = sb("s_pre", [128, 8])
    s_t8 = sb("s_t8", [128, 8])
    s_offs = sb("s_offs", [1, 8])
    s_crow = sb("s_crow", [1, 8])
    s_meta = sb("s_meta", [128, 8])

    p_bc1 = ps("p_bc1", [128, 512])
    p_bc2 = ps("p_bc2", [128, 512])
    p_sc = ps("p_sc", [128, 8])
    p_pre = ps("p_pre", [128, 8])
    p_off = ps("p_off", [128, 8])
    p_cr = ps("p_cr", [1, 8])

    qS = es.enter_context(nc.semaphore("qS"))
    qG = es.enter_context(nc.semaphore("qG"))
    cT = es.enter_context(nc.semaphore("cT"))
    cV = es.enter_context(nc.semaphore("cV"))
    cS = es.enter_context(nc.semaphore("cS"))

    with es, nc.Block() as block:
        @block.sync
        def _sync_(sync):
            C.sec = "sync"
            for i in range(DT):
                C.inc(sync.dma_start(out=sx[i][:], in_=x1_in[128 * i:128 * (i + 1), :]), qS, 16)
            C.inc(sync.dma_start(out=sonesc[:], in_=onesc[:]), qS, 16)
            C.inc(sync.dma_start(out=sonesr[:], in_=onesr[:]), qS, 16)
            C.inc(sync.dma_start(out=stri[:], in_=tri[:]), qS, 16)
            C.inc(sync.dma_start(out=sonescb[:], in_=onescb[:]), qS, 16)
            C.inc(sync.dma_start(out=sio[:], in_=iota8[:]), qS, 16, key="ld_x")
            for i in range(DT):
                C.w(sync, cV, f"bf16{i}")
                C.inc(sync.dma_start(out=o_xln[128 * i:128 * (i + 1), :], in_=sxb[i][:]), qS, 16)
            for tt in range(TT_):
                C.w(sync, cV, f"meta{tt}")
                C.inc(sync.dma_start(out=o_meta[128 * tt:128 * (tt + 1), :], in_=s_meta[:]), qS, 16,
                      key=f"metaout{tt}")

            C.w(sync, cV, "cnt_done")
            C.inc(sync.dma_start(out=o_cnt[:], in_=s_offs[:]), qS, 16)
            C.wv(sync, qS, C.val(qS))
            C.w(sync, qG, "qG_total")

        @block.gpsimd
        def _gpsimd_(g):
            C.sec = "gpsimd"
            C.inc(g.dma_start(out=s_wr[0][:], in_=wr_in[0:128, :]), qG, 16)
            for i in range(1, DT):
                C.inc(g.dma_start(out=s_wr[i][:], in_=wr_in[128 * i:128 * (i + 1), :]), qG, 16)
            M["qG_total"] = C.val(qG)

        @block.scalar
        def _scalar_(s):
            C.sec = "scalar"
            C.w(s, qS, "ld_x")
            for ci, (c0, w) in enumerate(CHO):
                for i in range(DT):
                    g2 = ci * DT + i
                    if g2 > 0:
                        C.w(s, cT, f"st_mm_g{g2 - 1}")
                    C.inc(s.activation(ssq[:, 0:w], in_=sx[i][:, c0:c0 + w], func=AF.Square),
                          cS, 1, key=f"sq_g{g2}")
            C.w(s, cV, "var_row")
            s.activation(s_var[:], in_=s_var[:], func=AF.Ln)
            C.inc(s.activation(s_var[:], in_=s_var[:], func=AF.Exp, scale=-0.5), cS, 1, key="rstd")
            for tt in range(TT_):
                C.w(s, cV, f"negmax{tt}")
                C.inc(s.activation(s_prob[:], in_=p_sc[:], func=AF.Exp, bias=s_col[:]),
                      cS, 1, key=f"sexp{tt}")
            M["cS_total"] = C.val(cS)

        @block.tensor
        def _tensor_(t):
            C.sec = "tensor"
            C.w(t, qS, "ld_x")
            for ci, (c0, w) in enumerate(CHO):
                if ci > 0:
                    C.w(t, cV, f"st_ev{ci - 1}")
                for i in range(DT):
                    g2 = ci * DT + i
                    C.w(t, cS, f"sq_g{g2}")
                    t.matmul(p_bc1[0:1, 0:w], sonesc[:], sx[i][:, c0:c0 + w],
                             start=(i == 0), stop=(i == DT - 1))
                    C.inc(t.matmul(p_bc2[0:1, 0:w], sonesc[:], ssq[:, 0:w],
                                   start=(i == 0), stop=(i == DT - 1)), cT, 1, key=f"st_mm_g{g2}")
            C.w(t, cS, "rstd")
            C.w(t, cV, "mr_row")
            for ci, (c0, w) in enumerate(CHO):
                if ci > 0:
                    C.w(t, cV, f"ln_ev{ci - 1}")
                t.matmul(p_bc1[:, 0:w], sonesr[:], s_var[:, c0:c0 + w], start=True, stop=True)
                C.inc(t.matmul(p_bc2[:, 0:w], sonesr[:], s_mr[:, c0:c0 + w], start=True, stop=True),
                      cT, 1, key=f"bc{ci}")
            C.w(t, qG, "qG_total")
            for tt in range(TT_):
                if tt > 0:
                    C.w(t, cS, f"sexp{tt - 1}")
                for i in range(DT):
                    C.w(t, cV, f"bf16{i}")
                    mmx = t.matmul(p_sc[:], sxb[i][:, 128 * tt:128 * (tt + 1)], s_wr[i][:],
                                   start=(i == 0), stop=(i == DT - 1))
                C.inc(mmx, cT, 1, key=f"sc_mm{tt}")
                C.w(t, cV, f"ind{tt}")
                t.matmul(p_pre[:], stri[:], s_ind[:], start=True, stop=True)
                C.inc(t.matmul(p_cr[:], sonescb[:], s_ind[:], start=True, stop=True),
                      cT, 1, key=f"pre_mm{tt}")
                C.w(t, cV, f"offs{tt}")
                C.inc(t.matmul(p_off[:], sonesr[:], s_offs[:], start=True, stop=True),
                      cT, 1, key=f"off_mm{tt}")

        @block.vector
        def _vector_(v):
            C.sec = "vector"
            for ci, (c0, w) in enumerate(CHO):
                C.w(v, cT, f"st_mm_g{ci * DT + DT - 1}")
                v.tensor_copy(s_mu[:, c0:c0 + w], p_bc1[0:1, 0:w])
                C.inc(v.tensor_copy(s_var[:, c0:c0 + w], p_bc2[0:1, 0:w]), cV, 1, key=f"st_ev{ci}")
            v.wait_ge(cV, C.val(cV))
            v.tensor_scalar(s_mu[:], s_mu[:], 1.0 / D, None, ALU.mult)
            v.tensor_scalar(s_var[:], s_var[:], 1.0 / D, None, ALU.mult)
            v.tensor_tensor(s_mr[:], s_mu[:], s_mu[:], ALU.mult)
            v.tensor_tensor(s_var[:], s_var[:], s_mr[:], ALU.subtract)
            C.inc(v.tensor_scalar(s_var[:], s_var[:], float(LN_EPS), None, ALU.add), cV, 1,
                  key="var_row")
            C.w(v, cS, "rstd")
            C.inc(v.tensor_tensor(s_mr[:], s_mu[:], s_var[:], ALU.mult), cV, 1, key="mr_row")
            for ci, (c0, w) in enumerate(CHO):
                C.w(v, cT, f"bc{ci}")
                for i in range(DT):
                    v.scalar_tensor_tensor(sx[i][:, c0:c0 + w], sx[i][:, c0:c0 + w], 1.0,
                                           p_bc1[:, 0:w], ALU.mult, ALU.mult)
                    last = v.tensor_tensor(sx[i][:, c0:c0 + w], sx[i][:, c0:c0 + w],
                                           p_bc2[:, 0:w], ALU.subtract)
                C.inc(last, cV, 1, key=f"ln_ev{ci}")
            for i in range(DT):
                C.inc(v.tensor_copy(sxb[i][:], sx[i][:]), cV, 1, key=f"bf16{i}")
            C.inc(v.memset(s_offs[:], 0.0), cV, 1, key="offs_init")
            for tt in range(TT_):
                bar = lambda i: C.bar(i, v, cV)
                C.w(v, cT, f"sc_mm{tt}")
                bar(v.tensor_reduce(s_col[:], p_sc[:], AX.X, ALU.max))
                C.inc(v.tensor_scalar(s_col[:], s_col[:], -1.0, None, ALU.mult), cV, 1,
                      key=f"negmax{tt}")
                C.w(v, cS, f"sexp{tt}")
                bar(v.tensor_reduce(s_col[:], s_prob[:], AX.X, ALU.add))
                bar(v.reciprocal(s_col[:], s_col[:]))
                bar(v.tensor_scalar(s_prob[:], s_prob[:], s_col[:], None, ALU.mult))
                bar(v.max(out=s_mx[:], in_=s_prob[:]))
                bar(v.tensor_scalar(s_oh1[:], s_prob[:], s_mx[:, 0:1], None, ALU.is_equal))
                bar(v.tensor_scalar(s_oh2[:], s_prob[:], s_mx[:, 1:2], None, ALU.is_equal))
                C.inc(v.tensor_tensor(s_ind[:], s_oh1[:], s_oh2[:], ALU.add), cV, 1, key=f"ind{tt}")
                M[f"offs{tt}"] = C.M.get("offs_init", 0) if tt == 0 else M[f"offsup{tt - 1}"]
                C.w(v, cT, f"pre_mm{tt}")
                C.w(v, cT, f"off_mm{tt}")
                bar(v.tensor_copy(s_pre[:], p_pre[:]))
                bar(v.tensor_tensor(s_pre[:], s_pre[:], p_off[:], ALU.add))
                if tt > 0:
                    C.w(v, qS, f"metaout{tt - 1}")
                bar(v.tensor_tensor(s_t8[:], s_oh1[:], sio[:], ALU.mult))
                bar(v.tensor_reduce(s_meta[:, 4:5].rearrange("p (a o) -> p a o", o=1),
                                    s_t8[:].rearrange("p (a e) -> p a e", e=8), AX.X, ALU.add))
                bar(v.tensor_tensor(s_t8[:], s_oh2[:], sio[:], ALU.mult))
                bar(v.tensor_reduce(s_meta[:, 5:6].rearrange("p (a o) -> p a o", o=1),
                                    s_t8[:].rearrange("p (a e) -> p a e", e=8), AX.X, ALU.add))
                bar(v.tensor_copy(s_meta[:, 2:4], s_mx[:, 0:2]))
                bar(v.memset(s_meta[:, 6:8], 0.0))
                bar(v.tensor_tensor(s_t8[:], s_oh1[:], s_pre[:], ALU.mult))
                bar(v.tensor_reduce(s_meta[:, 0:1].rearrange("p (a o) -> p a o", o=1),
                                    s_t8[:].rearrange("p (a e) -> p a e", e=8), AX.X, ALU.add))
                bar(v.tensor_tensor(s_t8[:], s_oh2[:], s_pre[:], ALU.mult))
                C.inc(v.tensor_reduce(s_meta[:, 1:2].rearrange("p (a o) -> p a o", o=1),
                                      s_t8[:].rearrange("p (a e) -> p a e", e=8), AX.X, ALU.add),
                      cV, 1, key=f"meta{tt}")
                bar(v.tensor_copy(s_crow[:], p_cr[:]))
                C.inc(v.tensor_tensor(s_offs[:], s_offs[:], s_crow[:], ALU.add), cV, 1,
                      key=f"offsup{tt}")
            M["cnt_done"] = M[f"offsup{TT_ - 1}"]

    return nc


def build_moer():
    return two_pass(emit_moer)


def build_l1a():
    return two_pass(emit_l1a)



def emit_moed(C):
    nc = bass.Bass("TRN2", target_bir_lowering=False)
    M = C.M
    x1_in = nc.dram_tensor("x1", [D, TC], F32, kind="ExternalInput")
    xln_in = nc.dram_tensor("xln1", [D, TC], BF16, kind="ExternalInput")
    meta_in = nc.dram_tensor("meta", [TC, 8], F32, kind="ExternalInput")
    cnt_in = nc.dram_tensor("cnt_all", [8, 8], F32, kind="ExternalInput")
    sel_in = nc.dram_tensor("sel", [8, 1], F32, kind="ExternalInput")
    w1_in = nc.dram_tensor("w1", [8, D, 4096], F32, kind="ExternalInput")
    w2_in = nc.dram_tensor("w2", [8, 4096, D], F32, kind="ExternalInput")
    bias_in = nc.dram_tensor("bias", [D], F32, kind="ExternalInput")
    iota8 = nc.dram_tensor("iota8", [128, 8], F32, kind="ExternalInput")
    eyef = nc.dram_tensor("eyef", [128, 128], F32, kind="ExternalInput")
    onesr = nc.dram_tensor("onesr", [1, 128], F32, kind="ExternalInput")

    o_x2 = nc.dram_tensor("o_x2", [D, TC], F32, kind="ExternalOutput")

    DT = D // 128
    FT = 4096 // 128
    TT_ = TC // 128
    NCH = 4
    E = 8

    es = ExitStack()
    sb = lambda n, s, d=F32: es.enter_context(nc.sbuf_tensor(n, s, d))
    ps = lambda n, s, d=F32: es.enter_context(nc.psum_tensor(n, s, d))

    sxln = [sb(f"sxln{i}", [128, TC], BF16) for i in range(DT)]
    s_acc = [sb(f"s_acc{i}", [128, TC], BF16) for i in range(DT)]
    s_h = [sb(f"s_h{j}", [128, 512], BF16) for j in range(FT)]
    s_w1 = [sb(f"s_w1{k}", [128, 4096], BF16) for k in range(DT)]
    s_w2 = [sb(f"s_w2{k}", [128, D], BF16) for k in range(2)]
    s_meta = sb("s_meta", [128, 8])
    s_cnt = sb("s_cnt", [8, 8])
    s_sel = sb("s_sel", [8, 1])
    s_gp = sb("s_gp", [1, 8])
    s_gpb = sb("s_gpb", [128, 8])
    s_oh = sb("s_oh", [128, 8])
    s_tmp8 = sb("s_tmp8", [128, 8])
    s_col = sb("s_col", [128, 1])
    s_col2 = sb("s_col2", [128, 1])
    s_ct = [sb(f"s_ct{tt}", [128, 8]) for tt in range(TC // 128)]
    s_coefT = sb("s_coefT", [128, 8])
    s_crow = sb("s_crow", [1, 512])
    s_cb = sb("s_cb", [128, 512])
    s_ytmp = sb("s_ytmp", [128, 512])
    s_bias = sb("s_bias", [128, DT])
    s_x1 = sb("s_x1", [128, TC])
    s_x2 = sb("s_x2", [128, TC])
    sio = sb("sio", [128, 8])
    seyef = sb("seyef", [128, 128])
    sonesr = sb("sonesr", [1, 128])

    p_pool = ps("p_pool", [128, 4096])
    p_h = [p_pool[:, 0:512], p_pool[:, 512:1024]]
    p_y = [p_pool[:, 512 * o:512 * (o + 1)] for o in range(8)]
    p_gp = p_pool[0:1, 0:8]
    p_row = p_pool[0:1, 1024:1536]
    p_cbc = p_pool[:, 1536:2048]

    qS = es.enter_context(nc.semaphore("qS"))
    qG = es.enter_context(nc.semaphore("qG"))
    cT = es.enter_context(nc.semaphore("cT"))
    cV = es.enter_context(nc.semaphore("cV"))
    cS = es.enter_context(nc.semaphore("cS"))

    def prev_ec(e, ch):
        if ch > 0:
            return e, ch - 1
        return e - 1, NCH - 1

    with es, nc.Block() as block:
        @block.sync
        def _sync_(sync):
            C.sec = "sync"
            C.inc(sync.dma_start(out=s_cnt[:], in_=cnt_in[:]), qS, 16)
            C.inc(sync.dma_start(out=s_sel[:], in_=sel_in[:]), qS, 16)
            C.inc(sync.dma_start(out=sio[:], in_=iota8[:]), qS, 16)
            C.inc(sync.dma_start(out=seyef[:], in_=eyef[:]), qS, 16)
            C.inc(sync.dma_start(out=sonesr[:], in_=onesr[:]), qS, 16)
            for _t in range(DT):
                C.inc(sync.dma_start(out=s_bias[:, _t:_t + 1],
                                     in_=bias_in[128 * _t:128 * (_t + 1)].unsqueeze(1)), qS, 16,
                      key="ld_c" if _t == DT - 1 else None)
            for i in range(DT):
                C.inc(sync.dma_start(out=sxln[i][:], in_=xln_in[128 * i:128 * (i + 1), :]),
                      qS, 16, key=f"xlnld{i}")
            for tt in range(TT_):
                C.w(sync, cV, f"coefT{tt - 1}" if tt > 0 else "none")
                C.inc(sync.dma_start(out=s_meta[:], in_=meta_in[128 * tt:128 * (tt + 1), :]),
                      qS, 16, key=f"meta{tt}")
            for i in range(DT):
                C.w(sync, cV, f"x2_{i - 1}" if i > 0 else "none")
                C.inc(sync.dma_start(out=s_x1[:], in_=x1_in[128 * i:128 * (i + 1), :]), qS, 16,
                      key=f"x1ld{i}")
                C.w(sync, cV, f"x2_{i}")
                C.inc(sync.dma_start(out=o_x2[128 * i:128 * (i + 1), :], in_=s_x2[:]), qS, 16,
                      key=f"x2out{i}")
            C.wv(sync, qS, C.val(qS))
            C.w(sync, qG, "qG_total")
            C.w(sync, cS, "cS_total")

        @block.gpsimd
        def _gpsimd_(g):
            C.sec = "gpsimd"
            for e in range(E):
                for k in range(DT):
                    if k == 0 and e > 0:
                        C.w(g, cT, f"w1free_e{e - 1}")
                    C.inc(g.dma_start(out=s_w1[k][:], in_=w1_in[e, 128 * k:128 * (k + 1), :]),
                          qG, 16, key=f"w1ld_e{e}_k{k}")
                for ch in range(NCH):
                    for k in range(FT):
                        kg = (e * NCH + ch) * FT + k
                        if kg >= 2:
                            C.w(g, cT, f"w2used_g{kg - 2}")
                        C.inc(g.dma_start(out=s_w2[kg % 2][:],
                                          in_=w2_in[e, 128 * k:128 * (k + 1), :]),
                              qG, 16, key=f"w2ld_g{kg}")
            M["qG_total"] = C.val(qG)

        @block.scalar
        def _scalar_(s):
            C.sec = "scalar"
            for e in range(E):
                for ch in range(NCH):
                    for j in range(FT):
                        hg = (e * NCH + ch) * FT + j
                        C.w(s, cT, f"h_mm_g{hg}")
                        C.inc(s.activation(s_h[j][:], in_=p_h[j % 2], func=AF.Gelu_apprx_tanh),
                              cS, 1, key=f"gelu_g{hg}")
            M["cS_total"] = C.val(cS)

        @block.tensor
        def _tensor_(t):
            C.sec = "tensor"
            C.w(t, qS, "ld_c")
            C.inc(t.matmul(p_gp, s_sel[:], s_cnt[:], start=True, stop=True), cT, 1, key="gp_mm")
            C.w(t, cV, "gp_ev")
            C.inc(t.matmul(p_pool[:, 1024:1032], sonesr[:], s_gp[:], start=True, stop=True),
                  cT, 1, key="gpb_mm")
            C.w(t, cV, f"coefT{TT_ - 1}")
            for k in range(DT):
                C.w(t, qS, f"xlnld{k}")
            for e in range(E):
                for k in range(DT):
                    C.w(t, qG, f"w1ld_e{e}_k{k}")
                for ch in range(NCH):
                    c0 = 512 * ch
                    pe, pc = prev_ec(e, ch)
                    for j in range(FT):
                        hg = (e * NCH + ch) * FT + j
                        if hg >= 2:
                            C.w(t, cS, f"gelu_g{hg - 2}")
                        if j < 2:
                            if e == 0 and ch == 0:
                                C.w(t, cV, "acc_init")
                            else:
                                C.w(t, cV, f"accdone_e{pe}_c{pc}")
                        for k in range(DT):
                            mh = t.matmul(p_h[j % 2], s_w1[k][:, 128 * j:128 * (j + 1)],
                                          sxln[k][:, c0:c0 + 512],
                                          start=(k == 0), stop=(k == DT - 1))
                        C.inc(mh, cT, 1, key=f"h_mm_g{hg}")
                    if e > 0 or ch > 0:
                        C.w(t, cV, f"crow_ev_e{pe}_c{pc}")
                    for q in range(4):
                        mx = t.matmul(p_row[:, 128 * q:128 * (q + 1)],
                                      s_ct[4 * ch + q][:, e:e + 1], seyef[:],
                                      start=True, stop=True)
                    C.inc(mx, cT, 1, key=f"crow_mm_e{e}_c{ch}")
                    C.w(t, cV, f"crow_ev2_e{e}_c{ch}")
                    C.inc(t.matmul(p_cbc, sonesr[:], s_crow[:],
                                   start=True, stop=True), cT, 1, key=f"cb_mm_e{e}_c{ch}")
                    C.w(t, cS, f"gelu_g{(e * NCH + ch) * FT + FT - 1}")
                    C.w(t, cV, f"cb_ev_e{e}_c{ch}")
                    for k in range(FT):
                        kg = (e * NCH + ch) * FT + k
                        C.w(t, qG, f"w2ld_g{kg}")
                        for o in range(DT):
                            my = t.matmul(p_y[o], s_w2[kg % 2][:, 128 * o:128 * (o + 1)],
                                          s_h[k][:], start=(k == 0), stop=(k == FT - 1))
                        C.inc(my, cT, 1, key=f"w2used_g{kg}")
                    M[f"y_mm_e{e}_c{ch}"] = C.val(cT)
                M[f"w1free_e{e}"] = C.val(cT)

        @block.vector
        def _vector_(v):
            C.sec = "vector"
            for i in range(DT):
                last = v.memset(s_acc[i][:], 0.0)
            C.inc(last, cV, 1, key="acc_init")
            C.w(v, cT, "gp_mm")
            C.inc(v.tensor_copy(s_gp[:], p_gp), cV, 1, key="gp_ev")
            C.w(v, cT, "gpb_mm")
            C.inc(v.tensor_copy(s_gpb[:], p_pool[:, 1024:1032]), cV, 1, key="gpb_ev")
            for tt in range(TT_):
                bar = lambda i: C.bar(i, v, cV)
                C.w(v, qS, f"meta{tt}")
                bar(v.tensor_scalar(s_oh[:], sio[:], s_meta[:, 4:5], None, ALU.is_equal))
                bar(v.tensor_tensor(s_tmp8[:], s_oh[:], s_gpb[:], ALU.mult))
                bar(v.tensor_reduce(s_col[:].rearrange("p (a o) -> p a o", o=1),
                                    s_tmp8[:].rearrange("p (a e) -> p a e", e=8), AX.X, ALU.add))
                bar(v.tensor_tensor(s_col[:], s_col[:], s_meta[:, 0:1], ALU.add))
                bar(v.tensor_scalar(s_col[:], s_col[:], 4096.0, None, ALU.is_lt))
                bar(v.tensor_tensor(s_col[:], s_col[:], s_meta[:, 2:3], ALU.mult))
                bar(v.tensor_scalar(s_coefT[:], s_oh[:], s_col[:], None, ALU.mult))
                bar(v.tensor_scalar(s_oh[:], sio[:], s_meta[:, 5:6], None, ALU.is_equal))
                bar(v.tensor_tensor(s_tmp8[:], s_oh[:], s_gpb[:], ALU.mult))
                bar(v.tensor_reduce(s_col2[:].rearrange("p (a o) -> p a o", o=1),
                                    s_tmp8[:].rearrange("p (a e) -> p a e", e=8), AX.X, ALU.add))
                bar(v.tensor_tensor(s_col2[:], s_col2[:], s_meta[:, 1:2], ALU.add))
                bar(v.tensor_scalar(s_col2[:], s_col2[:], 4096.0, None, ALU.is_lt))
                bar(v.tensor_tensor(s_col2[:], s_col2[:], s_meta[:, 3:4], ALU.mult))
                bar(v.tensor_scalar(s_oh[:], s_oh[:], s_col2[:], None, ALU.mult))
                C.inc(v.tensor_tensor(s_ct[tt][:], s_coefT[:], s_oh[:], ALU.add), cV, 1,
                      key=f"coefT{tt}")
            for e in range(E):
                for ch in range(NCH):
                    c0 = 512 * ch
                    C.w(v, cT, f"crow_mm_e{e}_c{ch}")
                    C.inc(v.tensor_copy(s_crow[:], p_row), cV, 1, key=f"crow_ev2_e{e}_c{ch}")
                    M[f"crow_ev_e{e}_c{ch}"] = C.M[f"crow_ev2_e{e}_c{ch}"]
                    C.w(v, cT, f"cb_mm_e{e}_c{ch}")
                    C.inc(v.tensor_copy(s_cb[:], p_cbc), cV, 1, key=f"cb_ev_e{e}_c{ch}")
                    C.w(v, cT, f"y_mm_e{e}_c{ch}")
                    for o in range(DT):
                        v.tensor_tensor(s_ytmp[:], p_y[o], s_cb[:], ALU.mult)
                        last = v.tensor_tensor(s_acc[o][:, c0:c0 + 512], s_acc[o][:, c0:c0 + 512],
                                               s_ytmp[:], ALU.add)
                    C.inc(last, cV, 1, key=f"accdone_e{e}_c{ch}")
            for i in range(DT):
                C.w(v, qS, f"x1ld{i}")
                if i > 0:
                    C.w(v, qS, f"x2out{i - 1}")
                v.tensor_scalar(s_x2[:], s_acc[i][:], s_bias[:, i:i + 1], None, ALU.add)
                C.inc(v.tensor_tensor(s_x2[:], s_x2[:], s_x1[:], ALU.add), cV, 1, key=f"x2_{i}")

    return nc


def build_moed():
    return two_pass(emit_moed)



def emit_l1a(C):
    nc = bass.Bass("TRN2", target_bir_lowering=False)
    M = C.M
    x_in = nc.dram_tensor("x", [D, TP], F32, kind="ExternalInput")
    win_in = nc.dram_tensor("w_in", [D, 2 * DI], F32, kind="ExternalInput")
    convw_in = nc.dram_tensor("conv_w", [DI, 1, 4], F32, kind="ExternalInput")
    convb_in = nc.dram_tensor("conv_b", [DI], F32, kind="ExternalInput")
    wdt_in = nc.dram_tensor("w_dt", [DI, DI], F32, kind="ExternalInput")
    bdt_in = nc.dram_tensor("b_dt", [DI], F32, kind="ExternalInput")
    wxp_in = nc.dram_tensor("w_xproj", [DI, 2 * NST], F32, kind="ExternalInput")
    onesc = nc.dram_tensor("onesc", [128, 1], F32, kind="ExternalInput")
    onesr = nc.dram_tensor("onesr", [1, 128], F32, kind="ExternalInput")

    o_xc = nc.dram_tensor("o_xc", [DI, TC], BF16, kind="ExternalOutput")
    o_z = nc.dram_tensor("o_z", [DI, TC], BF16, kind="ExternalOutput")
    o_bssm = nc.dram_tensor("o_bssm", [NST, TC], F32, kind="ExternalOutput")
    o_dh = nc.dram_tensor("o_dh", [DI, TC], BF16, kind="ExternalOutput")
    o_dl = nc.dram_tensor("o_dl", [DI, TC], BF16, kind="ExternalOutput")
    o_xh = nc.dram_tensor("o_xh", [DI, TC], BF16, kind="ExternalOutput")
    o_xl = nc.dram_tensor("o_xl", [DI, TC], BF16, kind="ExternalOutput")

    DT = D // 128
    ET = 2 * DI // 128
    IT = DI // 128
    CH = t_chunks(TP)
    CHO = t_chunks(TC)

    es = ExitStack()
    sb = lambda n, s, d=F32: es.enter_context(nc.sbuf_tensor(n, s, d))
    ps = lambda n, s, d=F32: es.enter_context(nc.psum_tensor(n, s, d))

    sx = [sb(f"sx{i}", [128, TP]) for i in range(DT)]
    sxb = [sb(f"sxb{i}", [128, TP], BF16) for i in range(DT)]
    ssq = sb("ssq", [128, 512])
    sonesc = sb("sonesc", [128, 1])
    sonesr = sb("sonesr", [1, 128])
    s_mu = sb("s_mu", [1, TP])
    s_var = sb("s_var", [1, TP])
    s_mr = sb("s_mr", [1, TP])
    s_wcol = [sb(f"s_wcol{i}", [128, D], BF16) for i in range(2)]
    s_xz = sb("s_xz", [128, TP])
    s_conv = sb("s_conv", [128, TC])
    s_cw = sb("s_cw", [128, 64])
    s_cb = sb("s_cb", [128, 16])
    s_bdt = sb("s_bdt", [128, 16])
    s_xcb = [sb(f"s_xcb{i}", [128, TC], BF16) for i in range(2)]
    s_zb = [sb(f"s_zb{i}", [128, TC], BF16) for i in range(2)]
    s_xc = []
    for i in range(DT):
        s_xc.append(sx[i][:, 0:1024].bitcast(BF16))
        s_xc.append(sx[i][:, 1024:2048].bitcast(BF16))
    s_wdt = [sb(f"s_wdt{i}", [128, DI], BF16) for i in range(2)]
    s_wxp = [sb(f"s_wxp{k}", [128, 32], BF16) for k in range(IT)]
    s_dv = s_xz[:, 0:TC]
    s_abs = sb("s_abs", [128, TC])
    s_rel = s_conv[:]
    s_dhi = s_xcb[0][:]
    s_dlo = s_xcb[1][:]
    s_xhi = s_zb[0][:]
    s_xlo = s_zb[1][:]
    s_dxt = sb("s_dxt", [128, TC])
    s_sp32 = sb("s_sp32", [32, TC])

    p_pool = ps("p_pool", [128, 4096])
    p_bc1 = p_pool[:, 0:512]
    p_bc2 = p_pool[:, 512:1024]
    p_mm = p_pool[:, 1024:1536]
    p_dt = p_pool[:, 2048:4096]
    p_sp = p_pool[0:32, 0:2048]

    qS = es.enter_context(nc.semaphore("qS"))
    qG = es.enter_context(nc.semaphore("qG"))
    qA = es.enter_context(nc.semaphore("qA"))
    cT = es.enter_context(nc.semaphore("cT"))
    cV = es.enter_context(nc.semaphore("cV"))
    cS = es.enter_context(nc.semaphore("cS"))

    with es, nc.Block() as block:
        @block.sync
        def _sync_(sync):
            C.sec = "sync"
            for i in range(DT):
                C.inc(sync.dma_start(out=sx[i][:], in_=x_in[128 * i:128 * (i + 1), :]), qS, 16)
            C.inc(sync.dma_start(out=sonesc[:], in_=onesc[:]), qS, 16)
            C.inc(sync.dma_start(out=sonesr[:], in_=onesr[:]), qS, 16, key="ld_x")
            C.w(sync, cV, f"ln_ev{len(CH) - 1}")
            C.w(sync, cV, f"bf16{DT - 1}")
            for k in range(IT):
                C.w(sync, qG, f"xcout{k}")
                C.inc(sync.dma_start(out=s_xc[k], in_=o_xc[128 * k:128 * (k + 1), :]), qS, 16,
                      key=f"xcld{k}")
            C.wv(sync, qS, C.val(qS))
            C.w(sync, qG, "qG_total")
            C.w(sync, qA, "qA_total")
            C.w(sync, cS, "cS_total")
            C.w(sync, cV, "cV_total")

        @block.gpsimd
        def _gpsimd_(g):
            C.sec = "gpsimd"
            C.inc(g.dma_start(out=s_cw[:], in_=convw_in[:].rearrange("(t p) o k -> p t o k", p=128)), qG, 16)
            for _t in range(16):
                C.inc(g.dma_start(out=s_cb[:, _t:_t + 1],
                                  in_=convb_in[128 * _t:128 * (_t + 1)].unsqueeze(1)), qG, 16)
                C.inc(g.dma_start(out=s_bdt[:, _t:_t + 1],
                                  in_=bdt_in[128 * _t:128 * (_t + 1)].unsqueeze(1)), qG, 16,
                      key="ld_misc" if _t == 15 else None)
            for j in range(ET):
                if j >= 2:
                    C.w(g, cT, f"xz_done{j - 2}")
                C.inc(g.dma_start(out=s_wcol[j % 2][:],
                                  in_=win_in[:, 128 * j:128 * (j + 1)].rearrange(
                                      "(k p) e -> p k e", p=128)), qG, 16, key=f"wcol{j}")
                if j >= 2:
                    jo = j - 2
                    C.w(g, cS, f"act{jo}")
                    if jo < IT:
                        C.inc(g.dma_start(out=o_xc[128 * jo:128 * (jo + 1), :], in_=s_xcb[jo % 2][:]),
                              qG, 16, key=f"xcout{jo}")
                    else:
                        C.inc(g.dma_start(out=o_z[128 * (jo - IT):128 * (jo - IT + 1), :],
                                          in_=s_zb[jo % 2][:]), qG, 16, key=f"xcout{jo}")
            for jo in (ET - 2, ET - 1):
                C.w(g, cS, f"act{jo}")
                C.inc(g.dma_start(out=o_z[128 * (jo - IT):128 * (jo - IT + 1), :],
                                  in_=s_zb[jo % 2][:]), qG, 16, key=f"xcout{jo}")
            for k in range(IT):
                C.inc(g.dma_start(out=s_wxp[k][:], in_=wxp_in[128 * k:128 * (k + 1), :]), qG, 16,
                      key=f"wxp{k}")
            for o in range(IT):
                if o >= 2:
                    C.w(g, cT, f"dt_done{o - 2}")
                C.inc(g.dma_start(out=s_wdt[o % 2][:],
                                  in_=wdt_in[:, 128 * o:128 * (o + 1)].rearrange(
                                      "(k p) e -> p k e", p=128)), qG, 16, key=f"wdtcol{o}")
            M["qG_total"] = C.val(qG)

        @block.scalar
        def _scalar_(s):
            C.sec = "scalar"
            C.w(s, qS, "ld_x")
            for ci, (c0, w) in enumerate(CH):
                for i in range(DT):
                    g2 = ci * DT + i
                    if g2 > 0:
                        C.w(s, cT, f"st_mm_g{g2 - 1}")
                    C.inc(s.activation(ssq[:, 0:w], in_=sx[i][:, c0:c0 + w], func=AF.Square),
                          cS, 1, key=f"sq_g{g2}")
            C.w(s, cV, "var_row")
            s.activation(s_var[:], in_=s_var[:], func=AF.Ln)
            C.inc(s.activation(s_var[:], in_=s_var[:], func=AF.Exp, scale=-0.5), cS, 1, key="rstd")
            for j in range(ET):
                inner = j < IT
                C.w(s, cV, f"conv{j}" if inner else f"xz_ev{j}")
                if j >= 2:
                    C.w(s, qG, f"xcout{j - 2}")
                if inner:
                    C.inc(s.activation(s_xcb[j % 2][:], in_=s_conv[:],
                                       func=AF.Silu, bias=s_cb[:, j:j + 1]), cS, 1, key=f"act{j}")
                else:
                    C.inc(s.activation(s_zb[j % 2][:], in_=s_xz[:, PAD:], func=AF.Copy),
                          cS, 1, key=f"act{j}")
            def emit_douts(oo):
                C.w(s, cV, f"dxdone{oo}")
                C.inc(s.dma_start(out=o_dh[128 * oo:128 * (oo + 1), :], in_=s_dhi), qA, 16)
                C.inc(s.dma_start(out=o_dl[128 * oo:128 * (oo + 1), :], in_=s_dlo), qA, 16)
                C.inc(s.dma_start(out=o_xh[128 * oo:128 * (oo + 1), :], in_=s_xhi), qA, 16)
                C.inc(s.dma_start(out=o_xl[128 * oo:128 * (oo + 1), :], in_=s_xlo), qA, 16,
                      key=f"dout{oo}")
            for o in range(IT):
                C.w(s, cT, f"dt_mm{o}")
                if o > 0:
                    C.w(s, cV, f"absread{o - 1}")
                s.activation(s_abs[:], in_=p_dt, func=AF.Abs, bias=s_bdt[:, o:o + 1])
                s.activation(s_abs[:], in_=s_abs[:], func=AF.Exp, scale=-1.0)
                C.inc(s.activation(s_abs[:], in_=s_abs[:], func=AF.Ln, bias=1.0), cS, 1,
                      key=f"soft{o}")
                if o > 0:
                    emit_douts(o - 1)
            emit_douts(IT - 1)
            C.w(s, cV, "sp_ev")
            C.inc(s.dma_start(out=o_bssm[:], in_=s_sp32[16:32, :]), qA, 16)
            M["qA_total"] = C.val(qA)
            M["cS_total"] = C.val(cS)

        @block.tensor
        def _tensor_(t):
            C.sec = "tensor"
            C.w(t, qS, "ld_x")
            for ci, (c0, w) in enumerate(CH):
                if ci > 0:
                    C.w(t, cV, f"st_ev{ci - 1}")
                for i in range(DT):
                    g2 = ci * DT + i
                    C.w(t, cS, f"sq_g{g2}")
                    t.matmul(p_bc1[0:1, 0:w], sonesc[:], sx[i][:, c0:c0 + w],
                             start=(i == 0), stop=(i == DT - 1))
                    C.inc(t.matmul(p_bc2[0:1, 0:w], sonesc[:], ssq[:, 0:w],
                                   start=(i == 0), stop=(i == DT - 1)), cT, 1, key=f"st_mm_g{g2}")
            C.w(t, cS, "rstd")
            C.w(t, cV, "mr_row")
            for ci, (c0, w) in enumerate(CH):
                if ci > 0:
                    C.w(t, cV, f"ln_ev{ci - 1}")
                t.matmul(p_bc1[:, 0:w], sonesr[:], s_var[:, c0:c0 + w], start=True, stop=True)
                C.inc(t.matmul(p_bc2[:, 0:w], sonesr[:], s_mr[:, c0:c0 + w], start=True, stop=True),
                      cT, 1, key=f"bc{ci}")
            for i in range(DT):
                C.w(t, cV, f"bf16{i}")
            for j in range(ET):
                C.w(t, qG, f"wcol{j}")
                for ci, (c0, w) in enumerate(CH):
                    pg = j * len(CH) + ci
                    if pg > 0:
                        C.w(t, cV, f"xz_psfree_g{pg - 1}")
                    for i in range(DT):
                        mmx = t.matmul(p_mm[:, 0:w], s_wcol[j % 2][:, 128 * i:128 * (i + 1)],
                                       sxb[i][:, c0:c0 + w], start=(i == 0), stop=(i == DT - 1))
                    C.inc(mmx, cT, 1, key=f"xz_mm_g{pg}")
                M[f"xz_done{j}"] = C.val(cT)
            C.w(t, cV, f"xz_psfree_g{ET * len(CH) - 1}")
            for k in range(IT):
                C.w(t, qS, f"xcld{k}")
                C.w(t, qG, f"wxp{k}")
                for (c0, w) in CHO:
                    mms = t.matmul(p_sp[:, c0:c0 + w], s_wxp[k][:], s_xc[k][:, c0:c0 + w],
                                   start=(k == 0), stop=(k == IT - 1))
                C.inc(mms, cT, 1, key=f"sp_mm{k}")
            for o in range(IT):
                C.w(t, qG, f"wdtcol{o}")
                if o > 0:
                    C.w(t, cS, f"soft{o - 1}")
                    C.w(t, cV, f"dvread{o - 1}")
                for (c0, w) in CHO:
                    for k in range(IT):
                        mmd = t.matmul(p_dt[:, c0:c0 + w], s_wdt[o % 2][:, 128 * k:128 * (k + 1)],
                                       s_xc[k][:, c0:c0 + w], start=(k == 0), stop=(k == IT - 1))
                C.inc(mmd, cT, 1, key=f"dt_mm{o}")
                M[f"dt_done{o}"] = C.val(cT)

        @block.vector
        def _vector_(v):
            C.sec = "vector"
            for ci, (c0, w) in enumerate(CH):
                C.w(v, cT, f"st_mm_g{ci * DT + DT - 1}")
                v.tensor_copy(s_mu[:, c0:c0 + w], p_bc1[0:1, 0:w])
                C.inc(v.tensor_copy(s_var[:, c0:c0 + w], p_bc2[0:1, 0:w]), cV, 1, key=f"st_ev{ci}")
            C.bar(v.engine_nop() if False else v.tensor_scalar(s_mu[:, 0:1], s_mu[:, 0:1], 1.0, None, ALU.mult), v, cV)
            v.wait_ge(cV, C.val(cV))
            v.tensor_scalar(s_mu[:], s_mu[:], 1.0 / D, None, ALU.mult)
            v.tensor_scalar(s_var[:], s_var[:], 1.0 / D, None, ALU.mult)
            v.tensor_tensor(s_mr[:], s_mu[:], s_mu[:], ALU.mult)
            v.tensor_tensor(s_var[:], s_var[:], s_mr[:], ALU.subtract)
            C.inc(v.tensor_scalar(s_var[:], s_var[:], float(LN_EPS), None, ALU.add), cV, 1,
                  key="var_row")
            C.w(v, cS, "rstd")
            C.inc(v.tensor_tensor(s_mr[:], s_mu[:], s_var[:], ALU.mult), cV, 1, key="mr_row")
            for ci, (c0, w) in enumerate(CH):
                C.w(v, cT, f"bc{ci}")
                for i in range(DT):
                    v.scalar_tensor_tensor(sx[i][:, c0:c0 + w], sx[i][:, c0:c0 + w], 1.0,
                                           p_bc1[:, 0:w], ALU.mult, ALU.mult)
                    last = v.tensor_tensor(sx[i][:, c0:c0 + w], sx[i][:, c0:c0 + w],
                                           p_bc2[:, 0:w], ALU.subtract)
                C.inc(last, cV, 1, key=f"ln_ev{ci}")
            for i in range(DT):
                C.inc(v.tensor_copy(sxb[i][:], sx[i][:]), cV, 1, key=f"bf16{i}")
            for j in range(ET):
                if j >= 1:
                    C.w(v, cS, f"act{j - 1}")
                for ci, (c0, w) in enumerate(CH):
                    pg = j * len(CH) + ci
                    C.w(v, cT, f"xz_mm_g{pg}")
                    C.inc(v.tensor_copy(s_xz[:, c0:c0 + w], p_mm[:, 0:w]), cV, 1,
                          key=f"xz_psfree_g{pg}")
                M[f"xz_ev{j}"] = C.val(cV)
                if j < IT:
                    for k in range(4):
                        src = s_xz[:, PAD - 3 + k: PAD - 3 + k + TC]
                        wcol = s_cw[:, 4 * j + k: 4 * j + k + 1]
                        if k == 0:
                            v.tensor_scalar(s_conv[:], src, wcol, None, ALU.mult)
                        else:
                            last = v.scalar_tensor_tensor(s_conv[:], src, wcol, s_conv[:],
                                                          ALU.mult, ALU.add)
                    C.inc(last, cV, 1, key=f"conv{j}")
            C.w(v, cT, f"sp_mm{IT - 1}")
            C.inc(v.tensor_copy(s_sp32[:], p_sp), cV, 1, key="sp_ev")
            for o in range(IT):
                C.w(v, cS, f"soft{o}")
                C.inc(v.tensor_scalar(s_dv, p_dt, s_bdt[:, o:o + 1], None, ALU.add),
                      cV, 1, key=f"dvread{o}")
                v.tensor_scalar(s_rel, s_dv, 0.0, None, ALU.max)
                C.inc(v.tensor_tensor(s_dv, s_rel, s_abs[:], ALU.add), cV, 1,
                      key=f"absread{o}")
                C.w(v, qA, f"dout{o - 1}" if o > 0 else "none")
                v.tensor_copy(s_dhi, s_dv)
                v.tensor_tensor(s_dlo, s_dv, s_dhi, ALU.subtract)
                v.tensor_tensor(s_dxt[:], s_dv, s_xc[o], ALU.mult)
                v.tensor_copy(s_xhi, s_dxt[:])
                C.inc(v.tensor_tensor(s_xlo, s_dxt[:], s_xhi, ALU.subtract), cV, 1,
                      key=f"dxdone{o}")
            M["cV_total"] = C.val(cV)

    return nc



def emit_scan(C):
    nc = bass.Bass("TRN2", target_bir_lowering=False)
    M = C.M
    dh_in = nc.dram_tensor("dh", [256, B * L], BF16, kind="ExternalInput")
    dl_in = nc.dram_tensor("dl", [256, B * L], BF16, kind="ExternalInput")
    xh_in = nc.dram_tensor("xh", [256, B * L], BF16, kind="ExternalInput")
    xl_in = nc.dram_tensor("xl", [256, B * L], BF16, kind="ExternalInput")
    bssm_in = nc.dram_tensor("bssm", [NST, B * L], F32, kind="ExternalInput")
    alog_in = nc.dram_tensor("alog", [256, NST], F32, kind="ExternalInput")
    oh8_in = nc.dram_tensor("oh8", [8, 128], BF16, kind="ExternalInput")
    ind4_in = nc.dram_tensor("ind4", [4, 128, 32], BF16, kind="ExternalInput")

    o_y = nc.dram_tensor("o_y", [256, B * L], BF16, kind="ExternalOutput")

    NSUB = 128

    es = ExitStack()
    sb = lambda n, s, d=F32: es.enter_context(nc.sbuf_tensor(n, s, d))
    ps = lambda n, s, d=F32: es.enter_context(nc.psum_tensor(n, s, d))

    s_stg = sb("s_stg", [8, 4 * L], BF16)
    s_bb = sb("s_bb", [128, L])
    s_a = sb("s_a", [128, L])
    s_u = sb("s_u", [128, L])
    s_s = sb("s_s", [128, L])
    s_sb = sb("s_sb", [128, L], BF16)
    s_y = sb("s_y", [32, L])
    s_yb = sb("s_yb", [32, L], BF16)
    s_A = sb("s_A", [128, 32])
    s_oh8 = sb("s_oh8", [8, 128], BF16)
    s_ind = [sb(f"s_ind{j}", [128, 32], BF16) for j in range(4)]

    p_pool = ps("p_pool", [128, 4096])
    p_d = p_pool[:, 0:2048]
    p_x = p_pool[:, 2048:4096]
    p_y32 = p_pool[0:32, :]

    qS = es.enter_context(nc.semaphore("qS"))
    qG = es.enter_context(nc.semaphore("qG"))
    cT = es.enter_context(nc.semaphore("cT"))
    cV = es.enter_context(nc.semaphore("cV"))
    cS = es.enter_context(nc.semaphore("cS"))

    with es, nc.Block() as block:
        @block.sync
        def _sync_(sync):
            C.sec = "sync"
            C.inc(sync.dma_start(out=s_oh8[:], in_=oh8_in[:]), qS, 16)
            for j in range(4):
                C.inc(sync.dma_start(out=s_ind[j][:], in_=ind4_in[j]), qS, 16)
            aflat = alog_in[:].rearrange("d n -> (d n)")
            for _k in range(32):
                C.inc(sync.dma_start(out=s_A[:, _k:_k + 1],
                                     in_=aflat[128 * _k:128 * (_k + 1)].unsqueeze(1)), qS, 16,
                      key="ld_c" if _k == 31 else None)
            for ks in range(NSUB):
                s, m = ks // 32, ks % 32
                if ks > 0:
                    C.w(sync, cT, f"bcx1_{ks - 1}")
                C.inc(sync.dma_start(out=s_stg[:, 0:L], in_=dh_in[8 * m:8 * m + 8, L * s:L * (s + 1)]), qS, 16)
                C.inc(sync.dma_start(out=s_stg[:, L:2 * L], in_=dl_in[8 * m:8 * m + 8, L * s:L * (s + 1)]), qS, 16)
                C.inc(sync.dma_start(out=s_stg[:, 2 * L:3 * L], in_=xh_in[8 * m:8 * m + 8, L * s:L * (s + 1)]), qS, 16)
                C.inc(sync.dma_start(out=s_stg[:, 3 * L:4 * L], in_=xl_in[8 * m:8 * m + 8, L * s:L * (s + 1)]), qS, 16,
                      key=f"stg{ks}")
            C.wv(sync, qS, C.val(qS))
            C.w(sync, qG, "qG_total")
            C.w(sync, cV, "cV_total")

        @block.gpsimd
        def _gpsimd_(g):
            C.sec = "gpsimd"
            for gg in range(32):
                if gg % 8 == 0:
                    s = gg // 8
                    if s > 0:
                        C.w(g, cV, f"sb{32 * s - 1}")
                    rep = bssm_in[:, L * s:L * (s + 1)].unsqueeze(0).to_broadcast([8, NST, L])
                    C.inc(g.dma_start(out=s_bb[:], in_=rep), qG, 16, key=f"bb{s}")
                C.w(g, cV, f"yb{gg}")
                s2, grp = gg // 8, gg % 8
                C.inc(g.dma_start(out=o_y[32 * grp:32 * (grp + 1), L * s2:L * (s2 + 1)],
                                  in_=s_yb[:]), qG, 16, key=f"yout{gg}")
            M["qG_total"] = C.val(qG)

        @block.scalar
        def _scalar_(s_):
            C.sec = "scalar"
            C.w(s_, qS, "ld_c")
            C.inc(s_.activation(s_A[:], in_=s_A[:], func=AF.Exp), cS, 1, key="aexp")
            for ks in range(NSUB):
                for r in range(2):
                    C.w(s_, cT, f"bcd{r}_{ks}")
                    C.w(s_, cV, "aneg")
                    C.inc(s_.activation(s_a[:, 2048 * r:2048 * (r + 1)], in_=p_d,
                                        func=AF.Exp, scale=s_A[:, ks % 32:ks % 32 + 1]),
                          cS, 1, key=f"aev{r}_{ks}")
            M["cS_total"] = C.val(cS)

        @block.tensor
        def _tensor_(t):
            C.sec = "tensor"
            C.w(t, qS, "ld_c")
            for ks in range(NSUB):
                s = ks // 32
                j = ks % 4
                C.w(t, qS, f"stg{ks}")
                if ks > 0:
                    C.w(t, cV, f"yacc{ks - 1}")
                for r in range(2):
                    if r == 1:
                        C.w(t, cS, f"aev0_{ks}")
                        C.w(t, cV, f"uev0_{ks}")
                    c_lo = 2048 * r
                    for cc in range(4):
                        c0 = c_lo + 512 * cc
                        t.matmul(p_d[:, 512 * cc:512 * (cc + 1)], s_oh8[:],
                                 s_stg[:, c0:c0 + 512], start=True, stop=False)
                        mmd = t.matmul(p_d[:, 512 * cc:512 * (cc + 1)], s_oh8[:],
                                       s_stg[:, L + c0:L + c0 + 512], start=False, stop=True)
                    C.inc(mmd, cT, 1, key=f"bcd{r}_{ks}")
                    for cc in range(4):
                        c0 = c_lo + 512 * cc
                        t.matmul(p_x[:, 512 * cc:512 * (cc + 1)], s_oh8[:],
                                 s_stg[:, 2 * L + c0:2 * L + c0 + 512], start=True, stop=False)
                        mmx = t.matmul(p_x[:, 512 * cc:512 * (cc + 1)], s_oh8[:],
                                       s_stg[:, 3 * L + c0:3 * L + c0 + 512], start=False, stop=True)
                    C.inc(mmx, cT, 1, key=f"bcx{r}_{ks}")
                C.w(t, cV, f"sb{ks}")
                for cc in range(8):
                    mr = t.matmul(p_pool[0:32, 512 * cc:512 * (cc + 1)], s_ind[j][:],
                                  s_sb[:, 512 * cc:512 * (cc + 1)], start=True, stop=True)
                C.inc(mr, cT, 1, key=f"red{ks}")

        @block.vector
        def _vector_(v):
            C.sec = "vector"
            C.w(v, cS, "aexp")
            C.inc(v.tensor_scalar(s_A[:], s_A[:], -1.0, None, ALU.mult), cV, 1, key="aneg")
            for ks in range(NSUB):
                s = ks // 32
                g = (ks % 32) // 4
                j = ks % 4
                if ks == 0 or (ks % 32 == 0):
                    C.w(v, qG, f"bb{s}")
                for r in range(2):
                    C.w(v, cT, f"bcx{r}_{ks}")
                    C.inc(v.tensor_tensor(s_u[:, 2048 * r:2048 * (r + 1)], p_x,
                                          s_bb[:, 2048 * r:2048 * (r + 1)], ALU.mult),
                          cV, 1, key=f"uev{r}_{ks}")
                C.w(v, cS, f"aev1_{ks}")
                v.tensor_tensor_scan(s_s[:], s_a[:], s_u[:], 0.0, ALU.mult, ALU.add)
                C.inc(v.tensor_tensor(s_sb[:], s_s[:], s_bb[:], ALU.mult), cV, 1, key=f"sb{ks}")
                C.w(v, cT, f"red{ks}")
                if j == 0:
                    if ks >= 4:
                        gg_prev = (ks - 4) // 4
                        C.w(v, qG, f"yout{gg_prev}")
                    C.inc(v.tensor_copy(s_y[:], p_y32), cV, 1, key=f"yacc{ks}")
                else:
                    C.inc(v.tensor_tensor(s_y[:], s_y[:], p_y32, ALU.add), cV, 1, key=f"yacc{ks}")
                if j == 3:
                    gg = ks // 4
                    C.inc(v.tensor_copy(s_yb[:], s_y[:]), cV, 1, key=f"yb{gg}")
            M["cV_total"] = C.val(cV)

    return nc


def build_scan():
    return two_pass(emit_scan)



def emit_l1b(C):
    nc = bass.Bass("TRN2", target_bir_lowering=False)
    M = C.M
    y_in = nc.dram_tensor("y", [DI, TC], BF16, kind="ExternalInput")
    xc_in = nc.dram_tensor("xc", [DI, TC], BF16, kind="ExternalInput")
    z_in = nc.dram_tensor("z", [DI, TC], BF16, kind="ExternalInput")
    dp_in = nc.dram_tensor("dp", [DI], F32, kind="ExternalInput")
    wout_in = nc.dram_tensor("w_out", [DI, D], F32, kind="ExternalInput")
    x2_in = nc.dram_tensor("x2", [D, TC], F32, kind="ExternalInput")

    o_x3 = nc.dram_tensor("o_x3", [D, TC], F32, kind="ExternalOutput")

    IT = DI // 128
    OT = D // 128
    CHO = t_chunks(TC)

    es = ExitStack()
    sb = lambda n, s, d=F32: es.enter_context(nc.sbuf_tensor(n, s, d))
    ps = lambda n, s, d=F32: es.enter_context(nc.psum_tensor(n, s, d))

    s_y = [sb(f"s_y{j}", [128, TC], BF16) for j in range(IT)]
    s_w = [sb(f"s_w{j}", [128, D], BF16) for j in range(IT)]
    s_yt = sb("s_yt", [128, TC], BF16)
    s_xct = sb("s_xct", [128, TC], BF16)
    s_zt = sb("s_zt", [128, TC], BF16)
    s_slz = sb("s_slz", [128, TC], BF16)
    s_ydp = sb("s_ydp", [128, TC])
    s_dp = sb("s_dp", [128, IT])
    s_xt = sb("s_xt", [128, TC])
    s_o = sb("s_o", [128, TC])

    p_mm = ps("p_mm", [128, 2048])

    qS = es.enter_context(nc.semaphore("qS"))
    qG = es.enter_context(nc.semaphore("qG"))
    cT = es.enter_context(nc.semaphore("cT"))
    cV = es.enter_context(nc.semaphore("cV"))
    cS = es.enter_context(nc.semaphore("cS"))

    with es, nc.Block() as block:
        @block.sync
        def _sync_(sync):
            C.sec = "sync"
            for _t in range(IT):
                C.inc(sync.dma_start(out=s_dp[:, _t:_t + 1],
                                     in_=dp_in[128 * _t:128 * (_t + 1)].unsqueeze(1)), qS, 16,
                      key="ld_c" if _t == IT - 1 else None)
            for j in range(IT):
                C.w(sync, cV, f"y3_{j - 1}" if j > 0 else "none")
                C.inc(sync.dma_start(out=s_yt[:], in_=y_in[128 * j:128 * (j + 1), :]), qS, 16)
                C.inc(sync.dma_start(out=s_xct[:], in_=xc_in[128 * j:128 * (j + 1), :]), qS, 16)
                C.inc(sync.dma_start(out=s_zt[:], in_=z_in[128 * j:128 * (j + 1), :]), qS, 16,
                      key=f"tld{j}")
            for i in range(OT):
                C.w(sync, cV, f"x3_{i - 1}" if i > 0 else "none")
                C.inc(sync.dma_start(out=s_xt[:], in_=x2_in[128 * i:128 * (i + 1), :]), qS, 16,
                      key=f"x2ld{i}")
                C.w(sync, cV, f"x3_{i}")
                C.inc(sync.dma_start(out=o_x3[128 * i:128 * (i + 1), :], in_=s_o[:]), qS, 16,
                      key=f"x3out{i}")
            C.wv(sync, qS, C.val(qS))
            C.w(sync, qG, "qG_total")
            C.w(sync, cS, "cS_total")

        @block.gpsimd
        def _gpsimd_(g):
            C.sec = "gpsimd"
            for j in range(IT):
                C.inc(g.dma_start(out=s_w[j][:], in_=wout_in[128 * j:128 * (j + 1), :]), qG, 16,
                      key=f"wld{j}")
            M["qG_total"] = C.val(qG)

        @block.scalar
        def _scalar_(s):
            C.sec = "scalar"
            for j in range(IT):
                C.w(s, qS, f"tld{j}")
                if j >= 1:
                    C.w(s, cV, f"y3_{j - 1}")
                C.inc(s.activation(s_slz[:], in_=s_zt[:], func=AF.Silu), cS, 1, key=f"silz{j}")
            M["cS_total"] = C.val(cS)

        @block.tensor
        def _tensor_(t):
            C.sec = "tensor"
            for j in range(IT):
                C.w(t, qG, f"wld{j}")
            for i in range(OT):
                C.w(t, cV, f"y3_{IT - 1}")
                if i > 0:
                    C.w(t, cV, f"oev{i - 1}")
                for (c0, w) in CHO:
                    for j in range(IT):
                        mm = t.matmul(p_mm[:, c0:c0 + w][:, 0:w] if False else p_mm[:, c0:c0 + w],
                                      s_w[j][:, 128 * i:128 * (i + 1)], s_y[j][:, c0:c0 + w],
                                      start=(j == 0), stop=(j == IT - 1))
                C.inc(mm, cT, 1, key=f"att{i}")

        @block.vector
        def _vector_(v):
            C.sec = "vector"
            for j in range(IT):
                C.w(v, qS, f"tld{j}")
                C.w(v, cS, f"silz{j}")
                v.scalar_tensor_tensor(s_ydp[:], s_xct[:], s_dp[:, j:j + 1], s_yt[:],
                                       ALU.mult, ALU.add)
                C.inc(v.tensor_tensor(s_y[j][:], s_ydp[:], s_slz[:], ALU.mult), cV, 1,
                      key=f"y3_{j}")
            for i in range(OT):
                C.w(v, cT, f"att{i}")
                C.w(v, qS, f"x2ld{i}")
                if i > 0:
                    C.w(v, qS, f"x3out{i - 1}")
                C.inc(v.tensor_tensor(s_o[:], s_xt[:], p_mm[:], ALU.add), cV, 1, key=f"oev{i}")
                M[f"x3_{i}"] = M[f"oev{i}"]

    return nc


def build_l1b():
    return two_pass(emit_l1b)



def emit_quant_free(C, v, src_f32, out_bf16, W, scr, key, sign_key):
    NB = W // 16
    bar = lambda i: C.bar(i, v, C.sems_cV)
    xn, c, c2, c3, sgn, sc, sc2 = (scr[k] for k in ("xn", "c", "c2", "c3", "sgn", "sc", "sc2"))
    C.w(v, C.sems_cS, sign_key)
    bq = lambda i2: C.bar(i2, v, C.sems_cV)
    bq(v.tensor_reduce(sc.rearrange("p (b o) -> p b o", o=1),
                       xn.rearrange("p (b k) -> p b k", k=16), AX.X, ALU.max))
    bq(v.tensor_scalar(sc, sc, 1.0 / 6.0, None, ALU.mult))
    bq(v.tensor_scalar(sc, sc, 1e-12, None, ALU.max))
    bq(v.tensor_scalar(sc2, sc, 0.5, None, ALU.mult))
    bq(v.reciprocal(sc, sc))
    v.tensor_tensor(xn, xn, sc.rearrange("p (b o) -> p b o", o=1).to_broadcast([128, NB, 16]),
                    ALU.mult)
    v.tensor_scalar(c, xn, 0.25, None, ALU.is_gt)
    for thr in (0.75, 1.25, 1.75):
        v.scalar_tensor_tensor(c, xn, thr, c, ALU.is_gt, ALU.add)
    v.tensor_scalar(c2, xn, 2.5, None, ALU.is_gt)
    v.scalar_tensor_tensor(c2, xn, 3.5, c2, ALU.is_gt, ALU.add)
    v.tensor_scalar(c3, xn, 5.0, None, ALU.is_gt)
    v.scalar_tensor_tensor(c, c2, 2.0, c, ALU.mult, ALU.add)
    v.scalar_tensor_tensor(c, c3, 4.0, c, ALU.mult, ALU.add)
    v.tensor_tensor(c, c, sgn, ALU.mult)
    return C.inc(v.tensor_tensor(out_bf16, c,
                                 sc2.rearrange("p (b o) -> p b o", o=1).to_broadcast([128, NB, 16]),
                                 ALU.mult), C.sems_cV, 1, key=key)



def emit_quantw(C):
    nc = bass.Bass("TRN2", target_bir_lowering=False)
    M = C.M
    wg_in = nc.dram_tensor("wg_sl", [384, 1024], F32, kind="ExternalInput")
    wu_in = nc.dram_tensor("wu_sl", [384, 1024], F32, kind="ExternalInput")
    wd_in = nc.dram_tensor("wd_sl", [128, 2816], F32, kind="ExternalInput")
    eye = nc.dram_tensor("eye128", [128, 128], BF16, kind="ExternalInput")

    o_wgt = nc.dram_tensor("o_wgt", [1024, 384], BF16, kind="ExternalOutput")
    o_wut = nc.dram_tensor("o_wut", [1024, 384], BF16, kind="ExternalOutput")
    o_wdt = nc.dram_tensor("o_wdt", [2816, 128], BF16, kind="ExternalOutput")

    es = ExitStack()
    sb = lambda n, s, d=F32: es.enter_context(nc.sbuf_tensor(n, s, d))
    ps = lambda n, s, d=F32: es.enter_context(nc.psum_tensor(n, s, d))

    W = 2816
    s_x = sb("s_x", [128, W])
    s_q = sb("s_q", [128, W], BF16)
    scr = dict(xn=sb("xn", [128, W]), c=sb("c", [128, W], BF16), c2=sb("c2", [128, W], BF16),
               c3=sb("c3", [128, W], BF16), sgn=sb("sgn", [128, W], BF16),
               sc=sb("sc", [128, W // 16]), sc2=sb("sc2", [128, W // 16]))
    s_tr = sb("s_tr", [128, 128], BF16)
    seye = sb("seye", [128, 128], BF16)
    p_tr = ps("p_tr", [128, 128], BF16)

    qS = es.enter_context(nc.semaphore("qS"))
    cT = es.enter_context(nc.semaphore("cT"))
    cV = es.enter_context(nc.semaphore("cV"))
    cS = es.enter_context(nc.semaphore("cS"))
    C.sems_cV = cV
    C.sems_cS = cS

    jobs = []
    for r in range(3):
        jobs.append((f"g{r}", wg_in, r, 1024, o_wgt))
        jobs.append((f"u{r}", wu_in, r, 1024, o_wut))
    jobs.append(("d0", wd_in, 0, 2816, o_wdt))

    with es, nc.Block() as block:
        @block.sync
        def _sync_(sync):
            C.sec = "sync"
            C.inc(sync.dma_start(out=seye[:], in_=eye[:]), qS, 16, key="ld_eye")
            for name, din, r, w, dout in jobs:
                C.w(sync, cV, f"free_{name}")
                C.inc(sync.dma_start(out=s_x[:, 0:w], in_=din[128 * r:128 * (r + 1), :]), qS, 16,
                      key=f"ld_{name}")
                for k in range(w // 128):
                    C.w(sync, cV, f"trev_{name}_{k}")
                    C.inc(sync.dma_start(out=dout[128 * k:128 * (k + 1), 128 * r:128 * (r + 1)],
                                         in_=s_tr[:]), qS, 16, key=f"trout_{name}_{k}")
            C.wv(sync, qS, C.val(qS))
            C.w(sync, cS, "cS_total")

        @block.scalar
        def _scalar_(s_):
            C.sec = "scalar"
            prevq = None
            for name, din, r, w, dout in jobs:
                C.w(s_, qS, f"ld_{name}")
                if prevq is not None:
                    C.w(s_, cV, f"qdone_{prevq}")
                prevq = name
                s_.activation(scr["xn"][:, 0:w], in_=s_x[:, 0:w], func=AF.Abs)
                C.inc(s_.activation(scr["sgn"][:, 0:w], in_=s_x[:, 0:w], func=AF.Sign),
                      cS, 1, key=f"sgn_{name}")
            M["cS_total"] = C.val(cS)

        @block.vector
        def _vector_(v):
            C.sec = "vector"
            first = True
            for name, din, r, w, dout in jobs:
                M[f"free_{name}"] = 0 if first else M[f"qdone_{prev}"]
                first = False
                sub = dict(xn=scr["xn"][:, 0:w], c=scr["c"][:, 0:w], c2=scr["c2"][:, 0:w],
                           c3=scr["c3"][:, 0:w], sgn=scr["sgn"][:, 0:w],
                           sc=scr["sc"][:, 0:w // 16], sc2=scr["sc2"][:, 0:w // 16])
                emit_quant_free(C, v, s_x[:, 0:w], s_q[:, 0:w], w, sub,
                                f"qdone_{name}", f"sgn_{name}")
                for k in range(w // 128):
                    C.w(v, cT, f"tr_{name}_{k}")
                    C.inc(v.tensor_copy(s_tr[:], p_tr[:]), cV, 1, key=f"trev_{name}_{k}")
                    C.w(v, qS, f"trout_{name}_{k}")
                prev = name
            M["cV_total"] = C.val(cV)

        @block.tensor
        def _tensor_(t):
            C.sec = "tensor"
            C.w(t, qS, "ld_eye")
            for name, din, r, w, dout in jobs:
                C.w(t, cV, f"qdone_{name}")
                for k in range(w // 128):
                    if not (name == "g0" and k == 0):
                        C.w(t, cV, f"trev_{prev2[0]}_{prev2[1]}")
                    C.inc(t.transpose(p_tr[:], s_q[:, 128 * k:128 * (k + 1)], seye[:]),
                          cT, 1, key=f"tr_{name}_{k}")
                    prev2 = (name, k)

    return nc


def build_quantw():
    return two_pass(emit_quantw)



def emit_fp4(C):
    nc = bass.Bass("TRN2", target_bir_lowering=False)
    M = C.M
    x3_in = nc.dram_tensor("x3", [D, TC], F32, kind="ExternalInput")
    wgt_in = nc.dram_tensor("wgt", [D, 2816], BF16, kind="ExternalInput")
    wut_in = nc.dram_tensor("wut", [D, 2816], BF16, kind="ExternalInput")
    wdt_in = nc.dram_tensor("wdt", [2816, D], BF16, kind="ExternalInput")
    onesc = nc.dram_tensor("onesc", [128, 1], F32, kind="ExternalInput")
    onesr = nc.dram_tensor("onesr", [1, 128], F32, kind="ExternalInput")

    o_x4 = nc.dram_tensor("o_x4", [D, TC], F32, kind="ExternalOutput")

    DT = D // 128
    ITF = 22
    CHO = t_chunks(TC)

    es = ExitStack()
    sb = lambda n, s, d=F32: es.enter_context(nc.sbuf_tensor(n, s, d))
    ps = lambda n, s, d=F32: es.enter_context(nc.psum_tensor(n, s, d))

    sx = [sb(f"sx{i}", [128, TC]) for i in range(DT)]
    ssq = sb("ssq", [128, 512])
    sonesc = sb("sonesc", [128, 1])
    sonesr = sb("sonesr", [1, 128])
    s_mu = sb("s_mu", [1, TC])
    s_var = sb("s_var", [1, TC])
    s_mr = sb("s_mr", [1, TC])
    s_xq = [sb(f"s_xq{i}", [128, TC], BF16) for i in range(DT)]
    s_sil = sb("s_sil", [128, TC], BF16)
    s_up = sb("s_up", [128, TC], BF16)
    s_h = sb("s_h", [128, TC])
    s_x3t = sb("s_x3t", [128, TC])
    s_xT = s_x3t[:]
    s_qT = sb("s_qT", [128, TC], BF16)
    scr = dict(xn=sb("xn", [128, TC])[:], c=sb("c", [128, TC], BF16)[:],
               c2=s_sil[:], c3=s_up[:],
               sgn=sb("sgn", [128, TC], BF16)[:],
               sc=sb("sc", [128, TC // 16])[:], sc2=sb("sc2", [128, TC // 16])[:])
    s_hq = []
    for i in range(DT):
        s_hq.append(sx[i][:, 0:1024].bitcast(BF16))
        s_hq.append(sx[i][:, 1024:2048].bitcast(BF16))
    s_hq_extra = [sb(f"s_hqx{k}", [128, TC], BF16) for k in range(ITF - 16)]
    s_hq = s_hq + [t[:] for t in s_hq_extra]
    s_wg = [sb(f"s_wg{k}", [128, D], BF16) for k in range(2)]
    s_wu = [sb(f"s_wu{k}", [128, D], BF16) for k in range(2)]
    s_wd = [sb(f"s_wd{k}", [128, 2816], BF16) for k in range(1)]

    p_pool = ps("p_pool", [128, 4096])
    p_bc1 = p_pool[:, 0:512]
    p_bc2 = p_pool[:, 512:1024]
    p_g = p_pool[:, 0:2048]
    p_u = p_pool[:, 2048:4096]

    qS = es.enter_context(nc.semaphore("qS"))
    qG = es.enter_context(nc.semaphore("qG"))
    cT = es.enter_context(nc.semaphore("cT"))
    cV = es.enter_context(nc.semaphore("cV"))
    cS = es.enter_context(nc.semaphore("cS"))
    C.sems_cV = cV
    C.sems_cS = cS

    with es, nc.Block() as block:
        @block.sync
        def _sync_(sync):
            C.sec = "sync"
            for i in range(DT):
                C.inc(sync.dma_start(out=sx[i][:], in_=x3_in[128 * i:128 * (i + 1), :]), qS, 16)
            C.inc(sync.dma_start(out=sonesc[:], in_=onesc[:]), qS, 16)
            C.inc(sync.dma_start(out=sonesr[:], in_=onesr[:]), qS, 16, key="ld_x")
            for i in range(DT):
                C.w(sync, cV, f"x4_{i - 1}" if i > 0 else f"hq{ITF - 1}")
                C.inc(sync.dma_start(out=s_x3t[:], in_=x3_in[128 * i:128 * (i + 1), :]), qS, 16,
                      key=f"x3ld{i}")
                C.w(sync, cV, f"x4_{i}")
                C.inc(sync.dma_start(out=o_x4[128 * i:128 * (i + 1), :], in_=s_h[:]), qS, 16,
                      key=f"x4out{i}")
            C.wv(sync, qS, C.val(qS))
            C.w(sync, qG, "qG_total")
            C.w(sync, cS, "cS_total")

        @block.gpsimd
        def _gpsimd_(g):
            C.sec = "gpsimd"
            for it in range(ITF):
                if it >= 2:
                    C.w(g, cT, f"gup_done{it - 2}")
                C.inc(g.dma_start(out=s_wg[it % 2][:],
                                  in_=wgt_in[:, 128 * it:128 * (it + 1)].rearrange(
                                      "(k p) e -> p k e", p=128)), qG, 16, key=f"wg{it}")
                C.inc(g.dma_start(out=s_wu[it % 2][:],
                                  in_=wut_in[:, 128 * it:128 * (it + 1)].rearrange(
                                      "(k p) e -> p k e", p=128)), qG, 16, key=f"wu{it}")
            for o in range(DT):
                if o >= 1:
                    C.w(g, cT, f"dn_done{o - 1}")
                C.inc(g.dma_start(out=s_wd[0][:],
                                  in_=wdt_in[:, 128 * o:128 * (o + 1)].rearrange(
                                      "(k p) e -> p k e", p=128)), qG, 16, key=f"wd{o}")
            M["qG_total"] = C.val(qG)

        @block.scalar
        def _scalar_(s_):
            C.sec = "scalar"
            C.w(s_, qS, "ld_x")
            for ci, (c0, w) in enumerate(CHO):
                for i in range(DT):
                    g2 = ci * DT + i
                    if g2 > 0:
                        C.w(s_, cT, f"st_mm_g{g2 - 1}")
                    C.inc(s_.activation(ssq[:, 0:w], in_=sx[i][:, c0:c0 + w], func=AF.Square),
                          cS, 1, key=f"sq_g{g2}")
            C.w(s_, cV, "var_row")
            s_.activation(s_var[:], in_=s_var[:], func=AF.Ln)
            C.inc(s_.activation(s_var[:], in_=s_var[:], func=AF.Exp, scale=-0.5), cS, 1, key="rstd")
            for i in range(DT):
                C.w(s_, cV, f"xT{i}")
                s_.activation(scr["xn"], in_=s_xT, func=AF.Abs)
                C.inc(s_.activation(scr["sgn"], in_=s_xT, func=AF.Sign), cS, 1,
                      key=f"sgn_x{i}")
            for it in range(ITF):
                C.w(s_, cT, f"g_mm{it}")
                if it > 0:
                    C.w(s_, cV, f"hq{it - 1}")
                C.inc(s_.activation(s_sil[:], in_=p_g, func=AF.Silu), cS, 1, key=f"sil{it}")
                C.w(s_, cT, f"u_mm{it}")
                C.inc(s_.activation(s_up[:], in_=p_u, func=AF.Copy), cS, 1, key=f"up{it}")
                C.w(s_, cV, f"hT{it}")
                s_.activation(scr["xn"], in_=s_xT, func=AF.Abs)
                C.inc(s_.activation(scr["sgn"], in_=s_xT, func=AF.Sign), cS, 1,
                      key=f"sgn_h{it}")
            M["cS_total"] = C.val(cS)

        @block.tensor
        def _tensor_(t):
            C.sec = "tensor"
            C.w(t, qS, "ld_x")
            for ci, (c0, w) in enumerate(CHO):
                if ci > 0:
                    C.w(t, cV, f"st_ev{ci - 1}")
                for i in range(DT):
                    g2 = ci * DT + i
                    C.w(t, cS, f"sq_g{g2}")
                    t.matmul(p_bc1[0:1, 0:w], sonesc[:], sx[i][:, c0:c0 + w],
                             start=(i == 0), stop=(i == DT - 1))
                    C.inc(t.matmul(p_bc2[0:1, 0:w], sonesc[:], ssq[:, 0:w],
                                   start=(i == 0), stop=(i == DT - 1)), cT, 1, key=f"st_mm_g{g2}")
            C.w(t, cS, "rstd")
            C.w(t, cV, "mr_row")
            for ci, (c0, w) in enumerate(CHO):
                if ci > 0:
                    C.w(t, cV, f"ln_ev{ci - 1}")
                t.matmul(p_bc1[:, 0:w], sonesr[:], s_var[:, c0:c0 + w], start=True, stop=True)
                C.inc(t.matmul(p_bc2[:, 0:w], sonesr[:], s_mr[:, c0:c0 + w], start=True, stop=True),
                      cT, 1, key=f"bc{ci}")
            for it in range(ITF):
                C.w(t, qG, f"wg{it}")
                for i in range(DT):
                    C.w(t, cV, f"xq{i}")
                if it > 0:
                    C.w(t, cS, f"sil{it - 1}")
                for (c0, w) in CHO:
                    for k in range(DT):
                        mg = t.matmul(p_g[:, c0:c0 + w], s_wg[it % 2][:, 128 * k:128 * (k + 1)],
                                      s_xq[k][:, c0:c0 + w], start=(k == 0), stop=(k == DT - 1))
                C.inc(mg, cT, 1, key=f"g_mm{it}")
                C.w(t, cS, f"up{it - 1}" if it > 0 else "none")
                for (c0, w) in CHO:
                    for k in range(DT):
                        mu_ = t.matmul(p_u[:, c0:c0 + w], s_wu[it % 2][:, 128 * k:128 * (k + 1)],
                                       s_xq[k][:, c0:c0 + w], start=(k == 0), stop=(k == DT - 1))
                C.inc(mu_, cT, 1, key=f"u_mm{it}")
                M[f"gup_done{it}"] = C.val(cT)
            for o in range(DT):
                C.w(t, qG, f"wd{o}")
                for it in range(ITF):
                    C.w(t, cV, f"hq{it}")
                if o == 0:
                    C.w(t, cS, f"sil{ITF - 1}")
                    C.w(t, cS, f"up{ITF - 1}")
                    C.w(t, cV, f"hmul{ITF - 1}")
                if o > 0:
                    C.w(t, cV, f"x4_{o - 1}")
                for (c0, w) in CHO:
                    for k in range(ITF):
                        md = t.matmul(p_g[:, c0:c0 + w], s_wd[0][:, 128 * k:128 * (k + 1)],
                                      s_hq[k][:, c0:c0 + w], start=(k == 0), stop=(k == ITF - 1))
                C.inc(md, cT, 1, key=f"dn_mm{o}")
                M[f"dn_done{o}"] = C.val(cT)

        @block.vector
        def _vector_(v):
            C.sec = "vector"
            for ci, (c0, w) in enumerate(CHO):
                C.w(v, cT, f"st_mm_g{ci * DT + DT - 1}")
                v.tensor_copy(s_mu[:, c0:c0 + w], p_bc1[0:1, 0:w])
                C.inc(v.tensor_copy(s_var[:, c0:c0 + w], p_bc2[0:1, 0:w]), cV, 1, key=f"st_ev{ci}")
            C.bar(v.tensor_scalar(s_mu[:, 0:1], s_mu[:, 0:1], 1.0, None, ALU.mult), v, cV)
            v.wait_ge(cV, C.val(cV))
            v.tensor_scalar(s_mu[:], s_mu[:], 1.0 / D, None, ALU.mult)
            v.tensor_scalar(s_var[:], s_var[:], 1.0 / D, None, ALU.mult)
            v.tensor_tensor(s_mr[:], s_mu[:], s_mu[:], ALU.mult)
            v.tensor_tensor(s_var[:], s_var[:], s_mr[:], ALU.subtract)
            C.inc(v.tensor_scalar(s_var[:], s_var[:], float(LN_EPS), None, ALU.add), cV, 1,
                  key="var_row")
            C.w(v, cS, "rstd")
            C.inc(v.tensor_tensor(s_mr[:], s_mu[:], s_var[:], ALU.mult), cV, 1, key="mr_row")
            for ci, (c0, w) in enumerate(CHO):
                C.w(v, cT, f"bc{ci}")
                for i in range(DT):
                    v.scalar_tensor_tensor(sx[i][:, c0:c0 + w], sx[i][:, c0:c0 + w], 1.0,
                                           p_bc1[:, 0:w], ALU.mult, ALU.mult)
                    last = v.tensor_tensor(sx[i][:, c0:c0 + w], sx[i][:, c0:c0 + w],
                                           p_bc2[:, 0:w], ALU.subtract)
                C.inc(last, cV, 1, key=f"ln_ev{ci}")
            for i in range(DT):
                if i == 0:
                    C.w(v, cV, f"ln_ev{len(CHO) - 1}")
                C.inc(v.transpose(s_xT, sx[i][:]), cV, 1, key=f"xT{i}")
                emit_quant_free(C, v, s_xT, s_qT[:], TC, scr, f"xq_pre{i}", f"sgn_x{i}")
                C.inc(v.transpose(s_xq[i][:], s_qT[:]), cV, 1, key=f"xq{i}")
            for it in range(ITF):
                C.w(v, cS, f"sil{it}")
                C.w(v, cS, f"up{it}")
                C.inc(v.tensor_tensor(s_h[:], s_sil[:], s_up[:], ALU.mult), cV, 1,
                      key=f"hmul{it}")
                C.inc(v.transpose(s_xT, s_h[:]), cV, 1, key=f"hT{it}")
                emit_quant_free(C, v, s_xT, s_qT[:], TC, scr, f"hq_pre{it}", f"sgn_h{it}")
                C.inc(v.transpose(s_hq[it], s_qT[:]), cV, 1, key=f"hq{it}")
            for o in range(DT):
                C.w(v, cT, f"dn_mm{o}")
                C.w(v, qS, f"x3ld{o}")
                if o > 0:
                    C.w(v, qS, f"x4out{o - 1}")
                C.inc(v.tensor_tensor(s_h[:], s_x3t[:], p_g[:, 0:2048], ALU.add), cV, 1,
                      key=f"x4_{o}")

    return nc


def build_fp4():
    return two_pass(emit_fp4)



class Pool:

    def __init__(self, h, words):
        self.h = h
        self.words = words
        self.off = 0

    def reset(self):
        self.off = 0

    def f32(self, p, cols):
        w = (cols + 15) // 16 * 16
        assert self.off + w <= self.words, f"pool overflow {self.off}+{w}"
        v = self.h[0:p, self.off:self.off + cols]
        self.off += w
        return v

    def bf16(self, p, cols):
        w = ((cols + 1) // 2 + 15) // 16 * 16
        assert self.off + w <= self.words, f"pool overflow {self.off}+{w}"
        v = self.h[0:p, self.off:self.off + w].bitcast(BF16)[:, 0:cols]
        self.off += w
        return v

    def i32(self, p, cols):
        w = (cols + 15) // 16 * 16
        assert self.off + w <= self.words, f"pool overflow {self.off}+{w}"
        v = self.h[0:p, self.off:self.off + w].bitcast(I32)[:, 0:cols]
        self.off += w
        return v


def emit_mega(C):
    nc = bass.Bass("TRN2", target_bir_lowering=False)
    M = C.M
    EI, EO = "ExternalInput", "ExternalOutput"
    dt_ = nc.dram_tensor
    xwin_in = dt_("xwin", [D, TP], F32, kind=EI)
    x_in = dt_("xplain", [D, TC], F32, kind=EI)
    w0in_in = dt_("w0_in", [2 * DI, D], F32, kind=EI)
    c0w_in = dt_("c0_w", [DI, 1, 4], F32, kind=EI)
    c0b_in = dt_("c0_b", [DI], F32, kind=EI)
    w0x_in = dt_("w0_x", [2 * NST, DI], F32, kind=EI)
    blog_in = dt_("b_log", [DI, NST], F32, kind=EI)
    w0o_in = dt_("w0_out", [D, DI], F32, kind=EI)
    lng_in = dt_("ln_g", [D], F32, kind=EI)
    lnb_in = dt_("ln_b", [D], F32, kind=EI)
    wr_in = dt_("w_router", [D, 8], F32, kind=EI)
    w1_in = dt_("w1", [8, D, 4096], F32, kind=EI)
    w2_in = dt_("w2", [8, 4096, D], F32, kind=EI)
    bias_in = dt_("bias", [D], F32, kind=EI)
    w1in_in = dt_("w1_in", [D, 2 * DI], F32, kind=EI)
    c1w_in = dt_("c1_w", [DI, 1, 4], F32, kind=EI)
    c1b_in = dt_("c1_b", [DI], F32, kind=EI)
    wdt_in = dt_("w_dt", [DI, DI], F32, kind=EI)
    bdt_in = dt_("b_dt", [DI], F32, kind=EI)
    wxp_in = dt_("w_xproj", [DI, 2 * NST], F32, kind=EI)
    alog_in = dt_("alog_sl", [256, NST], F32, kind=EI)
    sidx_in = dt_("sidx", [8, 256], I32, kind=EI)
    dp_in = dt_("dp", [DI], F32, kind=EI)
    w1o_in = dt_("w1_out", [DI, D], F32, kind=EI)
    wg_in = dt_("wg_sl", [384, 1024], F32, kind=EI)
    wu_in = dt_("wu_sl", [384, 1024], F32, kind=EI)
    wd_in = dt_("wd_sl", [128, 2816], F32, kind=EI)
    eye_in = dt_("eye128", [128, 128], BF16, kind=EI)
    eyef_in = dt_("eyef", [128, 128], F32, kind=EI)
    onesc_in = dt_("onesc", [128, 1], F32, kind=EI)
    onesr_in = dt_("onesr", [1, 128], F32, kind=EI)
    onescb_in = dt_("onescb", [128, 1], BF16, kind=EI)
    iota8_in = dt_("iota8", [128, 8], F32, kind=EI)
    tri_in = dt_("tri128", [128, 128], BF16, kind=EI)
    oh8_in = dt_("oh8", [8, 128], BF16, kind=EI)
    ind4_in = dt_("ind4", [4, 128, 32], BF16, kind=EI)
    sel_in = dt_("sel", [8, 1], F32, kind=EI)
    hm_in = dt_("hmask", [128, 1], F32, kind=EI)
    gidx_in = dt_("gidx", [128, 16], I32, kind=EI)
    cidx_in = dt_("cidx", [128, 16], I32, kind=EI)

    o_out = dt_("o_out", [D, TC], F32, kind=EO)

    d_xc = dt_("d_xc", [DI, TC], BF16)
    d_z = dt_("d_z", [DI, TC], BF16)
    d_sc = dt_("d_sc", [NST, TC], F32)
    d_xln = dt_("d_xln", [D, TC], F32)
    d_xcg = dt_("d_xcg", [NC * DI, TC], BF16, addr_space="Shared")
    d_xcl = dt_("d_xcl", [256, B * L], BF16)
    d_hsl = dt_("d_hsl", [256, B * L], BF16)
    d_hg = dt_("d_hg", [DI, B * L], BF16, addr_space="Shared")
    d_x1 = dt_("d_x1", [D, TC], F32)
    d_xln1 = dt_("d_xln1", [D, TC], BF16)
    d_meta = dt_("d_meta", [TC, 8], F32)
    d_cnt = dt_("d_cnt", [1, 8], F32)
    d_cntg = dt_("d_cntg", [8, 8], F32, addr_space="Shared")
    d_x2 = dt_("d_x2", [D, TC], F32)
    d_haloin = dt_("d_haloin", [D, 3], F32)
    d_halo = dt_("d_halo", [2 * D, 3], F32)
    d_xc2 = dt_("d_xc2", [DI, TC], BF16)
    d_z2 = dt_("d_z2", [DI, TC], BF16)
    d_bs = dt_("d_bs", [NST, TC], F32)
    d_bg = dt_("d_bg", [NC * NST, TC], F32, addr_space="Shared")
    d_dh = dt_("d_dh", [DI, TC], BF16)
    d_dl = dt_("d_dl", [DI, TC], BF16)
    d_xh = dt_("d_xh", [DI, TC], BF16)
    d_xl = dt_("d_xl", [DI, TC], BF16)
    d_dhg = dt_("d_dhg", [NC * DI, TC], BF16, addr_space="Shared")
    d_dlg = dt_("d_dlg", [NC * DI, TC], BF16, addr_space="Shared")
    d_xhg = dt_("d_xhg", [NC * DI, TC], BF16, addr_space="Shared")
    d_xlg = dt_("d_xlg", [NC * DI, TC], BF16, addr_space="Shared")
    d_dhl = dt_("d_dhl", [256, B * L], BF16)
    d_dll = dt_("d_dll", [256, B * L], BF16)
    d_xhl = dt_("d_xhl", [256, B * L], BF16)
    d_xll = dt_("d_xll", [256, B * L], BF16)
    d_ysl = dt_("d_ysl", [256, B * L], BF16)
    d_yg = dt_("d_yg", [DI, B * L], BF16, addr_space="Shared")
    d_x3 = dt_("d_x3", [D, TC], F32)
    d_wgt = dt_("d_wgt", [1024, 384], BF16)
    d_wut = dt_("d_wut", [1024, 384], BF16)
    d_wdt = dt_("d_wdt", [2816, 128], BF16)
    d_wgtg = dt_("d_wgtg", [NC * 1024, 384], BF16, addr_space="Shared")
    d_wutg = dt_("d_wutg", [NC * 1024, 384], BF16, addr_space="Shared")
    d_wdtg = dt_("d_wdtg", [NC * 2816, 128], BF16, addr_space="Shared")

    es = ExitStack()
    POOLW = 52992
    pool = Pool(es.enter_context(nc.sbuf_tensor("POOL", [128, POOLW], F32)), POOLW)
    p_pool = es.enter_context(nc.psum_tensor("PPOOL", [128, 4096], F32))

    sem = lambda n: es.enter_context(nc.semaphore(n))
    qS = sem("qS"); qG = sem("qG"); qA = sem("qA")
    cT = sem("cT"); cV = sem("cV"); cS = sem("cS")
    ccs = [sem(f"cc{i}") for i in range(14)]
    C.sems_cV = cV
    C.sems_cS = cS
    SEMS = dict(qS=qS, qG=qG, qA=qA, cT=cT, cV=cV, cS=cS)

    CH = t_chunks(TP)
    CHO = t_chunks(TC)
    DT = D // 128
    IT = DI // 128
    ET = 2 * DI // 128

    def ln_scalar(s, sx, ssq, s_var, chs, nt):
        for ci, (c0, w) in enumerate(chs):
            for i in range(nt):
                g2 = ci * nt + i
                if g2 > 0:
                    C.w(s, cT, f"st_mm_g{g2 - 1}")
                C.inc(s.activation(ssq[:, 0:w], in_=sx[i][:, c0:c0 + w], func=AF.Square),
                      cS, 1, key=f"sq_g{g2}")
        C.w(s, cV, "var_row")
        s.activation(s_var, in_=s_var, func=AF.Ln)
        C.inc(s.activation(s_var, in_=s_var, func=AF.Exp, scale=-0.5), cS, 1, key="rstd")

    def ln_tensor(t, sx, ssq, s_var, s_mr, sonesc, sonesr, p_bc1, p_bc2, chs, nt):
        for ci, (c0, w) in enumerate(chs):
            if ci > 0:
                C.w(t, cV, f"st_ev{ci - 1}")
            for i in range(nt):
                g2 = ci * nt + i
                C.w(t, cS, f"sq_g{g2}")
                t.matmul(p_bc1[0:1, 0:w], sonesc, sx[i][:, c0:c0 + w],
                         start=(i == 0), stop=(i == nt - 1))
                C.inc(t.matmul(p_bc2[0:1, 0:w], sonesc, ssq[:, 0:w],
                               start=(i == 0), stop=(i == nt - 1)), cT, 1, key=f"st_mm_g{g2}")
        C.w(t, cS, "rstd")
        C.w(t, cV, "mr_row")
        for ci, (c0, w) in enumerate(chs):
            if ci > 0:
                C.w(t, cV, f"ln_ev{ci - 1}")
            t.matmul(p_bc1[:, 0:w], sonesr, s_var[0:1, c0:c0 + w], start=True, stop=True)
            C.inc(t.matmul(p_bc2[:, 0:w], sonesr, s_mr[0:1, c0:c0 + w], start=True, stop=True),
                  cT, 1, key=f"bc{ci}")

    def ln_vector(v, sx, sxb, s_mu, s_var, s_mr, p_bc1, p_bc2, chs, nt):
        for ci, (c0, w) in enumerate(chs):
            C.w(v, cT, f"st_mm_g{ci * nt + nt - 1}")
            v.tensor_copy(s_mu[0:1, c0:c0 + w], p_bc1[0:1, 0:w])
            C.inc(v.tensor_copy(s_var[0:1, c0:c0 + w], p_bc2[0:1, 0:w]), cV, 1, key=f"st_ev{ci}")
        C.w(v, cV, f"st_ev{len(chs) - 1}")
        v.tensor_scalar(s_mu, s_mu, 1.0 / D, None, ALU.mult)
        v.tensor_scalar(s_var, s_var, 1.0 / D, None, ALU.mult)
        v.tensor_tensor(s_mr, s_mu, s_mu, ALU.mult)
        v.tensor_tensor(s_var, s_var, s_mr, ALU.subtract)
        C.inc(v.tensor_scalar(s_var, s_var, float(LN_EPS), None, ALU.add), cV, 1, key="var_row")
        C.w(v, cS, "rstd")
        C.inc(v.tensor_tensor(s_mr, s_mu, s_var, ALU.mult), cV, 1, key="mr_row")
        for ci, (c0, w) in enumerate(chs):
            C.w(v, cT, f"bc{ci}")
            for i in range(nt):
                v.scalar_tensor_tensor(sx[i][:, c0:c0 + w], sx[i][:, c0:c0 + w], 1.0,
                                       p_bc1[:, 0:w], ALU.mult, ALU.mult)
                last = v.tensor_tensor(sx[i][:, c0:c0 + w], sx[i][:, c0:c0 + w],
                                       p_bc2[:, 0:w], ALU.subtract)
            C.inc(last, cV, 1, key=f"ln_ev{ci}")
        if sxb is not None:
            for i in range(nt):
                C.inc(v.tensor_copy(sxb[i][:], sx[i][:]), cV, 1, key=f"bf16{i}")

    pool.reset()
    z0 = {}
    z0["sx"] = [pool.f32(128, TP) for _ in range(DT)]
    z0["sxb"] = [pool.bf16(128, TP) for _ in range(DT)]
    z0["ssq"] = pool.f32(128, 512)
    z0["seye"] = pool.bf16(128, 128)
    z0["sonesc"] = pool.f32(128, 1)
    z0["sonesr"] = pool.f32(1, 128)
    z0["s_mu"] = pool.f32(1, TP)
    z0["s_var"] = pool.f32(1, TP)
    z0["s_mr"] = pool.f32(1, TP)
    z0["s_wrow"] = [pool.bf16(128, D) for _ in range(2)]
    z0["s_lhs"] = [pool.bf16(128, 128) for _ in range(DT)]
    z0["s_xz"] = pool.f32(128, TP)
    z0["s_conv"] = pool.f32(128, TC)
    z0["s_cw"] = pool.f32(128, 64)
    z0["s_cb"] = pool.f32(128, 16)
    z0["s_xcb"] = [pool.bf16(128, TC) for _ in range(2)]
    z0["s_zb"] = [pool.bf16(128, TC) for _ in range(2)]
    z0["s_wx"] = pool.bf16(32, DI)
    z0["s_wxT"] = [pool.bf16(128, 32) for _ in range(IT)]
    z0["s_sc32"] = pool.f32(32, TC)
    b0_bc1 = p_pool[:, 0:512]
    b0_bc2 = p_pool[:, 512:1024]
    b0_tr = p_pool[:, 1024:1088].bitcast(BF16)[:, 0:128]
    b0_mm = p_pool[:, 1536:2048]
    b0_sp = p_pool[0:32, 2048:4096]

    def p0_sync(sync):
        z = z0
        for i in range(DT):
            C.inc(sync.dma_start(out=z["sx"][i], in_=xwin_in[128 * i:128 * (i + 1), :]), qS, 16)
        C.inc(sync.dma_start(out=z["seye"], in_=eye_in[:]), qS, 16)
        C.inc(sync.dma_start(out=z["sonesc"], in_=onesc_in[:]), qS, 16)
        C.inc(sync.dma_start(out=z["sonesr"], in_=onesr_in[:]), qS, 16, key="ld_x")
        for i in range(DT):
            C.w(sync, cV, f"bf16{i}")
            C.inc(sync.dma_start(out=d_xln[128 * i:128 * (i + 1), :], in_=z["sx"][i][:, PAD:]),
                  qS, 16)
        C.w(sync, cV, "sc_ev")
        C.inc(sync.dma_start(out=d_sc[:], in_=z["s_sc32"][16:32, :]), qS, 16)

    def p0_gpsimd(g):
        z = z0
        C.inc(g.dma_start(out=z["s_cw"], in_=c0w_in[:].rearrange("(t p) o k -> p t o k", p=128)),
              qG, 16)
        for _t in range(16):
            C.inc(g.dma_start(out=z["s_cb"][:, _t:_t + 1],
                              in_=c0b_in[128 * _t:128 * (_t + 1)].unsqueeze(1)), qG, 16)
        C.inc(g.dma_start(out=z["s_wx"], in_=w0x_in[:]), qG, 16, key="ld_misc")
        for j in range(ET):
            if j >= 2:
                C.w(g, cT, f"wrow_free{j - 2}")
            C.inc(g.dma_start(out=z["s_wrow"][j % 2], in_=w0in_in[128 * j:128 * (j + 1), :]),
                  qG, 16, key=f"wrow{j}")
            if j >= 2:
                jo = j - 2
                C.w(g, cS, f"act{jo}")
                if jo < IT:
                    C.inc(g.dma_start(out=d_xc[128 * jo:128 * (jo + 1), :], in_=z["s_xcb"][jo % 2]),
                          qG, 16, key=f"actout{jo}")
                else:
                    C.inc(g.dma_start(out=d_z[128 * (jo - IT):128 * (jo - IT + 1), :],
                                      in_=z["s_zb"][jo % 2]), qG, 16, key=f"actout{jo}")
        for jo in (ET - 2, ET - 1):
            C.w(g, cS, f"act{jo}")
            C.inc(g.dma_start(out=d_z[128 * (jo - IT):128 * (jo - IT + 1), :],
                              in_=z["s_zb"][jo % 2]), qG, 16, key=f"actout{jo}")

    def p0_scalar(s):
        z = z0
        C.w(s, qS, "ld_x")
        ln_scalar(s, z["sx"], z["ssq"], z["s_var"], CH, DT)
        for j in range(ET):
            inner = j < IT
            C.w(s, cV, f"conv{j}" if inner else f"xz_ev{j}")
            if j >= 2:
                C.w(s, qG, f"actout{j - 2}")
                if j - 2 < IT:
                    C.w(s, cT, f"sp_mm{j - 2}")
            if inner:
                C.inc(s.activation(z["s_xcb"][j % 2], in_=z["s_conv"],
                                   func=AF.Silu, bias=z["s_cb"][:, j:j + 1]), cS, 1, key=f"act{j}")
            else:
                C.inc(s.activation(z["s_zb"][j % 2], in_=z["s_xz"][:, PAD:], func=AF.Copy),
                      cS, 1, key=f"act{j}")

    def p0_tensor(t):
        z = z0
        C.w(t, qS, "ld_x")
        ln_tensor(t, z["sx"], z["ssq"], z["s_var"], z["s_mr"], z["sonesc"], z["sonesr"],
                  b0_bc1, b0_bc2, CH, DT)
        C.w(t, qG, "ld_misc")
        for i in range(IT):
            if i > 0:
                C.w(t, cV, f"wxT_ev{i - 1}")
            C.inc(t.transpose(b0_tr[:, 0:32], z["s_wx"][:, 128 * i:128 * (i + 1)],
                              z["seye"][0:32, 0:32]), cT, 1, key=f"wxT{i}")
        for i in range(DT):
            C.w(t, cV, f"bf16{i}")
        for j in range(ET):
            C.w(t, qG, f"wrow{j}")
            for i in range(DT):
                tg = j * DT + i
                if tg == 0:
                    C.w(t, cV, f"wxT_ev{IT - 1}")
                else:
                    C.w(t, cV, f"lhs_ev_g{tg - 1}")
                C.inc(t.transpose(b0_tr, z["s_wrow"][j % 2][:, 128 * i:128 * (i + 1)],
                                  z["seye"]), cT, 1, key=f"lhsT_g{tg}")
            C.setk(f"wrow_free{j}", cT)
            C.w(t, cV, f"lhs_ev_g{j * DT + DT - 1}")
            for ci, (c0, w) in enumerate(CH):
                pg = j * len(CH) + ci
                if pg > 0:
                    C.w(t, cV, f"xz_psfree_g{pg - 1}")
                for i in range(DT):
                    mmx = t.matmul(b0_mm[:, 0:w], z["s_lhs"][i], z["sxb"][i][:, c0:c0 + w],
                                   start=(i == 0), stop=(i == DT - 1))
                C.inc(mmx, cT, 1, key=f"xz_mm_g{pg}")
            if j < IT:
                C.w(t, cS, f"act{j}")
                C.w(t, cV, f"wxT_ev{j}")
                for (c0, w) in CHO:
                    mms = t.matmul(b0_sp[:, c0:c0 + w], z["s_wxT"][j], z["s_xcb"][j % 2][:, c0:c0 + w],
                                   start=(j == 0), stop=(j == IT - 1))
                C.inc(mms, cT, 1, key=f"sp_mm{j}")

    def p0_vector(v):
        z = z0
        ln_vector(v, z["sx"], z["sxb"], z["s_mu"], z["s_var"], z["s_mr"], b0_bc1, b0_bc2, CH, DT)
        for i in range(IT):
            C.w(v, cT, f"wxT{i}")
            C.inc(v.tensor_copy(z["s_wxT"][i], b0_tr[:, 0:32]), cV, 1, key=f"wxT_ev{i}")
        for j in range(ET):
            for i in range(DT):
                tg = j * DT + i
                C.w(v, cT, f"lhsT_g{tg}")
                C.inc(v.tensor_copy(z["s_lhs"][i], b0_tr), cV, 1, key=f"lhs_ev_g{tg}")
            if j >= 1:
                C.w(v, cS, f"act{j - 1}")
            for ci, (c0, w) in enumerate(CH):
                pg = j * len(CH) + ci
                C.w(v, cT, f"xz_mm_g{pg}")
                C.inc(v.tensor_copy(z["s_xz"][:, c0:c0 + w], b0_mm[:, 0:w]), cV, 1,
                      key=f"xz_psfree_g{pg}")
            C.setk(f"xz_ev{j}", cV)
            if j < IT:
                for k in range(4):
                    srcv = z["s_xz"][:, PAD - 3 + k: PAD - 3 + k + TC]
                    wcol = z["s_cw"][:, 4 * j + k: 4 * j + k + 1]
                    if k == 0:
                        v.tensor_scalar(z["s_conv"], srcv, wcol, None, ALU.mult)
                    else:
                        last = v.scalar_tensor_tensor(z["s_conv"], srcv, wcol, z["s_conv"],
                                                      ALU.mult, ALU.add)
                C.inc(last, cV, 1, key=f"conv{j}")
        C.w(v, cT, f"sp_mm{IT - 1}")
        C.inc(v.tensor_copy(z["s_sc32"], b0_sp), cV, 1, key="sc_ev")

    PH0 = dict(sync=p0_sync, gpsimd=p0_gpsimd, scalar=p0_scalar, tensor=p0_tensor, vector=p0_vector)

    pool.reset()
    z1 = {}
    z1["gidx"] = pool.i32(128, 16)
    z1["s_in"] = [pool.bf16(128, L) for _ in range(2)]
    z1["s_out"] = [pool.bf16(128, L) for _ in range(2)]
    z1["s_dec"] = pool.f32(128, L)

    def p1_sync(sync):
        C.inc(sync.dma_start(out=z1["gidx"], in_=gidx_in[:]), qS, 16, key="gld")

    def p1_gpsimd(g):
        C.inc(g.collective_compute("AllGather", ALU.bypass, replica_groups=[list(range(NC))],
                             ins=[d_xc[:]], outs=[d_xcg[:]]), ccs[0], 1)
        g.wait_ge(ccs[0], 1)
        C.w(g, qS, "gld")
        xcg_flat = d_xcg[:]
        for k in range(8):
            dt2, s2 = k // 4, k % 4
            if k >= 2:
                C.w(g, cV, f"scan{k - 2}")
            for h in range(2):
                gcol = (2 * s2 + h) * 2 + dt2
                C.inc(g.indirect_dma_start(
                    out=z1["s_in"][k % 2][:, TC * h:TC * (h + 1)],
                    out_offset=None, in_=xcg_flat,
                    in_offset=bass.IndirectOffsetOnAxis(z1["gidx"][:, gcol:gcol + 1], 0)),
                    qG, 16, key=f"ld{k}" if h == 1 else None)
            if k >= 2:
                ko = k - 2
                C.w(g, cV, f"scan{ko}")
                dt3, s3 = ko // 4, ko % 4
                C.inc(g.dma_start(out=d_hsl[128 * dt3:128 * (dt3 + 1), L * s3:L * (s3 + 1)],
                                  in_=z1["s_out"][ko % 2]), qG, 16, key=f"st{ko}")
        for ko in (6, 7):
            C.w(g, cV, f"scan{ko}")
            dt3, s3 = ko // 4, ko % 4
            C.inc(g.dma_start(out=d_hsl[128 * dt3:128 * (dt3 + 1), L * s3:L * (s3 + 1)],
                              in_=z1["s_out"][ko % 2]), qG, 16, key=f"st{ko}")
        C.w(g, qG, "st7")
        C.inc(g.collective_compute("AllGather", ALU.bypass, replica_groups=[list(range(NC))],
                             ins=[d_hsl[:]], outs=[d_hg[:]]), ccs[1], 1)
        g.wait_ge(ccs[1], 1)

    def p1_scalar(s):
        pass

    def p1_tensor(t):
        pass

    def p1_vector(v):
        C.inc(v.memset(z1["s_dec"], DECAY), cV, 1, key="dec")
        for k in range(8):
            C.w(v, qG, f"ld{k}")
            if k >= 2:
                C.w(v, qG, f"st{k - 2}")
            C.inc(v.tensor_tensor_scan(z1["s_out"][k % 2], z1["s_dec"], z1["s_in"][k % 2],
                                       0.0, ALU.mult, ALU.add), cV, 1, key=f"scan{k}")

    PH1 = dict(sync=p1_sync, gpsimd=p1_gpsimd, scalar=p1_scalar, tensor=p1_tensor, vector=p1_vector)

    pool.reset()
    z2 = {}
    z2["cidx"] = pool.i32(128, 16)
    z2["seye"] = pool.bf16(128, 128)
    z2["sonesc"] = pool.f32(128, 1)
    z2["sonesr"] = pool.f32(1, 128)
    z2["s_scb"] = pool.bf16(NST, TC)
    z2["s_bt"] = pool.f32(128, NST)
    z2["s_btb"] = pool.bf16(128, NST)
    z2["s_BT"] = [pool.bf16(NST, 128) for _ in range(IT)]
    z2["s_h"] = pool.bf16(128, TC)
    z2["s_z"] = pool.bf16(128, TC)
    z2["s_slz"] = pool.bf16(128, TC)
    z2["s_y"] = [pool.bf16(128, TC) for _ in range(IT)]
    z2["s_wrow"] = [pool.bf16(128, DI) for _ in range(2)]
    z2["s_lhsW"] = [pool.bf16(128, 128) for _ in range(IT)]
    z2["s_res"] = [pool.f32(128, TC) for _ in range(DT)]
    z2["s_sq"] = pool.f32(128, 512)
    z2["s_mu"] = pool.f32(1, TC)
    z2["s_var"] = pool.f32(1, TC)
    z2["s_mr"] = pool.f32(1, TC)
    z2["s_g"] = pool.f32(128, DT)
    z2["s_bb"] = pool.f32(128, DT)
    z2["s_xlnt"] = pool.f32(128, TC)
    z2["s_xt"] = pool.f32(128, TC)
    b2_bc1 = p_pool[:, 0:512]
    b2_bc2 = p_pool[:, 512:1024]
    b2_tr = p_pool[:, 1024:1088].bitcast(BF16)[:, 0:128]
    b2_big = p_pool[:, 2048:4096]

    def p2_sync(sync):
        z = z2
        C.inc(sync.dma_start(out=z["cidx"], in_=cidx_in[:]), qS, 16)
        C.inc(sync.dma_start(out=z["seye"], in_=eye_in[:]), qS, 16)
        C.inc(sync.dma_start(out=z["sonesc"], in_=onesc_in[:]), qS, 16)
        C.inc(sync.dma_start(out=z["sonesr"], in_=onesr_in[:]), qS, 16)
        for _t in range(DT):
            C.inc(sync.dma_start(out=z["s_g"][:, _t:_t + 1],
                                 in_=lng_in[128 * _t:128 * (_t + 1)].unsqueeze(1)), qS, 16)
            C.inc(sync.dma_start(out=z["s_bb"][:, _t:_t + 1],
                                 in_=lnb_in[128 * _t:128 * (_t + 1)].unsqueeze(1)), qS, 16,
                  key="ld_c" if _t == DT - 1 else None)
        for i in range(DT):
            C.w(sync, cV, f"x1_{i}")
            C.inc(sync.dma_start(out=d_x1[128 * i:128 * (i + 1), :], in_=z["s_res"][i]), qS, 16,
                  key=f"x1out{i}")

    def p2_gpsimd(g):
        z = z2
        C.inc(g.dma_start(out=z["s_scb"], in_=d_sc[:]), qG, 16, key="ld_sc")
        for j in range(IT):
            C.inc(g.dma_start(out=z["s_bt"], in_=blog_in[128 * j:128 * (j + 1), :]), qG, 16,
                  key=f"blog{j}")
            C.w(g, cS, f"bexp{j}")
        hg_flat = d_hg[:].rearrange("r (a c) -> (r a) c", c=TC)
        for j in range(IT):
            if j >= 1:
                C.w(g, cV, f"yd2_{j - 1}")
            C.w(g, qS, "ld_c")
            C.inc(g.indirect_dma_start(out=z["s_h"], out_offset=None, in_=hg_flat,
                                       in_offset=bass.IndirectOffsetOnAxis(
                                           z["cidx"][:, j:j + 1], 0)), qG, 16, key=f"hld{j}")
            C.inc(g.dma_start(out=z["s_z"], in_=d_z[128 * j:128 * (j + 1), :]), qG, 16,
                  key=f"zld{j}")
        for i in range(DT):
            if i >= 2:
                C.w(g, cT, f"wo_free{i - 2}")
            C.inc(g.dma_start(out=z["s_wrow"][i % 2], in_=w0o_in[128 * i:128 * (i + 1), :]),
                  qG, 16, key=f"worow{i}")
            C.w(g, cV, f"att_ev{i - 1}" if i >= 1 else "none")
            C.inc(g.dma_start(out=z["s_xlnt"], in_=d_xln[128 * i:128 * (i + 1), :]), qG, 16,
                  key=f"xlnld{i}")
        for i in range(DT):
            C.w(g, cV, f"x1_{i - 1}" if i >= 1 else "none")
            C.inc(g.dma_start(out=z["s_xt"], in_=x_in[128 * i:128 * (i + 1), :]), qG, 16,
                  key=f"xld{i}")

    def p2_scalar(s):
        z = z2
        for j in range(IT):
            C.w(s, qG, f"blog{j}")
            if j > 0:
                C.w(s, cT, f"bT{j - 1}")
            C.inc(s.activation(z["s_btb"], in_=z["s_bt"], func=AF.Exp), cS, 1, key=f"bexp{j}")
        for j in range(IT):
            C.w(s, qG, f"zld{j}")
            if j >= 1:
                C.w(s, cV, f"yd2_{j - 1}")
            C.inc(s.activation(z["s_slz"], in_=z["s_z"], func=AF.Silu), cS, 1, key=f"silz{j}")
        for ci, (c0, w) in enumerate(CHO):
            for i in range(DT):
                g2 = ci * DT + i
                C.w(s, cV, f"att_ev{i}")
                if g2 > 0:
                    C.w(s, cT, f"rst_mm_g{g2 - 1}")
                C.inc(s.activation(z["s_sq"][:, 0:w], in_=z["s_res"][i][:, c0:c0 + w],
                                   func=AF.Square), cS, 1, key=f"rsq_g{g2}")
        C.w(s, cV, "var_row2")
        s.activation(z["s_var"], in_=z["s_var"], func=AF.Ln)
        C.inc(s.activation(z["s_var"], in_=z["s_var"], func=AF.Exp, scale=-0.5), cS, 1,
              key="rstd2")

    def p2_tensor(t):
        z = z2
        C.w(t, qS, "ld_c")
        for j in range(IT):
            C.w(t, cS, f"bexp{j}")
            if j > 0:
                C.w(t, cV, f"bT_ev{j - 1}")
            C.inc(t.transpose(b2_tr[0:NST, :], z["s_btb"], z["seye"]), cT, 1, key=f"bT{j}")
        C.w(t, qG, "ld_sc")
        for j in range(IT):
            C.w(t, cV, f"bT_ev{IT - 1}" if j == 0 else f"yd2_{j - 1}")
            for (c0, w) in CHO:
                mc = t.matmul(b2_big[:, c0:c0 + w], z["s_BT"][j], z["s_scb"][:, c0:c0 + w],
                              start=True, stop=True)
            C.inc(mc, cT, 1, key=f"cproj{j}")
        for i in range(DT):
            C.w(t, qG, f"worow{i}")
            for j in range(IT):
                tg = i * IT + j
                C.w(t, cV, f"bT_ev{IT - 1}" if tg == 0 else f"wo_ev_g{tg - 1}")
                C.inc(t.transpose(b2_tr, z["s_wrow"][i % 2][:, 128 * j:128 * (j + 1)],
                                  z["seye"]), cT, 1, key=f"woT_g{tg}")
            C.setk(f"wo_free{i}", cT)
            C.w(t, cV, f"wo_ev_g{i * IT + IT - 1}")
            C.w(t, cV, f"yd2_{IT - 1}")
            if i > 0:
                C.w(t, cV, f"att_ev{i - 1}")
            for (c0, w) in CHO:
                for j in range(IT):
                    ma = t.matmul(b2_big[:, c0:c0 + w], z["s_lhsW"][j], z["s_y"][j][:, c0:c0 + w],
                                  start=(j == 0), stop=(j == IT - 1))
            C.inc(ma, cT, 1, key=f"att{i}")
        for ci, (c0, w) in enumerate(CHO):
            if ci > 0:
                C.w(t, cV, f"st_ev{ci - 1}")
            for i in range(DT):
                g2 = ci * DT + i
                C.w(t, cS, f"rsq_g{g2}")
                t.matmul(b2_bc1[0:1, 0:w], z["sonesc"], z["s_res"][i][:, c0:c0 + w],
                         start=(i == 0), stop=(i == DT - 1))
                C.inc(t.matmul(b2_bc2[0:1, 0:w], z["sonesc"], z["s_sq"][:, 0:w],
                               start=(i == 0), stop=(i == DT - 1)), cT, 1, key=f"rst_mm_g{g2}")
        C.w(t, cS, "rstd2")
        C.w(t, cV, "mr_row2")
        for ci, (c0, w) in enumerate(CHO):
            if ci > 0:
                C.w(t, cV, f"ln2_ev{ci - 1}")
            t.matmul(b2_bc1[:, 0:w], z["sonesr"], z["s_var"][0:1, c0:c0 + w], start=True, stop=True)
            C.inc(t.matmul(b2_bc2[:, 0:w], z["sonesr"], z["s_mr"][0:1, c0:c0 + w],
                           start=True, stop=True), cT, 1, key=f"bc2_{ci}")

    def p2_vector(v):
        z = z2
        for j in range(IT):
            C.w(v, cT, f"bT{j}")
            C.inc(v.tensor_copy(z["s_BT"][j], b2_tr[0:NST, :]), cV, 1, key=f"bT_ev{j}")
        for j in range(IT):
            C.w(v, cT, f"cproj{j}")
            C.w(v, qG, f"hld{j}")
            C.w(v, cS, f"silz{j}")
            v.tensor_tensor(z["s_y"][j], z["s_h"], b2_big, ALU.mult)
            C.inc(v.tensor_tensor(z["s_y"][j], z["s_y"][j], z["s_slz"], ALU.mult), cV, 1,
                  key=f"yd2_{j}")
        for i in range(DT):
            for j in range(IT):
                tg = i * IT + j
                C.w(v, cT, f"woT_g{tg}")
                C.inc(v.tensor_copy(z["s_lhsW"][j], b2_tr), cV, 1, key=f"wo_ev_g{tg}")
            C.w(v, cT, f"att{i}")
            C.w(v, qG, f"xlnld{i}")
            C.inc(v.tensor_tensor(z["s_res"][i], z["s_xlnt"], b2_big, ALU.add), cV, 1,
                  key=f"att_ev{i}")
        for ci, (c0, w) in enumerate(CHO):
            C.w(v, cT, f"rst_mm_g{ci * DT + DT - 1}")
            v.tensor_copy(z["s_mu"][0:1, c0:c0 + w], b2_bc1[0:1, 0:w])
            C.inc(v.tensor_copy(z["s_var"][0:1, c0:c0 + w], b2_bc2[0:1, 0:w]), cV, 1,
                  key=f"st_ev{ci}")
        C.w(v, cV, f"st_ev{len(CHO) - 1}")
        v.tensor_scalar(z["s_mu"], z["s_mu"], 1.0 / D, None, ALU.mult)
        v.tensor_scalar(z["s_var"], z["s_var"], 1.0 / D, None, ALU.mult)
        v.tensor_tensor(z["s_mr"], z["s_mu"], z["s_mu"], ALU.mult)
        v.tensor_tensor(z["s_var"], z["s_var"], z["s_mr"], ALU.subtract)
        C.inc(v.tensor_scalar(z["s_var"], z["s_var"], float(LN_EPS), None, ALU.add), cV, 1,
              key="var_row2")
        C.w(v, cS, "rstd2")
        C.inc(v.tensor_tensor(z["s_mr"], z["s_mu"], z["s_var"], ALU.mult), cV, 1, key="mr_row2")
        for ci, (c0, w) in enumerate(CHO):
            C.w(v, cT, f"bc2_{ci}")
            for i in range(DT):
                v.scalar_tensor_tensor(z["s_res"][i][:, c0:c0 + w], z["s_res"][i][:, c0:c0 + w],
                                       1.0, b2_bc1[:, 0:w], ALU.mult, ALU.mult)
                last = v.tensor_tensor(z["s_res"][i][:, c0:c0 + w], z["s_res"][i][:, c0:c0 + w],
                                       b2_bc2[:, 0:w], ALU.subtract)
            C.inc(last, cV, 1, key=f"ln2_ev{ci}")
        for i in range(DT):
            C.w(v, qG, f"xld{i}")
            if i > 0:
                C.w(v, qS, f"x1out{i - 1}")
            v.tensor_scalar(z["s_res"][i], z["s_res"][i], z["s_g"][:, i:i + 1],
                            z["s_bb"][:, i:i + 1], ALU.mult, op1=ALU.add)
            C.inc(v.tensor_tensor(z["s_res"][i], z["s_res"][i], z["s_xt"], ALU.add), cV, 1,
                  key=f"x1_{i}")

    PH2 = dict(sync=p2_sync, gpsimd=p2_gpsimd, scalar=p2_scalar, tensor=p2_tensor, vector=p2_vector)

    pool.reset()
    z3 = {}
    z3["sx"] = [pool.f32(128, TC) for _ in range(DT)]
    z3["sxb"] = [pool.bf16(128, TC) for _ in range(DT)]
    z3["ssq"] = pool.f32(128, 512)
    z3["sonesc"] = pool.f32(128, 1)
    z3["sonesr"] = pool.f32(1, 128)
    z3["sonescb"] = pool.bf16(128, 1)
    z3["stri"] = pool.bf16(128, 128)
    z3["sio"] = pool.f32(128, 8)
    z3["s_wr"] = [pool.bf16(128, 8) for _ in range(DT)]
    z3["s_mu"] = pool.f32(1, TC)
    z3["s_var"] = pool.f32(1, TC)
    z3["s_mr"] = pool.f32(1, TC)
    z3["s_prob"] = pool.f32(128, 8)
    z3["s_col"] = pool.f32(128, 1)
    z3["s_mx"] = pool.f32(128, 8)
    z3["s_oh1"] = pool.f32(128, 8)
    z3["s_oh2"] = pool.f32(128, 8)
    z3["s_ind"] = pool.bf16(128, 8)
    z3["s_pre"] = pool.f32(128, 8)
    z3["s_t8"] = pool.f32(128, 8)
    z3["s_offs"] = pool.f32(1, 8)
    z3["s_crow"] = pool.f32(1, 8)
    z3["s_meta"] = pool.f32(128, 8)
    b3_bc1 = p_pool[:, 0:512]
    b3_bc2 = p_pool[:, 512:1024]
    b3_sc = p_pool[:, 1024:1032]
    b3_pre = p_pool[:, 1536:1544]
    b3_off = p_pool[:, 2048:2056]
    b3_cr = p_pool[0:1, 2560:2568]
    TT3 = TC // 128

    def p3_sync(sync):
        z = z3
        for i in range(DT):
            C.inc(sync.dma_start(out=z["sx"][i], in_=d_x1[128 * i:128 * (i + 1), :]), qS, 16)
        C.inc(sync.dma_start(out=z["sonesc"], in_=onesc_in[:]), qS, 16)
        C.inc(sync.dma_start(out=z["sonesr"], in_=onesr_in[:]), qS, 16)
        C.inc(sync.dma_start(out=z["stri"], in_=tri_in[:]), qS, 16)
        C.inc(sync.dma_start(out=z["sonescb"], in_=onescb_in[:]), qS, 16)
        C.inc(sync.dma_start(out=z["sio"], in_=iota8_in[:]), qS, 16, key="ld_x")
        for i in range(DT):
            C.w(sync, cV, f"bf16{i}")
            C.inc(sync.dma_start(out=d_xln1[128 * i:128 * (i + 1), :], in_=z["sxb"][i]), qS, 16)
        for tt in range(TT3):
            C.w(sync, cV, f"meta{tt}")
            C.inc(sync.dma_start(out=d_meta[128 * tt:128 * (tt + 1), :], in_=z["s_meta"]), qS, 16,
                  key=f"metaout{tt}")
        C.w(sync, cV, "cnt_done")
        C.inc(sync.dma_start(out=d_cnt[:], in_=z["s_offs"]), qS, 16, key="cntout")

    def p3_gpsimd(g):
        z = z3
        for i in range(DT):
            C.inc(g.dma_start(out=z["s_wr"][i], in_=wr_in[128 * i:128 * (i + 1), :]), qG, 16,
                  key="wrld" if i == DT - 1 else None)
        C.w(g, qS, "cntout")
        C.inc(g.collective_compute("AllGather", ALU.bypass, replica_groups=[list(range(NC))],
                             ins=[d_cnt[:]], outs=[d_cntg[:]]), ccs[2], 1)
        g.wait_ge(ccs[2], 1)

    def p3_scalar(s):
        z = z3
        C.w(s, qS, "ld_x")
        ln_scalar(s, z["sx"], z["ssq"], z["s_var"], CHO, DT)
        for tt in range(TT3):
            C.w(s, cV, f"negmax{tt}")
            C.inc(s.activation(z["s_prob"], in_=b3_sc, func=AF.Exp, bias=z["s_col"]),
                  cS, 1, key=f"sexp{tt}")

    def p3_tensor(t):
        z = z3
        C.w(t, qS, "ld_x")
        ln_tensor(t, z["sx"], z["ssq"], z["s_var"], z["s_mr"], z["sonesc"], z["sonesr"],
                  b3_bc1, b3_bc2, CHO, DT)
        C.w(t, qG, "wrld")
        for tt in range(TT3):
            if tt > 0:
                C.w(t, cS, f"sexp{tt - 1}")
            for i in range(DT):
                C.w(t, cV, f"bf16{i}")
                mmx = t.matmul(b3_sc, z["sxb"][i][:, 128 * tt:128 * (tt + 1)], z["s_wr"][i],
                               start=(i == 0), stop=(i == DT - 1))
            C.inc(mmx, cT, 1, key=f"sc_mm{tt}")
            C.w(t, cV, f"ind{tt}")
            t.matmul(b3_pre, z["stri"], z["s_ind"], start=True, stop=True)
            C.inc(t.matmul(b3_cr, z["sonescb"], z["s_ind"], start=True, stop=True),
                  cT, 1, key=f"pre_mm{tt}")
            C.w(t, cV, f"offs{tt}")
            C.inc(t.matmul(b3_off, z["sonesr"], z["s_offs"], start=True, stop=True),
                  cT, 1, key=f"off_mm{tt}")

    def p3_vector(v):
        z = z3
        ln_vector(v, z["sx"], z["sxb"], z["s_mu"], z["s_var"], z["s_mr"], b3_bc1, b3_bc2, CHO, DT)
        C.inc(v.memset(z["s_offs"], 0.0), cV, 1, key="offs_init")
        for tt in range(TT3):
            bar = lambda i2: C.bar(i2, v, cV)
            C.w(v, cT, f"sc_mm{tt}")
            bar(v.tensor_reduce(z["s_col"], b3_sc, AX.X, ALU.max))
            C.inc(v.tensor_scalar(z["s_col"], z["s_col"], -1.0, None, ALU.mult), cV, 1,
                  key=f"negmax{tt}")
            C.w(v, cS, f"sexp{tt}")
            bar(v.tensor_reduce(z["s_col"], z["s_prob"], AX.X, ALU.add))
            bar(v.reciprocal(z["s_col"], z["s_col"]))
            bar(v.tensor_scalar(z["s_prob"], z["s_prob"], z["s_col"], None, ALU.mult))
            bar(v.max(out=z["s_mx"], in_=z["s_prob"]))
            bar(v.tensor_scalar(z["s_oh1"], z["s_prob"], z["s_mx"][:, 0:1], None, ALU.is_equal))
            bar(v.tensor_scalar(z["s_oh2"], z["s_prob"], z["s_mx"][:, 1:2], None, ALU.is_equal))
            C.inc(v.tensor_tensor(z["s_ind"], z["s_oh1"], z["s_oh2"], ALU.add), cV, 1,
                  key=f"ind{tt}")
            M[C.prefix + f"offs{tt}"] = M.get(C.prefix + ("offs_init" if tt == 0 else f"offsup{tt - 1}"), 0)
            C.w(v, cT, f"pre_mm{tt}")
            C.w(v, cT, f"off_mm{tt}")
            bar(v.tensor_copy(z["s_pre"], b3_pre))
            bar(v.tensor_tensor(z["s_pre"], z["s_pre"], b3_off, ALU.add))
            if tt > 0:
                C.w(v, qS, f"metaout{tt - 1}")
            bar(v.tensor_tensor(z["s_t8"], z["s_oh1"], z["sio"], ALU.mult))
            bar(v.tensor_reduce(z["s_meta"][:, 4:5].rearrange("p (a o) -> p a o", o=1),
                                z["s_t8"].rearrange("p (a e) -> p a e", e=8), AX.X, ALU.add))
            bar(v.tensor_tensor(z["s_t8"], z["s_oh2"], z["sio"], ALU.mult))
            bar(v.tensor_reduce(z["s_meta"][:, 5:6].rearrange("p (a o) -> p a o", o=1),
                                z["s_t8"].rearrange("p (a e) -> p a e", e=8), AX.X, ALU.add))
            bar(v.tensor_copy(z["s_meta"][:, 2:4], z["s_mx"][:, 0:2]))
            bar(v.memset(z["s_meta"][:, 6:8], 0.0))
            bar(v.tensor_tensor(z["s_t8"], z["s_oh1"], z["s_pre"], ALU.mult))
            bar(v.tensor_reduce(z["s_meta"][:, 0:1].rearrange("p (a o) -> p a o", o=1),
                                z["s_t8"].rearrange("p (a e) -> p a e", e=8), AX.X, ALU.add))
            bar(v.tensor_tensor(z["s_t8"], z["s_oh2"], z["s_pre"], ALU.mult))
            C.inc(v.tensor_reduce(z["s_meta"][:, 1:2].rearrange("p (a o) -> p a o", o=1),
                                  z["s_t8"].rearrange("p (a e) -> p a e", e=8), AX.X, ALU.add),
                  cV, 1, key=f"meta{tt}")
            bar(v.tensor_copy(z["s_crow"], b3_cr))
            C.inc(v.tensor_tensor(z["s_offs"], z["s_offs"], z["s_crow"], ALU.add), cV, 1,
                  key=f"offsup{tt}")
        M[C.prefix + "cnt_done"] = M[C.prefix + f"offsup{TT3 - 1}"]

    PH3 = dict(sync=p3_sync, gpsimd=p3_gpsimd, scalar=p3_scalar, tensor=p3_tensor, vector=p3_vector)

    pool.reset()
    z4 = {}
    z4["sxln"] = [pool.bf16(128, TC) for _ in range(DT)]
    z4["s_acc"] = [pool.bf16(128, TC) for _ in range(DT)]
    z4["s_h"] = [pool.bf16(128, 512) for _ in range(32)]
    z4["s_w1"] = [pool.bf16(128, 4096) for _ in range(DT)]
    z4["s_w2"] = [pool.bf16(128, D) for _ in range(2)]
    z4["s_meta"] = pool.f32(128, 8)
    z4["s_cnt"] = pool.f32(8, 8)
    z4["s_sel"] = pool.f32(8, 1)
    z4["s_gp"] = pool.f32(1, 8)
    z4["s_gpb"] = pool.f32(128, 8)
    z4["s_oh"] = pool.f32(128, 8)
    z4["s_tmp8"] = pool.f32(128, 8)
    z4["s_col"] = pool.f32(128, 1)
    z4["s_col2"] = pool.f32(128, 1)
    z4["s_ct"] = [pool.f32(128, 8) for _ in range(16)]
    z4["s_coefT"] = pool.f32(128, 8)
    z4["s_crow"] = pool.f32(1, 512)
    z4["s_cb"] = pool.f32(128, 512)
    z4["s_ytmp"] = pool.f32(128, 512)
    z4["s_bias"] = pool.f32(128, DT)
    z4["s_x1"] = pool.f32(128, TC)
    z4["s_x2"] = pool.f32(128, TC)
    z4["sio"] = pool.f32(128, 8)
    z4["seyef"] = pool.f32(128, 128)
    z4["sonesr"] = pool.f32(1, 128)
    z4["s_halo"] = pool.f32(128, 3)
    b4_h = [p_pool[:, 0:512], p_pool[:, 512:1024]]
    b4_y = [p_pool[:, 512 * o:512 * (o + 1)] for o in range(8)]
    b4_gp = p_pool[0:1, 0:8]
    b4_gpb = p_pool[:, 1024:1032]
    b4_row = p_pool[0:1, 1024:1536]
    b4_cbc = p_pool[:, 1536:2048]
    NCH4 = 4

    def prev_ec4(e, ch):
        return (e, ch - 1) if ch > 0 else (e - 1, NCH4 - 1)

    def p4_sync(sync):
        z = z4
        C.inc(sync.dma_start(out=z["s_cnt"], in_=d_cntg[:]), qS, 16)
        C.inc(sync.dma_start(out=z["s_sel"], in_=sel_in[:]), qS, 16)
        C.inc(sync.dma_start(out=z["sio"], in_=iota8_in[:]), qS, 16)
        C.inc(sync.dma_start(out=z["seyef"], in_=eyef_in[:]), qS, 16)
        C.inc(sync.dma_start(out=z["sonesr"], in_=onesr_in[:]), qS, 16)
        for _t in range(DT):
            C.inc(sync.dma_start(out=z["s_bias"][:, _t:_t + 1],
                                 in_=bias_in[128 * _t:128 * (_t + 1)].unsqueeze(1)), qS, 16,
                  key="ld_c" if _t == DT - 1 else None)
        for i in range(DT):
            C.inc(sync.dma_start(out=z["sxln"][i], in_=d_xln1[128 * i:128 * (i + 1), :]),
                  qS, 16, key=f"xlnld{i}")
        for tt in range(16):
            C.w(sync, cV, f"coefT{tt - 1}" if tt > 0 else "none")
            C.inc(sync.dma_start(out=z["s_meta"], in_=d_meta[128 * tt:128 * (tt + 1), :]),
                  qS, 16, key=f"meta{tt}")
        for i in range(DT):
            C.w(sync, cV, f"x2_{i - 1}" if i > 0 else "none")
            C.inc(sync.dma_start(out=z["s_x1"], in_=d_x1[128 * i:128 * (i + 1), :]), qS, 16,
                  key=f"x1ld{i}")
            C.w(sync, cV, f"x2_{i}")
            C.inc(sync.dma_start(out=d_x2[128 * i:128 * (i + 1), :], in_=z["s_x2"]), qS, 16,
                  key=f"x2out{i}")
        C.w(sync, qS, f"x2out{DT - 1}")
        C.inc(sync.dma_start(out=d_haloin[:], in_=d_x2[:, TC - 3:TC]), qS, 16, key="haloin")

    def p4_gpsimd(g):
        z = z4
        for e in range(8):
            for k in range(DT):
                if k == 0 and e > 0:
                    C.w(g, cT, f"w1free_e{e - 1}")
                C.inc(g.dma_start(out=z["s_w1"][k], in_=w1_in[e, 128 * k:128 * (k + 1), :]),
                      qG, 16, key=f"w1ld_e{e}_k{k}")
            for ch in range(NCH4):
                for k in range(32):
                    kg = (e * NCH4 + ch) * 32 + k
                    if kg >= 2:
                        C.w(g, cT, f"w2used_g{kg - 2}")
                    C.inc(g.dma_start(out=z["s_w2"][kg % 2],
                                      in_=w2_in[e, 128 * k:128 * (k + 1), :]),
                          qG, 16, key=f"w2ld_g{kg}")
        C.w(g, qS, "haloin")
        C.inc(g.collective_compute("AllGather", ALU.bypass,
                             replica_groups=[[0, 1], [2, 3], [4, 5], [6, 7]],
                             ins=[d_haloin[:]], outs=[d_halo[:]]), ccs[3], 1)
        g.wait_ge(ccs[3], 1)

    def p4_scalar(s):
        z = z4
        for e in range(8):
            for ch in range(NCH4):
                for j in range(32):
                    hg = (e * NCH4 + ch) * 32 + j
                    C.w(s, cT, f"h_mm_g{hg}")
                    C.inc(s.activation(z["s_h"][j], in_=b4_h[j % 2], func=AF.Gelu_apprx_tanh),
                          cS, 1, key=f"gelu_g{hg}")

    def p4_tensor(t):
        z = z4
        C.w(t, qS, "ld_c")
        C.inc(t.matmul(b4_gp, z["s_sel"], z["s_cnt"], start=True, stop=True), cT, 1, key="gp_mm")
        C.w(t, cV, "gp_ev")
        C.inc(t.matmul(b4_gpb, z["sonesr"], z["s_gp"], start=True, stop=True), cT, 1,
              key="gpb_mm")
        C.w(t, cV, "coefT15")
        for k in range(DT):
            C.w(t, qS, f"xlnld{k}")
        for e in range(8):
            for k in range(DT):
                C.w(t, qG, f"w1ld_e{e}_k{k}")
            for ch in range(NCH4):
                c0 = 512 * ch
                pe, pc = prev_ec4(e, ch)
                for j in range(32):
                    hg = (e * NCH4 + ch) * 32 + j
                    if hg >= 2:
                        C.w(t, cS, f"gelu_g{hg - 2}")
                    if j < 2:
                        if e == 0 and ch == 0:
                            C.w(t, cV, "acc_init")
                        else:
                            C.w(t, cV, f"accdone_e{pe}_c{pc}")
                    for k in range(DT):
                        mh = t.matmul(b4_h[j % 2], z["s_w1"][k][:, 128 * j:128 * (j + 1)],
                                      z["sxln"][k][:, c0:c0 + 512],
                                      start=(k == 0), stop=(k == DT - 1))
                    C.inc(mh, cT, 1, key=f"h_mm_g{hg}")
                if e > 0 or ch > 0:
                    C.w(t, cV, f"crow_ev_e{pe}_c{pc}")
                for q in range(4):
                    mx2 = t.matmul(b4_row[:, 128 * q:128 * (q + 1)],
                                   z["s_ct"][4 * ch + q][:, e:e + 1], z["seyef"],
                                   start=True, stop=True)
                C.inc(mx2, cT, 1, key=f"crow_mm_e{e}_c{ch}")
                C.w(t, cV, f"crow_ev2_e{e}_c{ch}")
                C.inc(t.matmul(b4_cbc, z["sonesr"], z["s_crow"], start=True, stop=True),
                      cT, 1, key=f"cb_mm_e{e}_c{ch}")
                C.w(t, cS, f"gelu_g{(e * NCH4 + ch) * 32 + 31}")
                C.w(t, cV, f"cb_ev_e{e}_c{ch}")
                for k in range(32):
                    kg = (e * NCH4 + ch) * 32 + k
                    C.w(t, qG, f"w2ld_g{kg}")
                    for o in range(DT):
                        my = t.matmul(b4_y[o], z["s_w2"][kg % 2][:, 128 * o:128 * (o + 1)],
                                      z["s_h"][k], start=(k == 0), stop=(k == 31))
                    C.inc(my, cT, 1, key=f"w2used_g{kg}")
                C.setk(f"y_mm_e{e}_c{ch}", cT)
            C.setk(f"w1free_e{e}", cT)

    def p4_vector(v):
        z = z4
        for i in range(DT):
            last = v.memset(z["s_acc"][i], 0.0)
        C.inc(last, cV, 1, key="acc_init")
        C.w(v, cT, "gp_mm")
        C.inc(v.tensor_copy(z["s_gp"], b4_gp), cV, 1, key="gp_ev")
        C.w(v, cT, "gpb_mm")
        C.inc(v.tensor_copy(z["s_gpb"], b4_gpb), cV, 1, key="gpb_ev")
        for tt in range(16):
            bar = lambda i2: C.bar(i2, v, cV)
            C.w(v, qS, f"meta{tt}")
            C.w(v, cV, "gpb_ev")
            bar(v.tensor_scalar(z["s_oh"], z["sio"], z["s_meta"][:, 4:5], None, ALU.is_equal))
            bar(v.tensor_tensor(z["s_tmp8"], z["s_oh"], z["s_gpb"], ALU.mult))
            bar(v.tensor_reduce(z["s_col"].rearrange("p (a o) -> p a o", o=1),
                                z["s_tmp8"].rearrange("p (a e) -> p a e", e=8), AX.X, ALU.add))
            bar(v.tensor_tensor(z["s_col"], z["s_col"], z["s_meta"][:, 0:1], ALU.add))
            bar(v.tensor_scalar(z["s_col"], z["s_col"], 4096.0, None, ALU.is_lt))
            bar(v.tensor_tensor(z["s_col"], z["s_col"], z["s_meta"][:, 2:3], ALU.mult))
            bar(v.tensor_scalar(z["s_coefT"], z["s_oh"], z["s_col"], None, ALU.mult))
            bar(v.tensor_scalar(z["s_oh"], z["sio"], z["s_meta"][:, 5:6], None, ALU.is_equal))
            bar(v.tensor_tensor(z["s_tmp8"], z["s_oh"], z["s_gpb"], ALU.mult))
            bar(v.tensor_reduce(z["s_col2"].rearrange("p (a o) -> p a o", o=1),
                                z["s_tmp8"].rearrange("p (a e) -> p a e", e=8), AX.X, ALU.add))
            bar(v.tensor_tensor(z["s_col2"], z["s_col2"], z["s_meta"][:, 1:2], ALU.add))
            bar(v.tensor_scalar(z["s_col2"], z["s_col2"], 4096.0, None, ALU.is_lt))
            bar(v.tensor_tensor(z["s_col2"], z["s_col2"], z["s_meta"][:, 3:4], ALU.mult))
            bar(v.tensor_scalar(z["s_oh"], z["s_oh"], z["s_col2"], None, ALU.mult))
            C.inc(v.tensor_tensor(z["s_ct"][tt], z["s_coefT"], z["s_oh"], ALU.add), cV, 1,
                  key=f"coefT{tt}")
        for e in range(8):
            for ch in range(NCH4):
                c0 = 512 * ch
                C.w(v, cT, f"crow_mm_e{e}_c{ch}")
                C.inc(v.tensor_copy(z["s_crow"], b4_row), cV, 1, key=f"crow_ev2_e{e}_c{ch}")
                M[C.prefix + f"crow_ev_e{e}_c{ch}"] = M[C.prefix + f"crow_ev2_e{e}_c{ch}"]
                C.w(v, cT, f"cb_mm_e{e}_c{ch}")
                C.inc(v.tensor_copy(z["s_cb"], b4_cbc), cV, 1, key=f"cb_ev_e{e}_c{ch}")
                C.w(v, cT, f"y_mm_e{e}_c{ch}")
                for o in range(DT):
                    v.tensor_tensor(z["s_ytmp"], b4_y[o], z["s_cb"], ALU.mult)
                    last = v.tensor_tensor(z["s_acc"][o][:, c0:c0 + 512],
                                           z["s_acc"][o][:, c0:c0 + 512], z["s_ytmp"], ALU.add)
                C.inc(last, cV, 1, key=f"accdone_e{e}_c{ch}")
        for i in range(DT):
            C.w(v, qS, f"x1ld{i}")
            if i > 0:
                C.w(v, qS, f"x2out{i - 1}")
            v.tensor_scalar(z["s_x2"], z["s_acc"][i], z["s_bias"][:, i:i + 1], None, ALU.add)
            C.inc(v.tensor_tensor(z["s_x2"], z["s_x2"], z["s_x1"], ALU.add), cV, 1,
                  key=f"x2_{i}")

    PH4 = dict(sync=p4_sync, gpsimd=p4_gpsimd, scalar=p4_scalar, tensor=p4_tensor, vector=p4_vector)

    pool.reset()
    z5 = {}
    z5["sx"] = [pool.f32(128, TP) for _ in range(DT)]
    z5["sxb"] = [pool.bf16(128, TP) for _ in range(DT)]
    z5["ssq"] = pool.f32(128, 512)
    z5["sonesc"] = pool.f32(128, 1)
    z5["sonesr"] = pool.f32(1, 128)
    z5["s_mu"] = pool.f32(1, TP)
    z5["s_var"] = pool.f32(1, TP)
    z5["s_mr"] = pool.f32(1, TP)
    z5["s_wcol"] = [pool.bf16(128, D) for _ in range(2)]
    z5["s_xz"] = pool.f32(128, TP)
    z5["s_conv"] = pool.f32(128, TC)
    z5["s_cw"] = pool.f32(128, 64)
    z5["s_cb"] = pool.f32(128, 16)
    z5["s_bdt"] = pool.f32(128, 16)
    z5["s_xcb"] = [pool.bf16(128, TC) for _ in range(2)]
    z5["s_zb"] = [pool.bf16(128, TC) for _ in range(2)]
    z5["s_xc"] = []
    for i in range(DT):
        z5["s_xc"].append(z5["sx"][i][:, 0:1024].bitcast(BF16))
        z5["s_xc"].append(z5["sx"][i][:, 1024:2048].bitcast(BF16))
    z5["s_wdt"] = [pool.bf16(128, DI) for _ in range(2)]
    z5["s_wxp"] = [pool.bf16(128, 32) for _ in range(IT)]
    z5["s_dv"] = z5["s_xz"][:, 0:TC]
    z5["s_abs"] = pool.f32(128, TC)
    z5["s_rel"] = z5["s_conv"]
    z5["s_dhi"] = z5["s_xcb"][0]
    z5["s_dlo"] = z5["s_xcb"][1]
    z5["s_xhi"] = z5["s_zb"][0]
    z5["s_xlo"] = z5["s_zb"][1]
    z5["s_dxt"] = pool.f32(128, TC)
    z5["s_sp32"] = pool.f32(32, TC)
    z5["s_halo"] = pool.f32(128, 3)
    z5["s_hm"] = pool.f32(128, 1)
    b5_bc1 = p_pool[:, 0:512]
    b5_bc2 = p_pool[:, 512:1024]
    b5_mm = p_pool[:, 1024:1536]
    b5_dt = p_pool[:, 2048:4096]
    b5_sp = p_pool[0:32, 0:2048]

    def p5_sync(sync):
        z = z5
        C.inc(sync.dma_start(out=z["s_hm"], in_=hm_in[:]), qS, 16, key="hmld")
        for i in range(DT):
            C.w(sync, cV, f"zero{i}")
            C.inc(sync.dma_start(out=z["sx"][i][:, PAD:], in_=d_x2[128 * i:128 * (i + 1), :]),
                  qS, 16)
            C.inc(sync.dma_start(out=z["s_halo"], in_=d_halo[128 * i:128 * (i + 1), :]),
                  qS, 16, key=f"hld{i}")
            C.w(sync, cV, f"halo{i}")
        C.inc(sync.dma_start(out=z["sonesc"], in_=onesc_in[:]), qS, 16)
        C.inc(sync.dma_start(out=z["sonesr"], in_=onesr_in[:]), qS, 16, key="ld_x")
        C.w(sync, cV, f"ln_ev{len(CH) - 1}")
        C.w(sync, cV, f"bf16{DT - 1}")
        for k in range(IT):
            C.w(sync, qG, f"xcout{k}")
            C.inc(sync.dma_start(out=z["s_xc"][k], in_=d_xc2[128 * k:128 * (k + 1), :]), qS, 16,
                  key=f"xcld{k}")

    def p5_gpsimd(g):
        z = z5
        C.inc(g.dma_start(out=z["s_cw"], in_=c1w_in[:].rearrange("(t p) o k -> p t o k", p=128)),
              qG, 16)
        for _t in range(16):
            C.inc(g.dma_start(out=z["s_cb"][:, _t:_t + 1],
                              in_=c1b_in[128 * _t:128 * (_t + 1)].unsqueeze(1)), qG, 16)
            C.inc(g.dma_start(out=z["s_bdt"][:, _t:_t + 1],
                              in_=bdt_in[128 * _t:128 * (_t + 1)].unsqueeze(1)), qG, 16,
                  key="ld_misc" if _t == 15 else None)
        for j in range(ET):
            if j >= 2:
                C.w(g, cT, f"xz_done{j - 2}")
            C.inc(g.dma_start(out=z["s_wcol"][j % 2],
                              in_=w1in_in[:, 128 * j:128 * (j + 1)].rearrange(
                                  "(k p) e -> p k e", p=128)), qG, 16, key=f"wcol{j}")
            if j >= 2:
                jo = j - 2
                C.w(g, cS, f"act{jo}")
                if jo < IT:
                    C.inc(g.dma_start(out=d_xc2[128 * jo:128 * (jo + 1), :], in_=z["s_xcb"][jo % 2]),
                          qG, 16, key=f"xcout{jo}")
                else:
                    C.inc(g.dma_start(out=d_z2[128 * (jo - IT):128 * (jo - IT + 1), :],
                                      in_=z["s_zb"][jo % 2]), qG, 16, key=f"xcout{jo}")
        for jo in (ET - 2, ET - 1):
            C.w(g, cS, f"act{jo}")
            C.inc(g.dma_start(out=d_z2[128 * (jo - IT):128 * (jo - IT + 1), :],
                              in_=z["s_zb"][jo % 2]), qG, 16, key=f"xcout{jo}")
        for k in range(IT):
            C.inc(g.dma_start(out=z["s_wxp"][k], in_=wxp_in[128 * k:128 * (k + 1), :]), qG, 16,
                  key=f"wxp{k}")
        for o in range(IT):
            if o >= 2:
                C.w(g, cT, f"dt_done{o - 2}")
            C.inc(g.dma_start(out=z["s_wdt"][o % 2],
                              in_=wdt_in[:, 128 * o:128 * (o + 1)].rearrange(
                                  "(k p) e -> p k e", p=128)), qG, 16, key=f"wdtcol{o}")

    def p5_scalar(s):
        z = z5
        C.w(s, qS, "ld_x")
        C.w(s, cV, f"halo{DT - 1}")
        ln_scalar(s, z["sx"], z["ssq"], z["s_var"], CH, DT)
        for j in range(ET):
            inner = j < IT
            C.w(s, cV, f"conv{j}" if inner else f"xz_ev{j}")
            if j >= 2:
                C.w(s, qG, f"xcout{j - 2}")
            if inner:
                C.inc(s.activation(z["s_xcb"][j % 2], in_=z["s_conv"],
                                   func=AF.Silu, bias=z["s_cb"][:, j:j + 1]), cS, 1, key=f"act{j}")
            else:
                C.inc(s.activation(z["s_zb"][j % 2], in_=z["s_xz"][:, PAD:], func=AF.Copy),
                      cS, 1, key=f"act{j}")
        def emit_douts(oo):
            C.w(s, cV, f"dxdone{oo}")
            C.inc(s.dma_start(out=d_dh[128 * oo:128 * (oo + 1), :], in_=z["s_dhi"]), qA, 16)
            C.inc(s.dma_start(out=d_dl[128 * oo:128 * (oo + 1), :], in_=z["s_dlo"]), qA, 16)
            C.inc(s.dma_start(out=d_xh[128 * oo:128 * (oo + 1), :], in_=z["s_xhi"]), qA, 16)
            C.inc(s.dma_start(out=d_xl[128 * oo:128 * (oo + 1), :], in_=z["s_xlo"]), qA, 16,
                  key=f"dout{oo}")
        for o in range(IT):
            C.w(s, cT, f"dt_mm{o}")
            if o > 0:
                C.w(s, cV, f"absread{o - 1}")
            s.activation(z["s_abs"], in_=b5_dt, func=AF.Abs, bias=z["s_bdt"][:, o:o + 1])
            s.activation(z["s_abs"], in_=z["s_abs"], func=AF.Exp, scale=-1.0)
            C.inc(s.activation(z["s_abs"], in_=z["s_abs"], func=AF.Ln, bias=1.0), cS, 1,
                  key=f"soft{o}")
            if o > 0:
                emit_douts(o - 1)
        emit_douts(IT - 1)
        C.w(s, cV, "sp_ev")
        C.inc(s.dma_start(out=d_bs[:], in_=z["s_sp32"][16:32, :]), qA, 16, key="bsout")

    def p5_tensor(t):
        z = z5
        C.w(t, qS, "ld_x")
        C.w(t, cV, f"halo{DT - 1}")
        ln_tensor(t, z["sx"], z["ssq"], z["s_var"], z["s_mr"], z["sonesc"], z["sonesr"],
                  b5_bc1, b5_bc2, CH, DT)
        for i in range(DT):
            C.w(t, cV, f"bf16{i}")
        for j in range(ET):
            C.w(t, qG, f"wcol{j}")
            for ci, (c0, w) in enumerate(CH):
                pg = j * len(CH) + ci
                if pg > 0:
                    C.w(t, cV, f"xz_psfree_g{pg - 1}")
                for i in range(DT):
                    mmx = t.matmul(b5_mm[:, 0:w], z["s_wcol"][j % 2][:, 128 * i:128 * (i + 1)],
                                   z["sxb"][i][:, c0:c0 + w], start=(i == 0), stop=(i == DT - 1))
                C.inc(mmx, cT, 1, key=f"xz_mm_g{pg}")
            C.setk(f"xz_done{j}", cT)
        C.w(t, cV, f"xz_psfree_g{ET * len(CH) - 1}")
        for k in range(IT):
            C.w(t, qS, f"xcld{k}")
            C.w(t, qG, f"wxp{k}")
            for (c0, w) in CHO:
                mms = t.matmul(b5_sp[:, c0:c0 + w], z["s_wxp"][k], z["s_xc"][k][:, c0:c0 + w],
                               start=(k == 0), stop=(k == IT - 1))
            C.inc(mms, cT, 1, key=f"sp_mm{k}")
        for o in range(IT):
            C.w(t, qG, f"wdtcol{o}")
            if o > 0:
                C.w(t, cS, f"soft{o - 1}")
                C.w(t, cV, f"dvread{o - 1}")
            for (c0, w) in CHO:
                for k in range(IT):
                    mmd = t.matmul(b5_dt[:, c0:c0 + w], z["s_wdt"][o % 2][:, 128 * k:128 * (k + 1)],
                                   z["s_xc"][k][:, c0:c0 + w], start=(k == 0), stop=(k == IT - 1))
            C.inc(mmd, cT, 1, key=f"dt_mm{o}")
            C.setk(f"dt_done{o}", cT)

    def p5_vector(v):
        z = z5
        C.w(v, qS, "hmld")
        for i in range(DT):
            C.inc(v.memset(z["sx"][i][:, 0:PAD - 3], 0.0), cV, 1, key=f"zero{i}")
            C.w(v, qS, f"hld{i}")
            C.inc(v.tensor_scalar(z["sx"][i][:, PAD - 3:PAD], z["s_halo"], z["s_hm"],
                                  None, ALU.mult), cV, 1, key=f"halo{i}")
        ln_vector(v, z["sx"], z["sxb"], z["s_mu"], z["s_var"], z["s_mr"], b5_bc1, b5_bc2, CH, DT)
        for j in range(ET):
            if j >= 1:
                C.w(v, cS, f"act{j - 1}")
            for ci, (c0, w) in enumerate(CH):
                pg = j * len(CH) + ci
                C.w(v, cT, f"xz_mm_g{pg}")
                C.inc(v.tensor_copy(z["s_xz"][:, c0:c0 + w], b5_mm[:, 0:w]), cV, 1,
                      key=f"xz_psfree_g{pg}")
            C.setk(f"xz_ev{j}", cV)
            if j < IT:
                for k in range(4):
                    srcv = z["s_xz"][:, PAD - 3 + k: PAD - 3 + k + TC]
                    wcol = z["s_cw"][:, 4 * j + k: 4 * j + k + 1]
                    if k == 0:
                        v.tensor_scalar(z["s_conv"], srcv, wcol, None, ALU.mult)
                    else:
                        last = v.scalar_tensor_tensor(z["s_conv"], srcv, wcol, z["s_conv"],
                                                      ALU.mult, ALU.add)
                C.inc(last, cV, 1, key=f"conv{j}")
        C.w(v, cT, f"sp_mm{IT - 1}")
        C.inc(v.tensor_copy(z["s_sp32"], b5_sp), cV, 1, key="sp_ev")
        for o in range(IT):
            C.w(v, cS, f"soft{o}")
            C.inc(v.tensor_scalar(z["s_dv"], b5_dt, z["s_bdt"][:, o:o + 1], None, ALU.add),
                  cV, 1, key=f"dvread{o}")
            v.tensor_scalar(z["s_rel"], z["s_dv"], 0.0, None, ALU.max)
            C.inc(v.tensor_tensor(z["s_dv"], z["s_rel"], z["s_abs"], ALU.add), cV, 1,
                  key=f"absread{o}")
            C.w(v, qA, f"dout{o - 1}" if o > 0 else "none")
            v.tensor_copy(z["s_dhi"], z["s_dv"])
            v.tensor_tensor(z["s_dlo"], z["s_dv"], z["s_dhi"], ALU.subtract)
            v.tensor_tensor(z["s_dxt"], z["s_dv"], z["s_xc"][o], ALU.mult)
            v.tensor_copy(z["s_xhi"], z["s_dxt"])
            C.inc(v.tensor_tensor(z["s_xlo"], z["s_dxt"], z["s_xhi"], ALU.subtract), cV, 1,
                  key=f"dxdone{o}")

    PH5 = dict(sync=p5_sync, gpsimd=p5_gpsimd, scalar=p5_scalar, tensor=p5_tensor, vector=p5_vector)

    pool.reset()
    z6 = {}
    z6["sidx"] = pool.i32(8, 256)
    z6["s_stg"] = pool.bf16(8, 4 * L)
    z6["s_bb"] = pool.f32(128, L)
    z6["s_a"] = pool.f32(128, L)
    z6["s_u"] = pool.f32(128, L)
    z6["s_s"] = pool.f32(128, L)
    z6["s_sb"] = pool.bf16(128, L)
    z6["s_y"] = pool.f32(32, L)
    z6["s_yb"] = pool.bf16(32, L)
    z6["s_A"] = pool.f32(128, 32)
    z6["s_oh8"] = pool.bf16(8, 128)
    z6["s_ind"] = [pool.bf16(128, 32) for _ in range(4)]
    b6_d = p_pool[:, 0:2048]
    b6_x = p_pool[:, 2048:4096]
    b6_y32 = p_pool[0:32, :]

    def p6_sync(sync):
        z = z6
        C.inc(sync.dma_start(out=z["s_oh8"], in_=oh8_in[:]), qS, 16)
        C.inc(sync.dma_start(out=z["sidx"], in_=sidx_in[:]), qS, 16)
        for j in range(4):
            C.inc(sync.dma_start(out=z["s_ind"][j], in_=ind4_in[j]), qS, 16)
        aflat = alog_in[:].rearrange("d n -> (d n)")
        for _k in range(32):
            C.inc(sync.dma_start(out=z["s_A"][:, _k:_k + 1],
                                 in_=aflat[128 * _k:128 * (_k + 1)].unsqueeze(1)), qS, 16,
                  key="ld_c" if _k == 31 else None)

    def p6_gpsimd(g):
        z = z6
        for ti, (dsrc, ddst) in enumerate([(d_dh, d_dhg), (d_dl, d_dlg),
                                           (d_xh, d_xhg), (d_xl, d_xlg)]):
            C.inc(g.collective_compute("AllGather", ALU.bypass,
                                       replica_groups=[list(range(NC))],
                                       ins=[dsrc[:]], outs=[ddst[:]]), ccs[4 + ti], 1)
            g.wait_ge(ccs[4 + ti], 1)
        C.inc(g.collective_compute("AllGather", ALU.bypass, replica_groups=[list(range(NC))],
                             ins=[d_bs[:]], outs=[d_bg[:]]), ccs[8], 1)
        g.wait_ge(ccs[8], 1)
        C.w(g, qS, "ld_c")
        srcs = [d_dhg, d_dlg, d_xhg, d_xlg]
        for ks in range(128):
            s2, m = ks // 32, ks % 32
            if m == 0:
                if s2 > 0:
                    C.w(g, cV, f"sb{32 * s2 - 1}")
                for h in range(2):
                    blk = 2 * s2 + h
                    rep = d_bg[NST * blk:NST * (blk + 1), :].unsqueeze(0).to_broadcast(
                        [8, NST, TC])
                    C.inc(g.dma_start(out=z["s_bb"][:, TC * h:TC * (h + 1)], in_=rep), qG, 16,
                          key=f"bb{s2}" if h == 1 else None)
            if ks > 0:
                C.w(g, cT, f"bcx1_{ks - 1}")
            for ti in range(4):
                for h in range(2):
                    C.inc(g.indirect_dma_start(
                        out=z["s_stg"][0:8, L * ti + TC * h:L * ti + TC * (h + 1)],
                        out_offset=None, in_=srcs[ti][:],
                        in_offset=bass.IndirectOffsetOnAxis(
                            z["sidx"][:, 2 * ks + h:2 * ks + h + 1], 0)),
                        qG, 16, key=f"stg{ks}" if (ti == 3 and h == 1) else None)
        C.w(g, qA, "yout31")
        C.inc(g.collective_compute("AllGather", ALU.bypass, replica_groups=[list(range(NC))],
                             ins=[d_ysl[:]], outs=[d_yg[:]]), ccs[9], 1)
        g.wait_ge(ccs[9], 1)

    def p6_scalar(s_):
        z = z6
        C.w(s_, qS, "ld_c")
        C.inc(s_.activation(z["s_A"], in_=z["s_A"], func=AF.Exp), cS, 1, key="aexp")
        for ks in range(128):
            for r in range(2):
                C.w(s_, cT, f"bcd{r}_{ks}")
                C.w(s_, cV, "aneg")
                C.inc(s_.activation(z["s_a"][:, 2048 * r:2048 * (r + 1)], in_=b6_d,
                                    func=AF.Exp, scale=z["s_A"][:, ks % 32:ks % 32 + 1]),
                      cS, 1, key=f"aev{r}_{ks}")
            if ks % 4 == 3:
                gg = ks // 4
                C.w(s_, cV, f"yb{gg}")
                s3, grp = gg // 8, gg % 8
                C.inc(s_.dma_start(out=d_ysl[32 * grp:32 * (grp + 1), L * s3:L * (s3 + 1)],
                                   in_=z["s_yb"]), qA, 16, key=f"yout{gg}")

    def p6_tensor(t):
        z = z6
        C.w(t, qS, "ld_c")
        for ks in range(128):
            j = ks % 4
            C.w(t, qG, f"stg{ks}")
            if ks > 0:
                C.w(t, cV, f"yacc{ks - 1}")
            for r in range(2):
                if r == 1:
                    C.w(t, cS, f"aev0_{ks}")
                    C.w(t, cV, f"uev0_{ks}")
                c_lo = 2048 * r
                for cc2 in range(4):
                    c0 = c_lo + 512 * cc2
                    t.matmul(b6_d[:, 512 * cc2:512 * (cc2 + 1)], z["s_oh8"],
                             z["s_stg"][0:8, c0:c0 + 512], start=True, stop=False)
                    mmd = t.matmul(b6_d[:, 512 * cc2:512 * (cc2 + 1)], z["s_oh8"],
                                   z["s_stg"][0:8, L + c0:L + c0 + 512], start=False, stop=True)
                C.inc(mmd, cT, 1, key=f"bcd{r}_{ks}")
                for cc2 in range(4):
                    c0 = c_lo + 512 * cc2
                    t.matmul(b6_x[:, 512 * cc2:512 * (cc2 + 1)], z["s_oh8"],
                             z["s_stg"][0:8, 2 * L + c0:2 * L + c0 + 512], start=True, stop=False)
                    mmx = t.matmul(b6_x[:, 512 * cc2:512 * (cc2 + 1)], z["s_oh8"],
                                   z["s_stg"][0:8, 3 * L + c0:3 * L + c0 + 512],
                                   start=False, stop=True)
                C.inc(mmx, cT, 1, key=f"bcx{r}_{ks}")
            C.w(t, cV, f"sb{ks}")
            for cc2 in range(8):
                mr = t.matmul(p_pool[0:32, 512 * cc2:512 * (cc2 + 1)], z["s_ind"][j],
                              z["s_sb"][:, 512 * cc2:512 * (cc2 + 1)], start=True, stop=True)
            C.inc(mr, cT, 1, key=f"red{ks}")

    def p6_vector(v):
        z = z6
        C.w(v, cS, "aexp")
        C.inc(v.tensor_scalar(z["s_A"], z["s_A"], -1.0, None, ALU.mult), cV, 1, key="aneg")
        for ks in range(128):
            s2 = ks // 32
            j = ks % 4
            if ks % 32 == 0:
                C.w(v, qG, f"bb{s2}")
            for r in range(2):
                C.w(v, cT, f"bcx{r}_{ks}")
                C.inc(v.tensor_tensor(z["s_u"][:, 2048 * r:2048 * (r + 1)], b6_x,
                                      z["s_bb"][:, 2048 * r:2048 * (r + 1)], ALU.mult),
                      cV, 1, key=f"uev{r}_{ks}")
            C.w(v, cS, f"aev1_{ks}")
            v.tensor_tensor_scan(z["s_s"], z["s_a"], z["s_u"], 0.0, ALU.mult, ALU.add)
            C.inc(v.tensor_tensor(z["s_sb"], z["s_s"], z["s_bb"], ALU.mult), cV, 1,
                  key=f"sb{ks}")
            C.w(v, cT, f"red{ks}")
            if j == 0:
                if ks >= 4:
                    C.w(v, qA, f"yout{(ks - 4) // 4}")
                C.inc(v.tensor_copy(z["s_y"], b6_y32), cV, 1, key=f"yacc{ks}")
            else:
                C.inc(v.tensor_tensor(z["s_y"], z["s_y"], b6_y32, ALU.add), cV, 1,
                      key=f"yacc{ks}")
            if j == 3:
                C.inc(v.tensor_copy(z["s_yb"], z["s_y"]), cV, 1, key=f"yb{ks // 4}")

    PH6 = dict(sync=p6_sync, gpsimd=p6_gpsimd, scalar=p6_scalar, tensor=p6_tensor, vector=p6_vector)

    pool.reset()
    z7 = {}
    z7["cidx"] = pool.i32(128, 16)
    z7["s_y"] = [pool.bf16(128, TC) for _ in range(IT)]
    z7["s_w"] = [pool.bf16(128, D) for _ in range(IT)]
    z7["s_yt"] = pool.bf16(128, TC)
    z7["s_xct"] = pool.bf16(128, TC)
    z7["s_zt"] = pool.bf16(128, TC)
    z7["s_slz"] = pool.bf16(128, TC)
    z7["s_ydp"] = pool.f32(128, TC)
    z7["s_dp"] = pool.f32(128, IT)
    z7["s_xt"] = pool.f32(128, TC)
    z7["s_o"] = pool.f32(128, TC)
    b7_mm = p_pool[:, 0:2048]

    def p7_sync(sync):
        z = z7
        C.inc(sync.dma_start(out=z["cidx"], in_=cidx_in[:]), qS, 16, key="cild")
        for _t in range(IT):
            C.inc(sync.dma_start(out=z["s_dp"][:, _t:_t + 1],
                                 in_=dp_in[128 * _t:128 * (_t + 1)].unsqueeze(1)), qS, 16,
                  key="ld_c" if _t == IT - 1 else None)
        for j in range(IT):
            C.w(sync, cV, f"y3_{j - 1}" if j > 0 else "none")
            C.inc(sync.dma_start(out=z["s_xct"], in_=d_xc2[128 * j:128 * (j + 1), :]), qS, 16)
            C.inc(sync.dma_start(out=z["s_zt"], in_=d_z2[128 * j:128 * (j + 1), :]), qS, 16,
                  key=f"tld{j}")
        for i in range(DT):
            C.w(sync, cV, f"x3_{i - 1}" if i > 0 else "none")
            C.inc(sync.dma_start(out=z["s_xt"], in_=d_x2[128 * i:128 * (i + 1), :]), qS, 16,
                  key=f"x2ld{i}")
            C.w(sync, cV, f"x3_{i}")
            C.inc(sync.dma_start(out=d_x3[128 * i:128 * (i + 1), :], in_=z["s_o"]), qS, 16,
                  key=f"x3out{i}")

    def p7_gpsimd(g):
        z = z7
        yg_flat = d_yg[:].rearrange("r (a c) -> (r a) c", c=TC)
        C.w(g, qS, "cild")
        for j in range(IT):
            C.w(g, cV, f"y3_{j - 1}" if j > 0 else "none")
            C.inc(g.indirect_dma_start(out=z["s_yt"], out_offset=None, in_=yg_flat,
                                       in_offset=bass.IndirectOffsetOnAxis(
                                           z["cidx"][:, j:j + 1], 0)), qG, 16, key=f"yld{j}")
        for j in range(IT):
            C.inc(g.dma_start(out=z["s_w"][j], in_=w1o_in[128 * j:128 * (j + 1), :]), qG, 16,
                  key=f"wld{j}")

    def p7_scalar(s):
        z = z7
        for j in range(IT):
            C.w(s, qS, f"tld{j}")
            if j >= 1:
                C.w(s, cV, f"y3_{j - 1}")
            C.inc(s.activation(z["s_slz"], in_=z["s_zt"], func=AF.Silu), cS, 1, key=f"silz{j}")

    def p7_tensor(t):
        z = z7
        for j in range(IT):
            C.w(t, qG, f"wld{j}")
        for i in range(DT):
            C.w(t, cV, f"y3_{IT - 1}")
            if i > 0:
                C.w(t, cV, f"oev{i - 1}")
            for (c0, w) in CHO:
                for j in range(IT):
                    mm = t.matmul(b7_mm[:, c0:c0 + w], z["s_w"][j][:, 128 * i:128 * (i + 1)],
                                  z["s_y"][j][:, c0:c0 + w], start=(j == 0), stop=(j == IT - 1))
            C.inc(mm, cT, 1, key=f"att{i}")

    def p7_vector(v):
        z = z7
        for j in range(IT):
            C.w(v, qS, f"tld{j}")
            C.w(v, qG, f"yld{j}")
            C.w(v, cS, f"silz{j}")
            v.scalar_tensor_tensor(z["s_ydp"], z["s_xct"], z["s_dp"][:, j:j + 1], z["s_yt"],
                                   ALU.mult, ALU.add)
            C.inc(v.tensor_tensor(z["s_y"][j], z["s_ydp"], z["s_slz"], ALU.mult), cV, 1,
                  key=f"y3_{j}")
        for i in range(DT):
            C.w(v, cT, f"att{i}")
            C.w(v, qS, f"x2ld{i}")
            if i > 0:
                C.w(v, qS, f"x3out{i - 1}")
            C.inc(v.tensor_tensor(z["s_o"], z["s_xt"], b7_mm, ALU.add), cV, 1, key=f"oev{i}")
            M[C.prefix + f"x3_{i}"] = M[C.prefix + f"oev{i}"]

    PH7 = dict(sync=p7_sync, gpsimd=p7_gpsimd, scalar=p7_scalar, tensor=p7_tensor, vector=p7_vector)

    pool.reset()
    z8 = {}
    W8 = 2816
    z8["s_x"] = pool.f32(128, W8)
    z8["s_q"] = pool.bf16(128, W8)
    z8["scr"] = dict(xn=pool.f32(128, W8), c=pool.bf16(128, W8), c2=pool.bf16(128, W8),
                     c3=pool.bf16(128, W8), sgn=pool.bf16(128, W8),
                     sc=pool.f32(128, W8 // 16), sc2=pool.f32(128, W8 // 16))
    z8["s_tr"] = pool.bf16(128, 128)
    z8["seye"] = pool.bf16(128, 128)
    b8_tr = p_pool[:, 1024:1088].bitcast(BF16)[:, 0:128]
    jobs8 = []
    for r_ in range(3):
        jobs8.append((f"g{r_}", wg_in, r_, 1024, d_wgt))
        jobs8.append((f"u{r_}", wu_in, r_, 1024, d_wut))
    jobs8.append(("d0", wd_in, 0, 2816, d_wdt))

    def p8_sync(sync):
        z = z8
        C.inc(sync.dma_start(out=z["seye"], in_=eye_in[:]), qS, 16, key="ld_eye")
        for name, din, r_, w, dout in jobs8:
            C.w(sync, cV, f"free_{name}")
            C.inc(sync.dma_start(out=z["s_x"][:, 0:w], in_=din[128 * r_:128 * (r_ + 1), :]),
                  qS, 16, key=f"ld_{name}")
            for k in range(w // 128):
                C.w(sync, cV, f"trev_{name}_{k}")
                C.inc(sync.dma_start(out=dout[128 * k:128 * (k + 1), 128 * r_:128 * (r_ + 1)],
                                     in_=z["s_tr"]), qS, 16, key=f"trout_{name}_{k}")

    def p8_gpsimd(g):
        C.w(g, qS, f"trout_d0_{2816 // 128 - 1}")
        for di, (dsrc, ddst, cci) in enumerate([(d_wgt, d_wgtg, 10), (d_wut, d_wutg, 11),
                                                (d_wdt, d_wdtg, 12)]):
            C.inc(g.collective_compute("AllGather", ALU.bypass,
                                       replica_groups=[list(range(NC))],
                                       ins=[dsrc[:]], outs=[ddst[:]]), ccs[cci], 1)
            g.wait_ge(ccs[cci], 1)

    def p8_scalar(s_):
        z = z8
        prevq = None
        for name, din, r_, w, dout in jobs8:
            C.w(s_, qS, f"ld_{name}")
            if prevq is not None:
                C.w(s_, cV, f"qdone_{prevq}")
            prevq = name
            s_.activation(z["scr"]["xn"][:, 0:w], in_=z["s_x"][:, 0:w], func=AF.Abs)
            C.inc(s_.activation(z["scr"]["sgn"][:, 0:w], in_=z["s_x"][:, 0:w], func=AF.Sign),
                  cS, 1, key=f"sgn_{name}")

    def p8_tensor(t):
        z = z8
        C.w(t, qS, "ld_eye")
        prev2 = None
        for name, din, r_, w, dout in jobs8:
            C.w(t, cV, f"qdone_{name}")
            for k in range(w // 128):
                if prev2 is not None:
                    C.w(t, cV, f"trev_{prev2[0]}_{prev2[1]}")
                C.inc(t.transpose(b8_tr, z["s_q"][:, 128 * k:128 * (k + 1)], z["seye"]),
                      cT, 1, key=f"tr_{name}_{k}")
                prev2 = (name, k)

    def p8_vector(v):
        z = z8
        first = True
        prev = None
        for name, din, r_, w, dout in jobs8:
            M[C.prefix + f"free_{name}"] = 0 if first else M[C.prefix + f"qdone_{prev}"]
            first = False
            sub = dict(xn=z["scr"]["xn"][:, 0:w], c=z["scr"]["c"][:, 0:w],
                       c2=z["scr"]["c2"][:, 0:w], c3=z["scr"]["c3"][:, 0:w],
                       sgn=z["scr"]["sgn"][:, 0:w], sc=z["scr"]["sc"][:, 0:w // 16],
                       sc2=z["scr"]["sc2"][:, 0:w // 16])
            emit_quant_free(C, v, z["s_x"][:, 0:w], z["s_q"][:, 0:w], w, sub,
                            f"qdone_{name}", f"sgn_{name}")
            for k in range(w // 128):
                C.w(v, cT, f"tr_{name}_{k}")
                C.inc(v.tensor_copy(z["s_tr"], b8_tr), cV, 1, key=f"trev_{name}_{k}")
                C.w(v, qS, f"trout_{name}_{k}")
            prev = name

    PH8 = dict(sync=p8_sync, gpsimd=p8_gpsimd, scalar=p8_scalar, tensor=p8_tensor, vector=p8_vector)

    pool.reset()
    z9 = {}
    z9["sx"] = [pool.f32(128, TC) for _ in range(DT)]
    z9["ssq"] = pool.f32(128, 512)
    z9["sonesc"] = pool.f32(128, 1)
    z9["sonesr"] = pool.f32(1, 128)
    z9["s_mu"] = pool.f32(1, TC)
    z9["s_var"] = pool.f32(1, TC)
    z9["s_mr"] = pool.f32(1, TC)
    z9["s_xq"] = [pool.bf16(128, TC) for _ in range(DT)]
    z9["s_sil"] = pool.bf16(128, TC)
    z9["s_up"] = pool.bf16(128, TC)
    z9["s_h"] = pool.f32(128, TC)
    z9["s_x3t"] = pool.f32(128, TC)
    z9["s_xT"] = z9["s_x3t"]
    z9["s_qT"] = pool.bf16(128, TC)
    z9["scr"] = dict(xn=pool.f32(128, TC), c=pool.bf16(128, TC),
                     c2=z9["s_sil"], c3=z9["s_up"],
                     sgn=pool.bf16(128, TC),
                     sc=pool.f32(128, TC // 16), sc2=pool.f32(128, TC // 16))
    z9["s_hq"] = []
    for i in range(DT):
        z9["s_hq"].append(z9["sx"][i][:, 0:1024].bitcast(BF16))
        z9["s_hq"].append(z9["sx"][i][:, 1024:2048].bitcast(BF16))
    z9["s_hq_extra"] = [pool.bf16(128, TC) for _ in range(22 - 16)]
    z9["s_hq"] = z9["s_hq"] + z9["s_hq_extra"]
    z9["s_wg"] = [pool.bf16(128, D) for _ in range(2)]
    z9["s_wu"] = [pool.bf16(128, D) for _ in range(2)]
    z9["s_wd"] = [pool.bf16(128, W8) for _ in range(1)]
    b9_bc1 = p_pool[:, 0:512]
    b9_bc2 = p_pool[:, 512:1024]
    b9_g = p_pool[:, 0:2048]
    b9_u = p_pool[:, 2048:4096]
    ITF9 = 22

    def wgtile(it):
        if it < 18:
            return it // 3, it % 3
        return 6 + (it - 18) // 2, (it - 18) % 2

    def p9_sync(sync):
        z = z9
        for i in range(DT):
            C.inc(sync.dma_start(out=z["sx"][i], in_=d_x3[128 * i:128 * (i + 1), :]), qS, 16)
        C.inc(sync.dma_start(out=z["sonesc"], in_=onesc_in[:]), qS, 16)
        C.inc(sync.dma_start(out=z["sonesr"], in_=onesr_in[:]), qS, 16, key="ld_x")
        for i in range(DT):
            C.w(sync, cV, f"x4_{i - 1}" if i > 0 else f"hq{ITF9 - 1}")
            C.inc(sync.dma_start(out=z["s_x3t"], in_=d_x3[128 * i:128 * (i + 1), :]), qS, 16,
                  key=f"x3ld{i}")
            C.w(sync, cV, f"x4_{i}")
            C.inc(sync.dma_start(out=o_out[128 * i:128 * (i + 1), :], in_=z["s_h"]), qS, 16,
                  key=f"x4out{i}")

    def p9_gpsimd(g):
        z = z9
        for it in range(ITF9):
            cblk, rblk = wgtile(it)
            if it >= 2:
                C.w(g, cT, f"gup_done{it - 2}")
            C.inc(g.dma_start(out=z["s_wg"][it % 2],
                              in_=d_wgtg[1024 * cblk:1024 * (cblk + 1),
                                         128 * rblk:128 * (rblk + 1)].rearrange(
                                  "(k p) e -> p k e", p=128)), qG, 16, key=f"wg{it}")
            C.inc(g.dma_start(out=z["s_wu"][it % 2],
                              in_=d_wutg[1024 * cblk:1024 * (cblk + 1),
                                         128 * rblk:128 * (rblk + 1)].rearrange(
                                  "(k p) e -> p k e", p=128)), qG, 16, key=f"wu{it}")
        for o in range(DT):
            if o >= 1:
                C.w(g, cT, f"dn_done{o - 1}")
            C.inc(g.dma_start(out=z["s_wd"][0],
                              in_=d_wdtg[2816 * o:2816 * (o + 1), :].rearrange(
                                  "(k p) e -> p k e", p=128)), qG, 16, key=f"wd{o}")

    def p9_scalar(s_):
        z = z9
        C.w(s_, qS, "ld_x")
        ln_scalar(s_, z["sx"], z["ssq"], z["s_var"], CHO, DT)
        for i in range(DT):
            C.w(s_, cV, f"xT{i}")
            s_.activation(z["scr"]["xn"], in_=z["s_xT"], func=AF.Abs)
            C.inc(s_.activation(z["scr"]["sgn"], in_=z["s_xT"], func=AF.Sign), cS, 1,
                  key=f"sgn_x{i}")
        for it in range(ITF9):
            C.w(s_, cT, f"g_mm{it}")
            if it > 0:
                C.w(s_, cV, f"hq{it - 1}")
            C.inc(s_.activation(z["s_sil"], in_=b9_g, func=AF.Silu), cS, 1, key=f"sil{it}")
            C.w(s_, cT, f"u_mm{it}")
            C.inc(s_.activation(z["s_up"], in_=b9_u, func=AF.Copy), cS, 1, key=f"up{it}")
            C.w(s_, cV, f"hT{it}")
            s_.activation(z["scr"]["xn"], in_=z["s_xT"], func=AF.Abs)
            C.inc(s_.activation(z["scr"]["sgn"], in_=z["s_xT"], func=AF.Sign), cS, 1,
                  key=f"sgn_h{it}")

    def p9_tensor(t):
        z = z9
        C.w(t, qS, "ld_x")
        ln_tensor(t, z["sx"], z["ssq"], z["s_var"], z["s_mr"], z["sonesc"], z["sonesr"],
                  b9_bc1, b9_bc2, CHO, DT)
        for it in range(ITF9):
            C.w(t, qG, f"wg{it}")
            for i in range(DT):
                C.w(t, cV, f"xq{i}")
            if it > 0:
                C.w(t, cS, f"sil{it - 1}")
            for (c0, w) in CHO:
                for k in range(DT):
                    mg = t.matmul(b9_g[:, c0:c0 + w], z["s_wg"][it % 2][:, 128 * k:128 * (k + 1)],
                                  z["s_xq"][k][:, c0:c0 + w], start=(k == 0), stop=(k == DT - 1))
            C.inc(mg, cT, 1, key=f"g_mm{it}")
            C.w(t, qG, f"wu{it}")
            C.w(t, cS, f"up{it - 1}" if it > 0 else "none")
            for (c0, w) in CHO:
                for k in range(DT):
                    mu_ = t.matmul(b9_u[:, c0:c0 + w], z["s_wu"][it % 2][:, 128 * k:128 * (k + 1)],
                                   z["s_xq"][k][:, c0:c0 + w], start=(k == 0), stop=(k == DT - 1))
            C.inc(mu_, cT, 1, key=f"u_mm{it}")
            C.setk(f"gup_done{it}", cT)
        for o in range(DT):
            C.w(t, qG, f"wd{o}")
            for it in range(ITF9):
                C.w(t, cV, f"hq{it}")
            if o == 0:
                C.w(t, cS, f"sil{ITF9 - 1}")
                C.w(t, cS, f"up{ITF9 - 1}")
                C.w(t, cV, f"hmul{ITF9 - 1}")
            if o > 0:
                C.w(t, cV, f"x4_{o - 1}")
            for (c0, w) in CHO:
                for k in range(ITF9):
                    md = t.matmul(b9_g[:, c0:c0 + w], z["s_wd"][0][:, 128 * k:128 * (k + 1)],
                                  z["s_hq"][k][:, c0:c0 + w], start=(k == 0), stop=(k == ITF9 - 1))
            C.inc(md, cT, 1, key=f"dn_mm{o}")
            C.setk(f"dn_done{o}", cT)

    def p9_vector(v):
        z = z9
        ln_vector(v, z["sx"], None, z["s_mu"], z["s_var"], z["s_mr"], b9_bc1, b9_bc2, CHO, DT)
        for i in range(DT):
            if i == 0:
                C.w(v, cV, f"ln_ev{len(CHO) - 1}")
            C.inc(v.transpose(z["s_xT"], z["sx"][i][:]), cV, 1, key=f"xT{i}")
            emit_quant_free(C, v, z["s_xT"], z["s_qT"], TC, z["scr"], f"xq_pre{i}", f"sgn_x{i}")
            C.inc(v.transpose(z["s_xq"][i][:], z["s_qT"]), cV, 1, key=f"xq{i}")
        for it in range(ITF9):
            C.w(v, cS, f"sil{it}")
            C.w(v, cS, f"up{it}")
            C.inc(v.tensor_tensor(z["s_h"], z["s_sil"], z["s_up"], ALU.mult), cV, 1,
                  key=f"hmul{it}")
            C.inc(v.transpose(z["s_xT"], z["s_h"]), cV, 1, key=f"hT{it}")
            emit_quant_free(C, v, z["s_xT"], z["s_qT"], TC, z["scr"], f"hq_pre{it}",
                            f"sgn_h{it}")
            C.inc(v.transpose(z["s_hq"][it], z["s_qT"]), cV, 1, key=f"hq{it}")
        for o in range(DT):
            C.w(v, cT, f"dn_mm{o}")
            C.w(v, qS, f"x3ld{o}")
            if o > 0:
                C.w(v, qS, f"x4out{o - 1}")
            C.inc(v.tensor_tensor(z["s_h"], z["s_x3t"], b9_g, ALU.add), cV, 1, key=f"x4_{o}")

    PH9 = dict(sync=p9_sync, gpsimd=p9_gpsimd, scalar=p9_scalar, tensor=p9_tensor, vector=p9_vector)


    phases = [PH0, PH1, PH2, PH3, PH4, PH5, PH6, PH7, PH8, PH9]

    ENG_END = dict(sync=("qS",), gpsimd=("qG",), scalar=("cS", "qA"),
                   tensor=("cT",), vector=("cV",))
    BAR_SEMS = [("qS", qS), ("qG", qG), ("qA", qA), ("cS", cS), ("cT", cT), ("cV", cV)]

    s_dummy = es.enter_context(nc.sbuf_tensor("s_dummy", [1, 16], F32))

    def drive(engname):
        def fn(eng):
            C.sec = engname
            for k, ph in enumerate(phases):
                C.prefix = f"P{k}:"
                if k > 0:
                    for sname, sref in BAR_SEMS:
                        C.wg(eng, sref, f"BAR{k - 1}_{sname}")
                    for cci, ccref in enumerate(ccs):
                        C.wg(eng, ccref, f"CCBAR{k - 1}_{cci}")
                ph[engname](eng)
                if engname == "vector":
                    C.inc(eng.memset(s_dummy[:], 0.0), cV, 1)
                elif engname == "scalar":
                    C.inc(eng.activation(s_dummy[:], in_=s_dummy[:], func=AF.Copy), cS, 1)
                for sn in ENG_END[engname]:
                    C.M[f"BAR{k}_{sn}"] = C.val(SEMS[sn])
                if engname == "gpsimd":
                    for cci, ccref in enumerate(ccs):
                        C.M[f"CCBAR{k}_{cci}"] = C.val(ccref)
            C.prefix = ""
            if engname == "sync":
                C.wv(eng, qS, C.val(qS))
                for sname, sref in BAR_SEMS[1:]:
                    C.wg(eng, sref, f"BAR{len(phases) - 1}_{sname}")
        return fn

    with es, nc.Block() as block:
        block.sync(drive("sync"))
        block.gpsimd(drive("gpsimd"))
        block.scalar(drive("scalar"))
        block.tensor(drive("tensor"))
        block.vector(drive("vector"))

    return nc


def build_mega():
    return two_pass(emit_mega)



_CACHE = {}


def _get(name, builder):
    if name not in _CACHE:
        _CACHE[name] = builder()
    return _CACHE[name]


def _pad_windows(xf):
    F = xf.shape[2]
    outs = []
    for c in range(NC):
        s, h = c // 2, c % 2
        win = np.zeros((TP, F), dtype=xf.dtype)
        if h == 1:
            win[PAD - 3:PAD] = xf[s, TC - 3:TC]
        win[PAD:] = xf[s, h * TC:(h + 1) * TC]
        outs.append(np.ascontiguousarray(win.T))
    return outs


def run_l0a(x, W_in, conv_w, conv_b, W_x):
    eye = np.eye(128, dtype=BF16NP)
    onesc = np.ones((128, 1), dtype=np.float32)
    onesr = np.ones((1, 128), dtype=np.float32)
    nc = _get("l0a", build_l0a)
    wins = _pad_windows(x)
    in_maps = [dict(x=w, w_in=W_in, conv_w=conv_w, conv_b=conv_b, w_x=W_x,
                    eye128=eye, onesc=onesc, onesr=onesr) for w in wins]
    res = run_bass_kernel_spmd(nc, in_maps, core_ids=list(range(NC)))
    return res.results


def run_ema(x_conv_full):
    nc = _get("ema", build_ema)
    in_maps = []
    for c in range(NC):
        sl = x_conv_full[:, :, 256 * c:256 * (c + 1)]
        xce = np.ascontiguousarray(sl.transpose(2, 0, 1).reshape(256, B * L))
        in_maps.append(dict(xce=xce))
    res = run_bass_kernel_spmd(nc, in_maps, core_ids=list(range(NC)))
    h = np.empty((4, L, DI), dtype=BF16NP)
    for c in range(NC):
        h[:, :, 256 * c:256 * (c + 1)] = np.asarray(
            res.results[c]["o_h"]).reshape(256, B, L).transpose(1, 2, 0)
    return h


def run_l0b(h_full, sc_by_core, z_by_core, xln_by_core, x, B_log, W_out, ln_g, ln_b):
    eye = np.eye(128, dtype=BF16NP)
    onesc = np.ones((128, 1), dtype=np.float32)
    onesr = np.ones((1, 128), dtype=np.float32)
    nc = _get("l0b", build_l0b)
    in_maps = []
    for c in range(NC):
        s, hh = c // 2, c % 2
        sl = slice(hh * TC, (hh + 1) * TC)
        in_maps.append(dict(
            h=np.ascontiguousarray(h_full[s, sl].T),
            s_c=sc_by_core[c], z=z_by_core[c],
            b_log=B_log, w_out=W_out, ln_g=ln_g, ln_b=ln_b,
            xln=xln_by_core[c],
            x=np.ascontiguousarray(x[s, sl].T),
            eye128=eye, onesc=onesc, onesr=onesr))
    res = run_bass_kernel_spmd(nc, in_maps, core_ids=list(range(NC)))
    return [res.results[c]["o_x1"] for c in range(NC)]


_EYE = np.eye(128, dtype=BF16NP)
_EYEF = np.eye(128, dtype=np.float32)
_ONESC = np.ones((128, 1), dtype=np.float32)
_ONESCB = np.ones((128, 1), dtype=BF16NP)
_ONESR = np.ones((1, 128), dtype=np.float32)
_IOTA8 = np.tile(np.arange(8, dtype=np.float32), (128, 1))
_TRI = np.triu(np.ones((128, 128), dtype=BF16NP), 1).astype(BF16NP)


def run_moer(x1_by_core, w_router):
    nc = _get("moer", build_moer)
    in_maps = [dict(x1=x1_by_core[c], w_router=w_router, iota8=_IOTA8, tri128=_TRI,
                    onesc=_ONESC, onesr=_ONESR, onescb=_ONESCB) for c in range(NC)]
    res = run_bass_kernel_spmd(nc, in_maps, core_ids=list(range(NC)))
    return res.results


def run_moed(x1_by_core, xln1_by_core, meta_by_core, cnt_all, w1, w2, bias):
    nc = _get("moed", build_moed)
    in_maps = []
    for c in range(NC):
        sel = (np.arange(8) < c).astype(np.float32)[:, None]
        in_maps.append(dict(x1=x1_by_core[c], xln1=xln1_by_core[c], meta=meta_by_core[c],
                            cnt_all=cnt_all, sel=sel, w1=w1, w2=w2, bias=bias,
                            iota8=_IOTA8, eyef=_EYEF, onesr=_ONESR))
    res = run_bass_kernel_spmd(nc, in_maps, core_ids=list(range(NC)))
    return [res.results[c]["o_x2"] for c in range(NC)]


_OH8 = np.zeros((8, 128), dtype=BF16NP)
for _k in range(8):
    _OH8[_k, 16 * _k:16 * (_k + 1)] = 1
_IND4 = np.zeros((4, 128, 32), dtype=BF16NP)
for _j in range(4):
    for _p in range(128):
        _IND4[_j, _p, 8 * _j + _p // 16] = 1


def run_l1a(x2_full, W_in, conv_w, conv_b, W_dt, b_dt, W_xproj):
    nc = _get("l1a", build_l1a)
    wins = _pad_windows(x2_full)
    in_maps = [dict(x=w, w_in=W_in, conv_w=conv_w, conv_b=conv_b, w_dt=W_dt,
                    b_dt=b_dt, w_xproj=W_xproj, onesc=_ONESC, onesr=_ONESR)
               for w in wins]
    res = run_bass_kernel_spmd(nc, in_maps, core_ids=list(range(NC)))
    return res.results


def run_scan(dh, dl, xh, xl, bssm, A_log):
    nc = _get("scan", build_scan)
    in_maps = []
    for c in range(NC):
        sl = slice(256 * c, 256 * (c + 1))
        in_maps.append(dict(dh=np.ascontiguousarray(dh[sl]), dl=np.ascontiguousarray(dl[sl]),
                            xh=np.ascontiguousarray(xh[sl]), xl=np.ascontiguousarray(xl[sl]),
                            bssm=bssm, alog=np.ascontiguousarray(A_log[sl]),
                            oh8=_OH8, ind4=_IND4))
    res = run_bass_kernel_spmd(nc, in_maps, core_ids=list(range(NC)))
    y = np.empty((DI, B * L), dtype=BF16NP)
    for c in range(NC):
        y[256 * c:256 * (c + 1)] = np.asarray(res.results[c]["o_y"])
    return y


def run_l1b(y_full, xc_by_core, z_by_core, Dp, W_out, x2_by_core):
    nc = _get("l1b", build_l1b)
    in_maps = []
    for c in range(NC):
        in_maps.append(dict(y=np.ascontiguousarray(y_full[:, TC * c:TC * (c + 1)]),
                            xc=xc_by_core[c], z=z_by_core[c], dp=Dp, w_out=W_out,
                            x2=x2_by_core[c]))
    res = run_bass_kernel_spmd(nc, in_maps, core_ids=list(range(NC)))
    return [res.results[c]["o_x3"] for c in range(NC)]


def run_quantw(wg, wu, wd):
    nc = _get("quantw", build_quantw)
    eye = np.eye(128, dtype=BF16NP)
    wgp = np.zeros((3072, 1024), dtype=np.float32)
    wgp[:2816] = wg
    wup = np.zeros((3072, 1024), dtype=np.float32)
    wup[:2816] = wu
    in_maps = []
    for c in range(NC):
        in_maps.append(dict(wg_sl=np.ascontiguousarray(wgp[384 * c:384 * (c + 1)]),
                            wu_sl=np.ascontiguousarray(wup[384 * c:384 * (c + 1)]),
                            wd_sl=np.ascontiguousarray(wd[128 * c:128 * (c + 1)]),
                            eye128=eye))
    res = run_bass_kernel_spmd(nc, in_maps, core_ids=list(range(NC)))
    wgt = np.concatenate([np.asarray(res.results[c]["o_wgt"])[:, :min(384, 2816 - 384 * c)]
                          for c in range(NC)], axis=1)
    wut = np.concatenate([np.asarray(res.results[c]["o_wut"])[:, :min(384, 2816 - 384 * c)]
                          for c in range(NC)], axis=1)
    wdt = np.concatenate([np.asarray(res.results[c]["o_wdt"]) for c in range(NC)], axis=1)
    return wgt, wut, wdt


def run_fp4(x3_by_core, wgt, wut, wdt):
    nc = _get("fp4", build_fp4)
    in_maps = [dict(x3=x3_by_core[c], wgt=wgt, wut=wut, wdt=wdt,
                    onesc=_ONESC, onesr=_ONESR) for c in range(NC)]
    res = run_bass_kernel_spmd(nc, in_maps, core_ids=list(range(NC)))
    return [res.results[c]["o_x4"] for c in range(NC)]


def _to_full(by_core):
    full = np.empty((B, L, D), dtype=np.float32)
    for c in range(NC):
        s, h = c // 2, c % 2
        full[s, h * TC:(h + 1) * TC] = np.asarray(by_core[c]).T
    return full


def mega_inmaps(inputs):
    x = np.asarray(inputs["L0_attn_x"])
    wins = _pad_windows(x)
    wgp = np.zeros((3072, 1024), dtype=np.float32)
    wgp[:2816] = np.asarray(inputs["L1_ffn_w_gate"])
    wup = np.zeros((3072, 1024), dtype=np.float32)
    wup[:2816] = np.asarray(inputs["L1_ffn_w_up"])
    wd = np.asarray(inputs["L1_ffn_w_down"])
    wg_starts = [384 * c for c in range(6)] + [2304, 2560]
    in_maps = []
    for c in range(NC):
        s, h = c // 2, c % 2
        gidx = np.empty((128, 16), np.int32)
        for cblk in range(8):
            for dt2 in range(2):
                gidx[:, cblk * 2 + dt2] = cblk * 2048 + 256 * c + 128 * dt2 + np.arange(128)
        cidx = np.empty((128, 16), np.int32)
        for j in range(16):
            cidx[:, j] = (128 * j + np.arange(128)) * 8 + c
        sidx = np.empty((8, 256), np.int32)
        for ks in range(128):
            s2, m = ks // 32, ks % 32
            for r in range(8):
                for hh in range(2):
                    sidx[r, 2 * ks + hh] = (2 * s2 + hh) * 2048 + 256 * c + 8 * m + r
        sel = (np.arange(8) < c).astype(np.float32)[:, None]
        hmask = np.full((128, 1), 1.0 if h == 1 else 0.0, np.float32)
        gs = wg_starts[c]
        in_maps.append(dict(
            xwin=wins[c],
            xplain=np.ascontiguousarray(x[s, h * TC:(h + 1) * TC].T),
            w0_in=np.asarray(inputs["L0_attn_W_in"]),
            c0_w=np.asarray(inputs["L0_attn_conv_w"]),
            c0_b=np.asarray(inputs["L0_attn_conv_b"]),
            w0_x=np.asarray(inputs["L0_attn_W_x"]),
            b_log=np.asarray(inputs["L0_attn_B_log"]),
            w0_out=np.asarray(inputs["L0_attn_W_out"]),
            ln_g=np.asarray(inputs["L0_attn_ln_g"]),
            ln_b=np.asarray(inputs["L0_attn_ln_b"]),
            w_router=np.asarray(inputs["L0_ffn_w_router"]),
            w1=np.asarray(inputs["L0_ffn_w1"]),
            w2=np.asarray(inputs["L0_ffn_w2"]),
            bias=np.ascontiguousarray(np.asarray(inputs["L0_ffn_bias"]).reshape(-1)),
            w1_in=np.asarray(inputs["L1_attn_W_in"]),
            c1_w=np.asarray(inputs["L1_attn_conv_w"]),
            c1_b=np.asarray(inputs["L1_attn_conv_b"]),
            w_dt=np.asarray(inputs["L1_attn_W_dt"]),
            b_dt=np.asarray(inputs["L1_attn_b_dt"]),
            w_xproj=np.asarray(inputs["L1_attn_W_xproj"]),
            alog_sl=np.ascontiguousarray(np.asarray(inputs["L1_attn_A_log"])[256 * c:256 * (c + 1)]),
            sidx=sidx,
            dp=np.asarray(inputs["L1_attn_Dp"]),
            w1_out=np.asarray(inputs["L1_attn_W_out"]),
            wg_sl=np.ascontiguousarray(wgp[gs:gs + 384]),
            wu_sl=np.ascontiguousarray(wup[gs:gs + 384]),
            wd_sl=np.ascontiguousarray(wd[128 * c:128 * (c + 1)]),
            eye128=_EYE, eyef=_EYEF, onesc=_ONESC, onesr=_ONESR, onescb=_ONESCB,
            iota8=_IOTA8, tri128=_TRI, oh8=_OH8, ind4=_IND4,
            sel=sel, hmask=hmask, gidx=gidx, cidx=cidx,
        ))
    return in_maps


def kernel_mega(**inputs):
    nc = _get("mega", build_mega)
    in_maps = mega_inmaps(inputs)
    res = run_bass_kernel_spmd(nc, in_maps, core_ids=list(range(NC)))
    return _to_full([np.asarray(res.results[c]["o_out"]) for c in range(NC)])


def kernel(**inputs):
    return kernel_mega(**inputs)


def kernel_multilaunch(**inputs):
    x = np.asarray(inputs["L0_attn_x"])
    r1 = run_l0a(x, np.asarray(inputs["L0_attn_W_in"]), np.asarray(inputs["L0_attn_conv_w"]),
                 np.asarray(inputs["L0_attn_conv_b"]), np.asarray(inputs["L0_attn_W_x"]))
    xcf = np.empty((B, L, DI), dtype=BF16NP)
    for c in range(NC):
        s, h = c // 2, c % 2
        xcf[s, h * TC:(h + 1) * TC] = np.asarray(r1[c]["o_xconv"]).T
    hfull = run_ema(xcf)
    x1s = run_l0b(hfull, [np.asarray(r1[c]["o_sc"]) for c in range(NC)],
                  [np.asarray(r1[c]["o_z"]) for c in range(NC)],
                  [np.asarray(r1[c]["o_xln"]) for c in range(NC)],
                  x, np.asarray(inputs["L0_attn_B_log"]), np.asarray(inputs["L0_attn_W_out"]),
                  np.asarray(inputs["L0_attn_ln_g"]), np.asarray(inputs["L0_attn_ln_b"]))
    x1_by_core = [np.ascontiguousarray(np.asarray(a)) for a in x1s]
    rr = run_moer(x1_by_core, np.asarray(inputs["L0_ffn_w_router"]))
    cnt_all = np.stack([np.asarray(rr[c]["o_cnt"])[0] for c in range(NC)]).astype(np.float32)
    x2s = run_moed(x1_by_core, [np.asarray(rr[c]["o_xln1"]) for c in range(NC)],
                   [np.asarray(rr[c]["o_meta"]) for c in range(NC)], cnt_all,
                   np.asarray(inputs["L0_ffn_w1"]), np.asarray(inputs["L0_ffn_w2"]),
                   np.ascontiguousarray(np.asarray(inputs["L0_ffn_bias"]).reshape(-1)))
    x2_full = _to_full(x2s)
    r5 = run_l1a(x2_full, np.asarray(inputs["L1_attn_W_in"]), np.asarray(inputs["L1_attn_conv_w"]),
                 np.asarray(inputs["L1_attn_conv_b"]), np.asarray(inputs["L1_attn_W_dt"]),
                 np.asarray(inputs["L1_attn_b_dt"]), np.asarray(inputs["L1_attn_W_xproj"]))

    def asm(key, rows, dt):
        full = np.empty((rows, B * L), dtype=dt)
        for c in range(NC):
            full[:, TC * c:TC * (c + 1)] = np.asarray(r5[c][key])
        return full

    dh = asm("o_dh", DI, BF16NP)
    dl = asm("o_dl", DI, BF16NP)
    xh = asm("o_xh", DI, BF16NP)
    xl = asm("o_xl", DI, BF16NP)
    bssm = np.ascontiguousarray(asm("o_bssm", NST, np.float32))
    y_full = run_scan(dh, dl, xh, xl, bssm, np.asarray(inputs["L1_attn_A_log"]))
    x3s = run_l1b(y_full, [np.asarray(r5[c]["o_xc"]) for c in range(NC)],
                  [np.asarray(r5[c]["o_z"]) for c in range(NC)],
                  np.asarray(inputs["L1_attn_Dp"]), np.asarray(inputs["L1_attn_W_out"]),
                  [np.ascontiguousarray(np.asarray(a)) for a in x2s])
    wgt, wut, wdt = run_quantw(np.asarray(inputs["L1_ffn_w_gate"]),
                               np.asarray(inputs["L1_ffn_w_up"]),
                               np.asarray(inputs["L1_ffn_w_down"]))
    x4s = run_fp4([np.ascontiguousarray(np.asarray(a)) for a in x3s],
                  np.ascontiguousarray(wgt), np.ascontiguousarray(wut),
                  np.ascontiguousarray(wdt))
    return _to_full(x4s)
